# version 01
# baseline (speedup 1.0000x reference)
import numpy as np
import ml_dtypes
import concourse.bass as bass
import concourse.mybir as mybir
from concourse.bass_utils import run_bass_kernel_spmd

F32 = mybir.dt.float32
BF16 = mybir.dt.bfloat16
AF = mybir.ActivationFunctionType
ALU = mybir.AluOpType
AX = mybir.AxisListType.X

D = 2048
S = 4096
KT = 16
H0 = 5504
H0T = 43
H0_TRUE = 5461
H1 = 8192
H1T = 64
NW = 8
WIN = 1152
EPS = 1e-5
ATT_SCALE = float(1.0 / np.sqrt(128.0))


def build_nc(debug=False):
    nc = bass.Bass("TRN2", target_bir_lowering=False)

    x0 = nc.dram_tensor("x0", [S, D], F32, kind="ExternalInput").ap()
    ident = nc.dram_tensor("ident", [128, 128], BF16, kind="ExternalInput").ap()
    wat = nc.dram_tensor("wat", [KT, 128, D], F32, kind="ExternalInput").ap()
    wxt = nc.dram_tensor("wxt", [KT, 128, D], F32, kind="ExternalInput").ap()
    ba_t = nc.dram_tensor("ba_t", [128, KT], F32, kind="ExternalInput").ap()
    bx_t = nc.dram_tensor("bx_t", [128, KT], F32, kind="ExternalInput").ap()
    lam_t = nc.dram_tensor("lam_t", [128, KT], F32, kind="ExternalInput").ap()
    w1t = nc.dram_tensor("w1t", [H0T, 128, D], F32, kind="ExternalInput").ap()
    w3t = nc.dram_tensor("w3t", [H0T, 128, D], F32, kind="ExternalInput").ap()
    w2t = nc.dram_tensor("w2t", [KT, 128, H0], F32, kind="ExternalInput").ap()
    wqt = nc.dram_tensor("wqt", [KT, 128, D], F32, kind="ExternalInput").ap()
    wkt = nc.dram_tensor("wkt", [2, 128, D], F32, kind="ExternalInput").ap()
    wvt = nc.dram_tensor("wvt", [2, 128, D], F32, kind="ExternalInput").ap()
    wot = nc.dram_tensor("wot", [KT, 128, D], F32, kind="ExternalInput").ap()
    bq_t = nc.dram_tensor("bq_t", [128, KT], F32, kind="ExternalInput").ap()
    bk_t = nc.dram_tensor("bk_t", [128, 2], F32, kind="ExternalInput").ap()
    bv_t = nc.dram_tensor("bv_t", [128, 2], F32, kind="ExternalInput").ap()
    bo_t = nc.dram_tensor("bo_t", [128, KT], F32, kind="ExternalInput").ap()
    wg_in = nc.dram_tensor("wg_in", [KT, 128, H1], F32, kind="ExternalInput").ap()
    wu_in = nc.dram_tensor("wu_in", [KT, 128, H1], F32, kind="ExternalInput").ap()
    wd_in = nc.dram_tensor("wd_in", [KT, 128, H1], F32, kind="ExternalInput").ap()
    pmask = nc.dram_tensor("pmask", [128, 2], F32, kind="ExternalInput").ap()
    consts = nc.dram_tensor("consts", [128, 2], F32, kind="ExternalInput").ap()
    onesr_in = nc.dram_tensor("onesr_in", [1, 128], F32, kind="ExternalInput").ap()

    out_e = nc.dram_tensor("out", [2048, D], F32, kind="ExternalOutput").ap()

    ikind = "ExternalOutput" if debug else "Internal"
    LOGA = nc.dram_tensor("LOGA", [KT, 128, S], BF16, kind="Internal").ap()
    UDAT = nc.dram_tensor("UDAT", [KT, 128, S], BF16, kind="Internal").ap()
    YSC = nc.dram_tensor("YSC", [D, S], BF16, kind="Internal").ap()
    xr1 = nc.dram_tensor("xr1", [S, D], F32, kind=ikind).ap()
    xr2 = nc.dram_tensor("xr2", [S, D], F32, kind=ikind).ap()
    xr3 = nc.dram_tensor("xr3", [2048, D], F32, kind=ikind).ap()
    q1s = nc.dram_tensor("q1s", [H0T, 128, D], BF16, kind="Internal").ap()
    q3s = nc.dram_tensor("q3s", [H0T, 128, D], BF16, kind="Internal").ap()
    q2s = nc.dram_tensor("q2s", [KT, 128, H0], BF16, kind="Internal").ap()
    wg_s = nc.dram_tensor("wg_s", [KT, 128, H1], BF16, kind="Internal").ap()
    wu_s = nc.dram_tensor("wu_s", [KT, 128, H1], BF16, kind="Internal").ap()
    wd_s = nc.dram_tensor("wd_s", [KT, 128, H1], BF16, kind="Internal").ap()
    och = nc.dram_tensor("och", [D, 2048], BF16, kind="Internal").ap()

    import contextlib
    es = contextlib.ExitStack()
    sb = lambda name, shape, dt: es.enter_context(nc.sbuf_tensor(name, shape, dt))

    NBF = 64 * 1024
    NF32 = 18 * 1024 + 512
    ABF = sb("ABF", [128, NBF], BF16)
    AF32 = sb("AF32", [128, NF32], F32)
    small = sb("small", [128, 128], F32)
    stat = sb("stat", [128, 64], F32)
    idn = sb("idn", [128, 128], BF16)
    ones_r = sb("ones_r", [1, 128], F32)
    ones_c = sb("ones_c", [128, 1], F32)
    s11 = sb("s11", [1, 8], F32)

    def f32s(c0, n):
        assert c0 + n <= NF32, (c0, n)
        return AF32[:, c0:c0 + n]

    def bfs(c0, n):
        assert c0 + n <= NBF, (c0, n)
        return ABF[:, c0:c0 + n]

    ba_c = small[:, 0:16]
    bx_c = small[:, 16:32]
    mco_c = small[:, 32:48]
    bq_c = small[:, 48:64]
    bk_c = small[:, 64:66]
    bv_c = small[:, 66:68]
    bo_c = small[:, 68:84]
    thr1 = small[:, 84:85]
    thr3 = small[:, 85:86]
    thr2 = small[:, 86:87]
    s1c = small[:, 87:88]
    me_c = small[:, 88:89]
    mo_c = small[:, 89:90]
    lam_c = small[:, 90:106]
    eps_c = small[:, 111:112]
    nthr1 = small[:, 107:108]
    ones_cc = small[:, 110:111]
    nthr3 = small[:, 108:109]
    nthr2 = small[:, 109:110]

    PS = [es.enter_context(nc.psum_tensor(f"PS{i}", [128, 512], F32)) for i in range(6)]
    PB = [es.enter_context(nc.psum_tensor(f"PB{i}", [128, 1024], BF16)) for i in range(2)]

    SEM = {}
    for s_ in ("ld", "gd", "mm", "act", "dve"):
        SEM[s_] = es.enter_context(nc.semaphore(s_))

    def stream(en, e):
        cnt = {"ld": 0, "gd": 0, "mm": 0, "act": 0, "dve": 0}

        def ld(out_ap, in_ap, transpose=False):
            prev = cnt["ld"]
            cnt["ld"] += 16
            if en == "sync":
                if prev:
                    e.wait_ge(SEM["ld"], prev)
                e.dma_start(out=out_ap, in_=in_ap, transpose=transpose).then_inc(SEM["ld"], 16)
            return cnt["ld"]

        def gd(out_ap, in_ap):
            prev = cnt["gd"]
            cnt["gd"] += 16
            if en == "gpsimd":
                if prev:
                    e.wait_ge(SEM["gd"], prev)
                e.dma_start(out=out_ap, in_=in_ap).then_inc(SEM["gd"], 16)
            return cnt["gd"]

        def w(sem, val):
            if val:
                e.wait_ge(SEM[sem], val)

        def mm(out, lhsT, rhs, start, stop, inc=True):
            if inc:
                cnt["mm"] += 1
            if en == "tensor":
                i = e.matmul(out, lhsT, rhs, start=start, stop=stop)
                if inc:
                    i.then_inc(SEM["mm"], 1)
            return cnt["mm"]

        def tp(out, in_, inc=True):
            if inc:
                cnt["mm"] += 1
            if en == "tensor":
                i = e.transpose(out, in_, idn[:])
                if inc:
                    i.then_inc(SEM["mm"], 1)
            return cnt["mm"]

        def act(fn, inc=True):
            if inc:
                cnt["act"] += 1
            if en == "scalar":
                if cnt["act"] > 1:
                    e.wait_ge(SEM["act"], cnt["act"] - 1)
                i = fn(e)
                if inc:
                    i.then_inc(SEM["act"], 1)
            return cnt["act"]

        def dve(fn, inc=True):
            if inc:
                cnt["dve"] += 1
            if en == "vector":
                if cnt["dve"] > 1:
                    e.wait_ge(SEM["dve"], cnt["dve"] - 1)
                i = fn(e)
                if inc:
                    i.then_inc(SEM["dve"], 1)
            return cnt["dve"]

        def barrier():
            tgt = dict(cnt)
            for s_ in ("ld", "gd", "mm", "act", "dve"):
                w(s_, tgt[s_])

        def ln_stats(src, sc, sq_junk, pre_act=0, pre_dve=0):
            ssq = stat[:, sc:sc + 1]
            sm = stat[:, sc + 1:sc + 2]
            mn = stat[:, sc + 2:sc + 3]
            var = stat[:, sc + 3:sc + 4]
            rstd = stat[:, sc + 4:sc + 5]
            mb = stat[:, sc + 5:sc + 6]
            a1 = act(lambda s, pa=pre_act: (
                s.wait_ge(SEM["act"], pa) if pa else None,
                s.activation(out=sq_junk, in_=src, func=AF.Square, accum_out=ssq))[-1])
            d1 = dve(lambda v, pd=pre_dve: (
                v.wait_ge(SEM["dve"], pd) if pd else None,
                v.reduce_sum(sm, src, AX))[-1])
            if en == "vector":
                w("act", a1)
            dve(lambda v: v.tensor_scalar_mul(mn, sm, 1.0 / D))
            dve(lambda v: v.tensor_scalar_mul(var, ssq, 1.0 / D))
            dve(lambda v: v.tensor_tensor(mb, mn, mn, ALU.mult))
            d3 = dve(lambda v: v.tensor_tensor(var, var, mb, ALU.subtract))
            a2 = act(lambda s, dd=d3: (
                s.wait_ge(SEM["dve"], dd),
                s.activation(out=rstd, in_=var, func=AF.Sqrt, bias=eps_c))[-1])
            if en == "vector":
                w("act", a2)
            dve(lambda v: v.reciprocal(rstd, rstd))
            d5 = dve(lambda v: v.tensor_tensor(mb, mn, rstd, ALU.mult))
            return rstd, mb, d5

        def ln_apply(dst, src, rstd, mb):
            return dve(lambda v: v.tensor_scalar(dst, src, rstd, mb,
                                                 ALU.mult, ALU.subtract))

        def emit_t16(xln_ap, dst_tensor_ap, stride, coff, dapp):
            last = 0
            for half in range(2):
                pb = PB[half]
                if en == "tensor":
                    w("dve", dapp)
                tmm = 0
                for kk in range(8):
                    k = half * 8 + kk
                    tmm = tp(pb[:, kk * 128:(kk + 1) * 128],
                             xln_ap[:, k * 128:(k + 1) * 128])
                dst = dst_tensor_ap.rearrange("p (k c) -> p k c", c=stride)[
                    :, half * 8:(half + 1) * 8, coff:coff + 128]
                srcp = pb[:].rearrange("p (k c) -> p k c", k=8)
                last = act(lambda s, d=dst, sr=srcp, tm=tmm: (
                    s.wait_ge(SEM["mm"], tm),
                    s.copy(out=d, in_=sr))[-1])
                if en == "tensor":
                    w("act", last)
            return cnt["mm"], last

        ld(ba_c, ba_t)
        ld(bx_c, bx_t)
        ld(lam_c, lam_t)
        ld(bq_c, bq_t)
        ld(bk_c, bk_t)
        ld(bv_c, bv_t)
        ld(bo_c, bo_t)
        ld(small[:, 88:90], pmask)
        ld(idn[:], ident)
        ld(small[:, 110:112], consts)
        ld(ones_r[:], onesr_in)
        barrier()
        aS1 = act(lambda s: s.activation(out=mco_c, in_=lam_c, func=AF.Exp, scale=-1.0))
        aS2 = act(lambda s, p=aS1: (s.wait_ge(SEM["act"], p),
                  s.activation(out=mco_c, in_=mco_c, func=AF.Ln, bias=1.0))[-1])
        act(lambda s, p=aS2: (s.wait_ge(SEM["act"], p),
            s.mul(mco_c, mco_c, -8.0))[-1])
        barrier()

        XL0T = bfs(0, 32768)
        wam0 = bfs(32768, 2048)
        wam1 = bfs(34816, 2048)
        wxm0 = bfs(36864, 2048)
        wxm1 = bfs(38912, 2048)
        logab0 = bfs(40960, 2048)
        ub0 = bfs(43008, 2048)
        logab1 = bfs(45056, 2048)
        ub1 = bfs(47104, 2048)
        xlnA = bfs(49152, 2048)
        sqjA = bfs(51200, 2048)
        p1junk = bfs(53248, 5504)
        xtA0 = f32s(0, 2048)
        xtA1 = f32s(2048, 2048)
        rA = f32s(4096, 2048)
        iA = f32s(6144, 2048)
        ixA = f32s(8192, 2048)
        sqA = f32s(10240, 2048)
        wp1 = f32s(12288, 5504)
        QACC = f32s(17920, 104)

        pass1_list = ([(w1t[i], D) for i in range(H0T)] +
                      [(w3t[i], D) for i in range(H0T)] +
                      [(w2t[i], H0) for i in range(KT)])
        p1 = {"idx": 0, "aa": 0}

        def emit_pass1_some(nmax):
            done = 0
            while p1["idx"] < len(pass1_list) and done < nmax:
                ap_, cols = pass1_list[p1["idx"]]
                if en == "sync":
                    w("act", p1["aa"])
                t_ld = ld(wp1[:, 0:cols], ap_)
                p1["aa"] = act(lambda s, c_=cols, tl=t_ld, ii=p1["idx"]: (
                    s.wait_ge(SEM["ld"], tl),
                    s.activation(out=p1junk[:, 0:c_], in_=wp1[:, 0:c_], func=AF.Abs,
                                 accum_out=QACC[:, ii:ii + 1]))[-1])
                p1["idx"] += 1
                done += 1

        for p in range(2):
            t0 = 2048 * p
            prev_tp = 0
            prev_ev = 0
            prev_dappA = 0
            for tt in range(16):
                xt = xtA0 if tt % 2 == 0 else xtA1
                if en == "sync":
                    w("dve", prev_dappA)
                t_x = ld(xt, x0[t0 + 128 * tt: t0 + 128 * (tt + 1), :])
                if en in ("scalar", "vector"):
                    w("ld", t_x)
                rstd, mb, dstat = ln_stats(xt, 0 if tt % 2 == 0 else 8, sqjA,
                                           pre_act=prev_ev)
                if en == "vector":
                    w("mm", prev_tp)
                dapp = ln_apply(xlnA, xt, rstd, mb)
                prev_dappA = dapp
                prev_tp, prev_ev = emit_t16(xlnA, XL0T[:], 2048, tt * 128, dapp)
                emit_pass1_some(1)
            barrier()

            gd(wam0, wat[0])
            gd(wxm0, wxt[0])
            st_l = 0
            st_u = 0
            prev_du = 0
            prev_mlast = 0
            for m in range(KT):
                wam = wam0 if m % 2 == 0 else wam1
                wxm = wxm0 if m % 2 == 0 else wxm1
                t_w = cnt["gd"]
                if m + 1 < KT:
                    if en == "gpsimd":
                        w("mm", prev_mlast)
                    gd(wam1 if m % 2 == 0 else wam0, wat[m + 1])
                    gd(wxm1 if m % 2 == 0 else wxm0, wxt[m + 1])
                lb = logab0 if m % 2 == 0 else logab1
                uo = ub0 if m % 2 == 0 else ub1
                a_last = 0
                for c in range(4):
                    psr = PS[c % 2]
                    psi = PS[2 + c % 2]
                    if en == "tensor" and c == 0:
                        w("gd", t_w)
                    tmr = 0
                    for k in range(KT):
                        tmr = mm(psr[:, 0:512], wam[:, k * 128:(k + 1) * 128],
                                 XL0T[:, k * 2048 + 512 * c: k * 2048 + 512 * (c + 1)],
                                 start=(k == 0), stop=(k == KT - 1), inc=(k == KT - 1))
                    tmi = 0
                    for k in range(KT):
                        tmi = mm(psi[:, 0:512], wxm[:, k * 128:(k + 1) * 128],
                                 XL0T[:, k * 2048 + 512 * c: k * 2048 + 512 * (c + 1)],
                                 start=(k == 0), stop=(k == KT - 1), inc=(k == KT - 1))
                    ar = act(lambda s, ps=psr, cc=c, mm_=m, tm=tmr, pd=prev_du: (
                        s.wait_ge(SEM["mm"], tm),
                        s.wait_ge(SEM["dve"], pd) if (pd and cc == 0) else None,
                        s.activation(out=rA[:, 512 * cc:512 * (cc + 1)], in_=ps[:, 0:512],
                                     func=AF.Sigmoid, bias=ba_c[:, mm_:mm_ + 1]))[-1])
                    ai = act(lambda s, ps=psi, cc=c, mm_=m, tm=tmi: (
                        s.wait_ge(SEM["mm"], tm),
                        s.activation(out=iA[:, 512 * cc:512 * (cc + 1)], in_=ps[:, 0:512],
                                     func=AF.Sigmoid, bias=bx_c[:, mm_:mm_ + 1]))[-1])
                    a_last = ai
                    if en == "tensor":
                        w("act", ai)
                    if c == 3:
                        prev_mlast = tmi
                if en == "vector":
                    w("act", a_last)
                    w("ld", st_l)
                d_lg = dve(lambda v, lb_=lb, mm_=m: v.tensor_scalar_mul(
                    lb_, rA, mco_c[:, mm_:mm_ + 1]))
                d_ix = dve(lambda v, mm_=m: v.tensor_tensor(
                    ixA, iA, XL0T[:, mm_ * 2048:(mm_ + 1) * 2048], ALU.mult))
                a_a2 = act(lambda s, lb_=lb, dl=d_lg: (
                    s.wait_ge(SEM["dve"], dl),
                    s.activation(out=sqA, in_=lb_, func=AF.Exp, scale=2.0))[-1])
                a_sq = act(lambda s, p=a_a2: (s.wait_ge(SEM["act"], p),
                    s.activation(out=sqA, in_=sqA, func=AF.Sqrt,
                                 scale=-1.0, bias=1.0))[-1])
                if en == "vector":
                    w("act", a_sq)
                    w("ld", st_u)
                d_u = dve(lambda v, uo_=uo: v.tensor_tensor(uo_, ixA, sqA, ALU.mult))
                prev_du = d_u
                if en == "sync":
                    w("dve", d_u)
                st_l = ld(LOGA[m][:, t0:t0 + 2048], lb)
                st_u = ld(UDAT[m][:, t0:t0 + 2048], uo)
                emit_pass1_some(2)
            barrier()
        emit_pass1_some(10000)
        barrier()

        ranges = [(0, H0T, H0_TRUE * D, thr1, True),
                  (H0T, 2 * H0T, H0_TRUE * D, thr3, False),
                  (2 * H0T, 2 * H0T + KT, D * H0_TRUE, thr2, False)]
        for idx, (c0, c1, count, thr_col, want_s1) in enumerate(ranges):
            dr = dve(lambda v, a=c0, b=c1, ii=idx: v.reduce_sum(
                stat[:, 32 + ii:33 + ii], QACC[:, a:b], AX))
            if en == "tensor":
                w("dve", dr)
            tm = mm(PS[4][0:1, idx:idx + 1], stat[:, 32 + idx:33 + idx], ones_cc,
                    start=True, stop=True)
            aa = act(lambda s, ii=idx, cc=count, tm_=tm: (
                s.wait_ge(SEM["mm"], tm_),
                s.activation(out=s11[0:1, ii:ii + 1], in_=PS[4][0:1, ii:ii + 1],
                             func=AF.Copy, scale=0.5 / cc))[-1])
            if en == "tensor":
                w("act", aa)
            tb = mm(PS[5][:, idx:idx + 1], ones_r[:], s11[0:1, idx:idx + 1],
                    start=True, stop=True)
            a_thr = act(lambda s, th=thr_col, tb_=tb: (
                s.wait_ge(SEM["mm"], tb_),
                s.copy(out=th, in_=PS[5][:, idx:idx + 1]))[-1])
            if want_s1:
                act(lambda s: s.mul(s1c, thr1, 2.0))
            nth = (nthr1, nthr3, nthr2)[idx]
            dve(lambda v, th=thr_col, nt=nth, p=a_thr: (
                v.wait_ge(SEM["act"], p),
                v.tensor_scalar_mul(nt, th, -1.0))[-1])
            barrier()

        lgB0 = bfs(0, 4096)
        lgB1 = bfs(4096, 4096)
        uB0 = bfs(8192, 4096)
        uB1 = bfs(12288, 4096)
        yB0 = bfs(16384, 4096)
        yB1 = bfs(20480, 4096)
        sgB = bfs(24576, 5504)
        mkB = bfs(30080, 5504)
        qB = bfs(35584, 5504)
        yttB = bfs(41088, 2048)
        sqjB = bfs(43136, 2048)
        aB = f32s(0, 4096)
        uB32 = f32s(4096, 4096)
        xtB = f32s(8192, 2048)
        tB = f32s(10240, 2048)

        pass2_list = ([(w1t[i], q1s[i], D, thr1, nthr1) for i in range(H0T)] +
                      [(w3t[i], q3s[i], D, thr3, nthr3) for i in range(H0T)] +
                      [(w2t[i], q2s[i], H0, thr2, nthr2) for i in range(KT)])
        p2 = {"idx": 0, "st": 0, "dmk": 0, "dq": 0}

        def emit_pass2_some(nmax):
            done = 0
            while p2["idx"] < len(pass2_list) and done < nmax:
                src, dst, cols, thr_col, nth_col = pass2_list[p2["idx"]]
                if en == "sync":
                    w("dve", p2["dmk"])
                t_ld = ld(wp1[:, 0:cols], src)
                dpos = dve(lambda v, c_=cols, th=thr_col, tl=t_ld: (
                    v.wait_ge(SEM["ld"], tl),
                    v.tensor_scalar(mkB[:, 0:c_], wp1[:, 0:c_], th, None,
                                    ALU.is_gt))[-1])
                dmk = dve(lambda v, c_=cols, nt=nth_col, pl=p2["st"]: (
                    v.wait_ge(SEM["ld"], pl) if pl else None,
                    v.tensor_scalar(sgB[:, 0:c_], wp1[:, 0:c_], nt, None,
                                    ALU.is_lt))[-1])
                dq = dve(lambda v, c_=cols: v.tensor_tensor(
                    qB[:, 0:c_], mkB[:, 0:c_], sgB[:, 0:c_], ALU.subtract))
                if en == "sync":
                    w("dve", dq)
                p2["st"] = ld(dst, qB[:, 0:cols])
                p2["dmk"] = dmk
                p2["dq"] = dq
                p2["idx"] += 1
                done += 1

        st_y = 0
        prev_dy = 0
        prev_au = 0
        ld(lgB0, LOGA[0][:, :])
        ld(uB0, UDAT[0][:, :])
        for m in range(KT):
            lg = lgB0 if m % 2 == 0 else lgB1
            uu = uB0 if m % 2 == 0 else uB1
            yy = yB0 if m % 2 == 0 else yB1
            t_in = cnt["ld"]
            if m + 1 < KT:
                if en == "sync":
                    w("act", prev_au)
                ld(lgB1 if m % 2 == 0 else lgB0, LOGA[m + 1][:, :])
                ld(uB1 if m % 2 == 0 else uB0, UDAT[m + 1][:, :])
            a_a = act(lambda s, lg_=lg, ti=t_in, pd=prev_dy: (
                s.wait_ge(SEM["ld"], ti),
                s.wait_ge(SEM["dve"], pd) if pd else None,
                s.activation(out=aB, in_=lg_, func=AF.Exp))[-1])
            a_u = act(lambda s, uu_=uu: s.copy(out=uB32, in_=uu_))
            prev_au = a_u
            if en == "vector":
                w("act", a_u)
                w("ld", st_y)
            dve(lambda v, yy_=yy: v.tensor_tensor_scan(
                out=yy_[:, 0:2048], data0=aB[:, 0:2048], data1=uB32[:, 0:2048],
                initial=0.0, op0=ALU.mult, op1=ALU.add))
            dve(lambda v, yy_=yy: v.tensor_copy(stat[:, 30:31], yy_[:, 2047:2048]))
            d_y = dve(lambda v, yy_=yy: v.tensor_tensor_scan(
                out=yy_[:, 2048:4096], data0=aB[:, 2048:4096], data1=uB32[:, 2048:4096],
                initial=stat[:, 30:31], op0=ALU.mult, op1=ALU.add))
            prev_dy = d_y
            if en == "sync":
                w("dve", d_y)
            st_y = ld(YSC[m * 128:(m + 1) * 128, :], yy)
            emit_pass2_some(4)
        emit_pass2_some(10000)
        barrier()

        prev_dr = 0
        for tt in range(32):
            if en == "sync":
                w("dve", prev_dr)
            t_y = ld(yttB, YSC[:, tt * 128:(tt + 1) * 128], transpose=True)
            t_x = ld(xtB, x0[tt * 128:(tt + 1) * 128, :])
            if en in ("scalar", "vector"):
                w("ld", t_y)
            rstd, mb, _ = ln_stats(yttB, 0, sqjB)
            ln_apply(tB, yttB, rstd, mb)
            if en == "vector":
                w("ld", t_x)
            d_r = dve(lambda v: v.tensor_tensor(tB, tB, xtB, ALU.add))
            prev_dr = d_r
            if en == "sync":
                w("dve", d_r)
            ld(xr1[tt * 128:(tt + 1) * 128, :], tB)
        barrier()

        XL1T = bfs(0, 8192)
        GBLK = bfs(8192, 43 * 512)
        q1b0 = bfs(30208, 2048)
        q1b1 = bfs(32256, 2048)
        q3b0 = bfs(34304, 2048)
        q3b1 = bfs(36352, 2048)
        q2b0 = bfs(38400, 5504)
        q2b1 = bfs(43904, 5504)
        xlnC = bfs(49408, 2048)
        sqjC = bfs(51456, 2048)
        ocbC = bfs(53504, 512)
        ytC = bfs(54016, 2048)
        xaC = f32s(0, 2048)
        xbC = f32s(2048, 2048)
        xqC = f32s(4096, 2048)
        silC = f32s(6144, 512)
        tC = f32s(6656, 2048)

        cb0 = bfs(56064, 4096)
        cb1 = bfs(60160, 4096)
        wcast_units = []
        for pair in ([(wg_in[k], wg_s[k]) for k in range(KT)] +
                     [(wu_in[k], wu_s[k]) for k in range(KT)] +
                     [(wd_in[k], wd_s[k]) for k in range(KT)]):
            srcw, dstw = pair
            wcast_units.append((srcw[:, 0:4096], dstw[:, 0:4096]))
            wcast_units.append((srcw[:, 4096:8192], dstw[:, 4096:8192]))
        wc = {"idx": 0, "st": [0, 0]}

        def emit_wcast_some(nmax):
            done = 0
            while wc["idx"] < len(wcast_units) and done < nmax:
                srcp, dstp = wcast_units[wc["idx"]]
                buf = cb0 if wc["idx"] % 2 == 0 else cb1
                if en == "gpsimd":
                    w("ld", wc["st"][wc["idx"] % 2])
                t_c = gd(buf, srcp)
                if en == "sync":
                    w("gd", t_c)
                wc["st"][wc["idx"] % 2] = ld(dstp, buf)
                wc["idx"] += 1
                done += 1

        blocks = [("P", 1024, None), ("P", 1536, None), ("P", 2048, None),
                  ("P", 2560, None), ("Q", 0, 3072), ("Q", 512, 3584)]

        for btype, rA_, rB_ in blocks:
            prev_tp = 0
            prev_ev = 0
            prev_dapp = 0
            for tt in range(4):
                if en == "sync":
                    w("dve", prev_dapp)
                t_a = ld(xaC, xr1[rA_ + 128 * tt: rA_ + 128 * (tt + 1), :])
                src = xaC
                dpre = 0
                if btype == "Q":
                    t_b = ld(xbC, xr1[rB_ + 128 * tt: rB_ + 128 * (tt + 1), :])
                    if en == "vector":
                        w("ld", t_b)
                    dve(lambda v: v.tensor_scalar_mul(xqC, xaC, me_c))
                    dpre = dve(lambda v: v.scalar_tensor_tensor(
                        xqC, xbC, mo_c, xqC, ALU.mult, ALU.add))
                    src = xqC
                else:
                    if en == "vector":
                        w("ld", t_a)
                if en == "scalar":
                    if btype == "Q":
                        w("dve", dpre)
                    else:
                        w("ld", t_a)
                rstd, mb, _ = ln_stats(src, 0, sqjC, pre_act=prev_ev)
                if en == "vector":
                    w("mm", prev_tp)
                dapp = ln_apply(xlnC, src, rstd, mb)
                prev_dapp = dapp
                prev_tp, prev_ev = emit_t16(xlnC, XL1T[:], 512, tt * 128, dapp)
            barrier()

            ld(q1b0, q1s[0])
            ld(q3b0, q3s[0])
            prev_dg = 0
            prev_h_mm = 0
            for mh in range(H0T):
                qb1 = q1b0 if mh % 2 == 0 else q1b1
                qb3 = q3b0 if mh % 2 == 0 else q3b1
                t_ql = cnt["ld"]
                if mh + 1 < H0T:
                    if en == "sync":
                        w("mm", prev_h_mm)
                    ld(q1b1 if mh % 2 == 0 else q1b0, q1s[mh + 1])
                    ld(q3b1 if mh % 2 == 0 else q3b0, q3s[mh + 1])
                emit_wcast_some(1)
                ph1 = PS[mh % 2]
                ph3 = PS[2 + mh % 2]
                if en == "tensor":
                    w("ld", t_ql)
                tm1 = 0
                for k in range(KT):
                    tm1 = mm(ph1[:, 0:512], qb1[:, k * 128:(k + 1) * 128],
                             XL1T[:, k * 512:(k + 1) * 512],
                             start=(k == 0), stop=(k == KT - 1), inc=(k == KT - 1))
                tm3 = 0
                for k in range(KT):
                    tm3 = mm(ph3[:, 0:512], qb3[:, k * 128:(k + 1) * 128],
                             XL1T[:, k * 512:(k + 1) * 512],
                             start=(k == 0), stop=(k == KT - 1), inc=(k == KT - 1))
                a_s = act(lambda s, p_=ph1, tm=tm1, pd=prev_dg: (
                    s.wait_ge(SEM["mm"], tm),
                    s.wait_ge(SEM["dve"], pd) if pd else None,
                    s.activation(out=silC, in_=p_[:, 0:512], func=AF.Silu,
                                 scale=s1c))[-1])
                if en == "vector":
                    w("act", a_s)
                    w("mm", tm3)
                d_g = dve(lambda v, p_=ph3, mh_=mh: v.tensor_tensor(
                    GBLK[:, mh_ * 512:(mh_ + 1) * 512], silC, p_[:, 0:512], ALU.mult))
                prev_dg = d_g
                prev_h_mm = tm3
                if en == "tensor":
                    w("dve", d_g)

            ld(q2b0, q2s[0])
            st_o = 0
            prev_d_mm = 0
            for m2 in range(KT):
                qb2 = q2b0 if m2 % 2 == 0 else q2b1
                t_ql = cnt["ld"]
                if m2 + 1 < KT:
                    if en == "sync":
                        w("mm", prev_d_mm)
                    ld(q2b1 if m2 % 2 == 0 else q2b0, q2s[m2 + 1])
                emit_wcast_some(1)
                po = PS[4 + m2 % 2]
                if en == "tensor":
                    w("ld", t_ql)
                tmo = 0
                for k in range(H0T):
                    tmo = mm(po[:, 0:512], qb2[:, k * 128:(k + 1) * 128],
                             GBLK[:, k * 512:(k + 1) * 512],
                             start=(k == 0), stop=(k == H0T - 1), inc=(k == H0T - 1))
                prev_d_mm = tmo
                aev = act(lambda s, p_=po, tm=tmo, so=st_o: (
                    s.wait_ge(SEM["mm"], tm),
                    s.wait_ge(SEM["ld"], so) if so else None,
                    s.copy(out=ocbC, in_=p_[:, 0:512]))[-1])
                if en == "sync":
                    w("act", aev)
                st_o = ld(och[m2 * 128:(m2 + 1) * 128, 0:512], ocbC)
                if en == "tensor":
                    w("act", aev)
            barrier()

            prev_last = 0
            for tt in range(4):
                if en == "sync":
                    w("dve", prev_last)
                t_y = ld(ytC, och[:, tt * 128:(tt + 1) * 128], transpose=True)
                t_a = ld(xaC, xr1[rA_ + 128 * tt: rA_ + 128 * (tt + 1), :])
                src = xaC
                if btype == "Q":
                    t_b = ld(xbC, xr1[rB_ + 128 * tt: rB_ + 128 * (tt + 1), :])
                    if en == "vector":
                        w("ld", t_b)
                    dve(lambda v: v.tensor_scalar_mul(xqC, xaC, me_c))
                    dve(lambda v: v.scalar_tensor_tensor(
                        xqC, xbC, mo_c, xqC, ALU.mult, ALU.add))
                    src = xqC
                else:
                    if en == "vector":
                        w("ld", t_a)
                if en in ("scalar", "vector"):
                    w("ld", t_y)
                rstd, mb, _ = ln_stats(ytC, 0, sqjC)
                ln_apply(tC, ytC, rstd, mb)
                d_r = dve(lambda v, s_=src: v.tensor_tensor(tC, tC, s_, ALU.add))
                prev_last = d_r
                if en == "sync":
                    w("dve", d_r)
                t_st = ld(xr2[rA_ + 128 * tt: rA_ + 128 * (tt + 1), :], tC)
                if btype == "Q":
                    t_st = ld(xr2[rB_ + 128 * tt: rB_ + 128 * (tt + 1), :], tC)
                if en == "vector":
                    w("ld", t_st)
            barrier()

        emit_wcast_some(100000)
        barrier()

        qS = bfs(0, KT * WIN)
        kS = bfs(18432, 2 * WIN)
        vS = bfs(20736, 2 * WIN)
        vT = bfs(23040, 2 * WIN)
        XWT = bfs(25344, KT * WIN)
        OSB = bfs(25344, KT * 2048)
        xlnD = bfs(43776, 2048)
        sqjD = bfs(45824, 2048)
        pnS = bfs(58112, 512)
        ptS = bfs(58624, 512)
        wqb0 = bfs(59136, 2048)
        wqb1 = bfs(61184, 2048)
        ytD = bfs(0, 2048)
        ocbD = bfs(2048, 512)
        sqjD2 = bfs(2560, 2048)
        xaD = f32s(0, 2048)
        xbD = f32s(2048, 2048)
        xqD = f32s(4096, 2048)
        tD = f32s(6144, 2048)

        prev_tp = 0
        prev_ev = 0
        prev_dapp = 0
        for tt in range(9):
            if en == "sync":
                w("dve", prev_dapp)
            t_a = ld(xaD, xr2[128 * tt:128 * (tt + 1), :])
            t_b = ld(xbD, xr2[1024 + 128 * tt: 1024 + 128 * (tt + 1), :])
            if en == "vector":
                w("ld", t_b)
            dve(lambda v: v.tensor_scalar_mul(xqD, xaD, me_c))
            dpre = dve(lambda v: v.scalar_tensor_tensor(
                xqD, xbD, mo_c, xqD, ALU.mult, ALU.add))
            if en == "scalar":
                w("dve", dpre)
            rstd, mb, _ = ln_stats(xqD, 0, sqjD, pre_act=prev_ev)
            if en == "vector":
                w("mm", prev_tp)
            dapp = ln_apply(xlnD, xqD, rstd, mb)
            prev_dapp = dapp
            prev_tp, prev_ev = emit_t16(xlnD, XWT[:], WIN, tt * 128, dapp)
        barrier()

        qkv_list = ([(wqt[m_], bq_c[:, m_:m_ + 1], qS, m_) for m_ in range(KT)] +
                    [(wkt[m_], bk_c[:, m_:m_ + 1], kS, m_) for m_ in range(2)] +
                    [(wvt[m_], bv_c[:, m_:m_ + 1], vS, m_) for m_ in range(2)])
        gd(wqb0, qkv_list[0][0])
        prev_q_mm = 0
        for i_, (wsrc, bias, dstbuf, m_) in enumerate(qkv_list):
            wb = wqb0 if i_ % 2 == 0 else wqb1
            t_wl = cnt["gd"]
            if i_ + 1 < len(qkv_list):
                if en == "gpsimd":
                    w("mm", prev_q_mm)
                gd(wqb1 if i_ % 2 == 0 else wqb0, qkv_list[i_ + 1][0])
            if en == "tensor":
                w("gd", t_wl)
            for ci, (c0, cw) in enumerate(((0, 512), (512, 512), (1024, 128))):
                pq = PS[ci % 2]
                tmq = 0
                for k in range(KT):
                    tmq = mm(pq[:, 0:cw], wb[:, k * 128:(k + 1) * 128],
                             XWT[:, k * WIN + c0: k * WIN + c0 + cw],
                             start=(k == 0), stop=(k == KT - 1), inc=(k == KT - 1))
                prev_q_mm = tmq
                aev = act(lambda s, p_=pq, d=dstbuf, mm_=m_, a=c0, ww=cw, b_=bias, tm=tmq: (
                    s.wait_ge(SEM["mm"], tm),
                    s.activation(out=d[:, mm_ * WIN + a: mm_ * WIN + a + ww],
                                 in_=p_[:, 0:ww], func=AF.Identity, bias=b_))[-1])
                if en == "tensor":
                    w("act", aev)
        for g_ in range(2):
            for tt in range(9):
                pb = PB[tt % 2]
                tmm = tp(pb[:, 0:128],
                         vS[:, g_ * WIN + tt * 128: g_ * WIN + (tt + 1) * 128])
                aev = act(lambda s, g0=g_, t0_=tt, tm=tmm: (
                    s.wait_ge(SEM["mm"], tm),
                    s.copy(out=vT[:, (g0 * 9 + t0_) * 128:(g0 * 9 + t0_ + 1) * 128],
                           in_=PB[t0_ % 2][:, 0:128]))[-1])
                if en == "tensor":
                    w("act", aev)
        barrier()

        last_o = {}
        last_p = {}
        prev_tmt = 0
        prev_tmo = 0
        for j in range(NW):
            for h in range(KT):
                g_ = h % 2
                ctx = (j * KT + h) % 2
                psS = PS[ctx]
                psO = PS[2 + ctx]
                pbP = PB[ctx]
                if en == "tensor" and ("s", ctx) in last_o:
                    w("act", last_o[("s", ctx)])
                tms = 0
                for qt in range(2):
                    tms = mm(psS[:, qt * 256:(qt + 1) * 256],
                             qS[:, h * WIN + j * 128 + qt * 128:
                                h * WIN + j * 128 + (qt + 1) * 128],
                             kS[:, g_ * WIN + j * 128: g_ * WIN + j * 128 + 256],
                             start=True, stop=True)
                dn_last = 0
                for qt in range(2):
                    s0 = 16 + 8 * qt
                    mx = stat[:, s0:s0 + 1]
                    mxs = stat[:, s0 + 1:s0 + 2]
                    rs = stat[:, s0 + 2:s0 + 3]
                    rr = stat[:, s0 + 3:s0 + 4]
                    dm = dve(lambda v, p_=psS, q_=qt, tm=tms, m_=mx: (
                        v.wait_ge(SEM["mm"], tm),
                        v.reduce_max(m_, p_[:, q_ * 256:(q_ + 1) * 256], AX))[-1])
                    dmx = dve(lambda v, m_=mx, ms=mxs: v.tensor_scalar_mul(
                        ms, m_, -ATT_SCALE))
                    ae = act(lambda s, p_=psS, q_=qt, dm_=dmx, ms=mxs, rs_=rs,
                             pt_=prev_tmt: (
                        s.wait_ge(SEM["dve"], dm_),
                        s.wait_ge(SEM["mm"], pt_) if (pt_ and q_ == 0) else None,
                        s.activation(out=pnS[:, q_ * 256:(q_ + 1) * 256],
                                     in_=p_[:, q_ * 256:(q_ + 1) * 256],
                                     func=AF.Exp, scale=ATT_SCALE, bias=ms,
                                     accum_out=rs_))[-1])
                    dr = dve(lambda v, a_=ae, rs_=rs, rr_=rr: (
                        v.wait_ge(SEM["act"], a_),
                        v.reciprocal(rr_, rs_))[-1])
                    dn_last = dve(lambda v, q_=qt, rr_=rr: v.tensor_scalar_mul(
                        pnS[:, q_ * 256:(q_ + 1) * 256],
                        pnS[:, q_ * 256:(q_ + 1) * 256], rr_))
                if en == "tensor":
                    w("dve", dn_last)
                    if ("p", ctx) in last_p:
                        w("act", last_p[("p", ctx)])
                tmt = 0
                for qt in range(2):
                    for kt in range(2):
                        tmt = tp(pbP[:, kt * 256 + qt * 128: kt * 256 + (qt + 1) * 128],
                                 pnS[:, qt * 256 + kt * 128: qt * 256 + (kt + 1) * 128])
                prev_tmt = tmt
                apt = act(lambda s, pb_=pbP, tm=tmt, po_=prev_tmo: (
                    s.wait_ge(SEM["mm"], tm),
                    s.wait_ge(SEM["mm"], po_) if po_ else None,
                    s.copy(out=ptS, in_=pb_[:, 0:512]))[-1])
                last_p[("p", ctx)] = apt
                if en == "tensor":
                    w("act", apt)
                tmo = 0
                for kt in range(2):
                    tmo = mm(psO[:, 0:256],
                             vT[:, (g_ * 9 + j + kt) * 128:(g_ * 9 + j + kt + 1) * 128],
                             ptS[:, kt * 256:(kt + 1) * 256],
                             start=(kt == 0), stop=(kt == 1), inc=(kt == 1))
                prev_tmo = tmo
                aoe = act(lambda s, p_=psO, h_=h, j_=j, tm=tmo: (
                    s.wait_ge(SEM["mm"], tm),
                    s.copy(out=OSB[:, h_ * 2048 + j_ * 256: h_ * 2048 + (j_ + 1) * 256],
                           in_=p_[:, 0:256]))[-1])
                last_o[("s", ctx)] = aoe
        barrier()

        gd(wqb0, wot[0])
        st_o = 0
        prev_o_mm = 0
        for m in range(KT):
            wb = wqb0 if m % 2 == 0 else wqb1
            t_wl = cnt["gd"]
            if m + 1 < KT:
                if en == "gpsimd":
                    w("mm", prev_o_mm)
                gd(wqb1 if m % 2 == 0 else wqb0, wot[m + 1])
            for c4 in range(4):
                po = PS[c4 % 2]
                if en == "tensor":
                    w("gd", t_wl)
                tmo = 0
                for k in range(KT):
                    tmo = mm(po[:, 0:512], wb[:, k * 128:(k + 1) * 128],
                             OSB[:, k * 2048 + c4 * 512: k * 2048 + (c4 + 1) * 512],
                             start=(k == 0), stop=(k == KT - 1), inc=(k == KT - 1))
                prev_o_mm = tmo
                aev = act(lambda s, p_=po, m_=m, tm=tmo, so=st_o: (
                    s.wait_ge(SEM["mm"], tm),
                    s.wait_ge(SEM["ld"], so) if so else None,
                    s.activation(out=ocbD, in_=p_[:, 0:512], func=AF.Identity,
                                 bias=bo_c[:, m_:m_ + 1]))[-1])
                if en == "sync":
                    w("act", aev)
                st_o = ld(och[m * 128:(m + 1) * 128, c4 * 512:(c4 + 1) * 512], ocbD)
                if en == "tensor":
                    w("act", aev)
        barrier()

        prev_last = 0
        for tt in range(16):
            if en == "sync":
                w("dve", prev_last)
            t_y = ld(ytD, och[:, tt * 128:(tt + 1) * 128], transpose=True)
            t_a = ld(xaD, xr2[128 * tt:128 * (tt + 1), :])
            t_b = ld(xbD, xr2[2048 + 128 * tt: 2048 + 128 * (tt + 1), :])
            if en == "vector":
                w("ld", t_b)
            dve(lambda v: v.tensor_scalar_mul(xqD, xaD, me_c))
            dve(lambda v: v.scalar_tensor_tensor(xqD, xbD, mo_c, xqD, ALU.mult, ALU.add))
            if en in ("scalar", "vector"):
                w("ld", t_y)
            rstd, mb, _ = ln_stats(ytD, 0, sqjD2)
            ln_apply(tD, ytD, rstd, mb)
            d_r = dve(lambda v: v.tensor_tensor(tD, tD, xqD, ALU.add))
            prev_last = d_r
            if en == "sync":
                w("dve", d_r)
            t_st = ld(xr3[128 * tt:128 * (tt + 1), :], tD)
            if en == "vector":
                w("ld", t_st)
        barrier()

        XL3T = bfs(0, KT * 512)
        GT = bfs(8192, H1T * 512)
        wgb = bfs(40960, H1)
        wub = bfs(49152, H1)
        wdb0 = wgb
        wdb1 = wub
        gsp = bfs(57344, 512)
        xlnE = bfs(57856, 2048)
        sqjE = bfs(59904, 2048)
        ocbE = bfs(61952, 512)
        ytE = bfs(62464, 1536)
        ytE = xlnE
        xaE = f32s(0, 2048)
        tE = f32s(2048, 2048)
        gfE = f32s(4096, 512)
        silE = f32s(4608, 512)
        abE = f32s(5120, 512)
        thrE = f32s(5632, 128)
        k0E = f32s(5760, 128)
        k1E = f32s(5888, 128)
        m1E = f32s(6016, 128)
        m2E = f32s(6144, 128)
        n1E = f32s(6272, 128)
        n2E = f32s(6400, 128)

        for q4 in range(4):
            r0 = q4 * 512
            prev_tp = 0
            prev_ev = 0
            prev_dapp = 0
            for tt in range(4):
                if en == "sync":
                    w("dve", prev_dapp)
                t_a = ld(xaE, xr3[r0 + 128 * tt: r0 + 128 * (tt + 1), :])
                if en in ("vector", "scalar"):
                    w("ld", t_a)
                rstd, mb, _ = ln_stats(xaE, 0, sqjE, pre_act=prev_ev)
                if en == "vector":
                    w("mm", prev_tp)
                dapp = ln_apply(xlnE, xaE, rstd, mb)
                prev_dapp = dapp
                prev_tp, prev_ev = emit_t16(xlnE, XL3T[:], 512, tt * 128, dapp)
            barrier()

            prev_tmu = 0
            prev_dsp = 0
            prev_tmt = 0
            for n in range(KT):
                if en == "sync":
                    w("mm", prev_tmu)
                t_wl = 0
                for k in range(KT):
                    t_wl = ld(wgb[:, k * 512:(k + 1) * 512],
                              wg_s[k][:, n * 512:(n + 1) * 512])
                    t_wl = ld(wub[:, k * 512:(k + 1) * 512],
                              wu_s[k][:, n * 512:(n + 1) * 512])
                for mtk in range(4):
                    pg = PS[mtk % 2]
                    pu = PS[2 + mtk % 2]
                    if en == "tensor" and mtk == 0:
                        w("ld", t_wl)
                    tmg = 0
                    for k in range(KT):
                        tmg = mm(pg[:, 0:512],
                                 XL3T[:, k * 512 + mtk * 128: k * 512 + (mtk + 1) * 128],
                                 wgb[:, k * 512:(k + 1) * 512],
                                 start=(k == 0), stop=(k == KT - 1), inc=(k == KT - 1))
                    tmu = 0
                    for k in range(KT):
                        tmu = mm(pu[:, 0:512],
                                 XL3T[:, k * 512 + mtk * 128: k * 512 + (mtk + 1) * 128],
                                 wub[:, k * 512:(k + 1) * 512],
                                 start=(k == 0), stop=(k == KT - 1), inc=(k == KT - 1))
                    prev_tmu = tmu
                    a_s = act(lambda s, p_=pg, tm=tmg, pd=prev_dsp: (
                        s.wait_ge(SEM["mm"], tm),
                        s.wait_ge(SEM["dve"], pd) if pd else None,
                        s.activation(out=silE, in_=p_[:, 0:512], func=AF.Silu))[-1])
                    if en == "vector":
                        w("act", a_s)
                        w("mm", tmu)
                        w("mm", prev_tmt)
                    d_g = dve(lambda v, p_=pu: v.tensor_tensor(
                        gfE, silE, p_[:, 0:512], ALU.mult))
                    a_ab = act(lambda s, dg=d_g: (
                        s.wait_ge(SEM["dve"], dg),
                        s.activation(out=abE, in_=gfE, func=AF.Abs))[-1])
                    if en == "vector":
                        w("act", a_ab)
                    ab3 = abE.rearrange("p (q f) -> p q f", f=4)
                    gf3 = gfE.rearrange("p (q f) -> p q f", f=4)
                    gs3 = gsp.rearrange("p (q f) -> p q f", f=4)
                    dve(lambda v: v.tensor_tensor(m1E, ab3[:, :, 0], ab3[:, :, 1], ALU.max))
                    dve(lambda v: v.tensor_tensor(m2E, ab3[:, :, 2], ab3[:, :, 3], ALU.max))
                    dve(lambda v: v.tensor_tensor(n1E, ab3[:, :, 0], ab3[:, :, 1], ALU.min))
                    dve(lambda v: v.tensor_tensor(n2E, ab3[:, :, 2], ab3[:, :, 3], ALU.min))
                    dve(lambda v: v.tensor_tensor(m1E, m1E, m2E, ALU.min))
                    dve(lambda v: v.tensor_tensor(n1E, n1E, n2E, ALU.max))
                    dve(lambda v: v.tensor_tensor(thrE, m1E, n1E, ALU.max))
                    d_sp = 0
                    for f4 in range(4):
                        kk_ = k0E if f4 % 2 == 0 else k1E
                        dve(lambda v, f_=f4, kb=kk_: v.tensor_tensor(
                            kb, ab3[:, :, f_], thrE, ALU.is_ge))
                        d_sp = dve(lambda v, f_=f4, kb=kk_: v.tensor_tensor(
                            gs3[:, :, f_], gf3[:, :, f_], kb, ALU.mult))
                    prev_dsp = d_sp
                    if en == "tensor":
                        w("dve", d_sp)
                    pbG = PB[mtk % 2]
                    tmt = 0
                    for c_ in range(4):
                        tmt = tp(pbG[:, c_ * 128:(c_ + 1) * 128],
                                 gsp[:, c_ * 128:(c_ + 1) * 128])
                    prev_tmt = tmt
                    aev = act(lambda s, pb_=pbG, n_=n, m_=mtk, tm=tmt: (
                        s.wait_ge(SEM["mm"], tm),
                        s.copy(out=GT[:].rearrange("p (k c) -> p k c", c=512)[
                            :, n_ * 4:(n_ + 1) * 4, m_ * 128:(m_ + 1) * 128],
                            in_=pb_[:, 0:512].rearrange("p (k c) -> p k c", k=4)))[-1])
                    if en == "tensor":
                        w("act", aev)
            barrier()

            gd(wdb0, wd_s[0])
            st_o = 0
            prev_e_mm = 0
            for m2 in range(KT):
                wb = wdb0 if m2 % 2 == 0 else wdb1
                t_wl = cnt["gd"]
                if m2 + 1 < KT:
                    if en == "gpsimd":
                        w("mm", prev_e_mm)
                    gd(wdb1 if m2 % 2 == 0 else wdb0, wd_s[m2 + 1])
                po = PS[4 + m2 % 2]
                if en == "tensor":
                    w("gd", t_wl)
                tmo = 0
                for k in range(H1T):
                    tmo = mm(po[:, 0:512], wb[:, k * 128:(k + 1) * 128],
                             GT[:, k * 512:(k + 1) * 512],
                             start=(k == 0), stop=(k == H1T - 1), inc=(k == H1T - 1))
                prev_e_mm = tmo
                aev = act(lambda s, p_=po, tm=tmo, so=st_o: (
                    s.wait_ge(SEM["mm"], tm),
                    s.wait_ge(SEM["ld"], so) if so else None,
                    s.copy(out=ocbE, in_=p_[:, 0:512]))[-1])
                if en == "sync":
                    w("act", aev)
                st_o = ld(och[m2 * 128:(m2 + 1) * 128, r0:r0 + 512], ocbE)
                if en == "tensor":
                    w("act", aev)
            barrier()

            prev_last = 0
            for tt in range(4):
                if en == "sync":
                    w("dve", prev_last)
                t_y = ld(ytE, och[:, r0 + tt * 128: r0 + (tt + 1) * 128], transpose=True)
                t_a = ld(xaE, xr3[r0 + 128 * tt: r0 + 128 * (tt + 1), :])
                if en in ("scalar", "vector"):
                    w("ld", t_y)
                rstd, mb, _ = ln_stats(ytE, 0, sqjE)
                ln_apply(tE, ytE, rstd, mb)
                if en == "vector":
                    w("ld", t_a)
                d_r = dve(lambda v: v.tensor_tensor(tE, tE, xaE, ALU.add))
                prev_last = d_r
                if en == "sync":
                    w("dve", d_r)
                t_st = ld(out_e[r0 + 128 * tt: r0 + 128 * (tt + 1), :], tE)
                if en == "vector":
                    w("ld", t_st)
            barrier()
        barrier()

    with nc.Block() as block:
        @block.sync
        def _(e):
            stream("sync", e)

        @block.gpsimd
        def _(e):
            stream("gpsimd", e)

        @block.tensor
        def _(e):
            stream("tensor", e)

        @block.scalar
        def _(e):
            stream("scalar", e)

        @block.vector
        def _(e):
            stream("vector", e)

    es.close()
    return nc



def _lhsT_tiles(A):
    K, M = A.shape
    kt, mt = K // 128, M // 128
    return np.ascontiguousarray(
        A.reshape(kt, 128, mt, 128).transpose(2, 1, 0, 3).reshape(mt, 128, kt * 128))


def kernel(**inp):
    x = np.asarray(inp["L0_attn_x"])
    Wa = np.asarray(inp["L0_attn_Wa"]); Wx = np.asarray(inp["L0_attn_Wx"])
    ba = np.asarray(inp["L0_attn_ba"]); bx = np.asarray(inp["L0_attn_bx"])
    Lam = np.asarray(inp["L0_attn_Lam"])
    w1 = np.asarray(inp["L0_ffn_w1"]); w3 = np.asarray(inp["L0_ffn_w3"])
    w2 = np.asarray(inp["L0_ffn_w2"])
    Wq = np.asarray(inp["L1_attn_Wq"]); bq = np.asarray(inp["L1_attn_bq"])
    Wk = np.asarray(inp["L1_attn_Wk"]); bk = np.asarray(inp["L1_attn_bk"])
    Wv = np.asarray(inp["L1_attn_Wv"]); bv = np.asarray(inp["L1_attn_bv"])
    Wo = np.asarray(inp["L1_attn_Wo"]); bo = np.asarray(inp["L1_attn_bo"])
    wg = np.asarray(inp["L1_ffn_w_gate"]); wu = np.asarray(inp["L1_ffn_w_up"])
    wd = np.asarray(inp["L1_ffn_w_down"])

    f32 = np.float32
    wat = _lhsT_tiles(np.ascontiguousarray(Wa.T).astype(f32))
    wxt = _lhsT_tiles(np.ascontiguousarray(Wx.T).astype(f32))
    w1p = np.zeros((D, H0), f32); w1p[:, :H0_TRUE] = w1.T
    w3p = np.zeros((D, H0), f32); w3p[:, :H0_TRUE] = w3.T
    w2p = np.zeros((H0, D), f32); w2p[:H0_TRUE, :] = w2.T
    w1tt = _lhsT_tiles(w1p)
    w3tt = _lhsT_tiles(w3p)
    w2tt = _lhsT_tiles(w2p)
    wqt = _lhsT_tiles(Wq.astype(f32))
    wkt = _lhsT_tiles(Wk.astype(f32))
    wvt = _lhsT_tiles(Wv.astype(f32))
    wot = _lhsT_tiles(Wo.astype(f32))
    wg_t = np.ascontiguousarray(wg.reshape(KT, 128, H1)).astype(f32)
    wu_t = np.ascontiguousarray(wu.reshape(KT, 128, H1)).astype(f32)
    wd_t = np.ascontiguousarray(_lhsT_tiles(wd.astype(f32)).reshape(KT, 128, H1))

    def colvec(v, n):
        return np.ascontiguousarray(np.asarray(v).reshape(n, 128).T).astype(f32)

    ba_t = colvec(ba, KT); bx_t = colvec(bx, KT); lam_t = colvec(Lam, KT)
    bq_t = colvec(bq, KT); bk_t = colvec(bk, 2); bv_t = colvec(bv, 2)
    bo_t = colvec(bo, KT)
    eye = np.eye(128).astype(ml_dtypes.bfloat16)
    cst = np.zeros((128, 2), f32); cst[:, 0] = 1.0; cst[:, 1] = EPS
    onesr = np.ones((1, 128), f32)

    nc = build_nc()
    in_maps = []
    for core in range(8):
        b, p = core // 2, core % 2
        pm = np.zeros((128, 2), f32)
        pm[:, 0] = 1.0 - p
        pm[:, 1] = p
        in_maps.append({
            "x0": np.ascontiguousarray(x[b]).astype(f32),
            "ident": eye,
            "wat": wat, "wxt": wxt, "ba_t": ba_t, "bx_t": bx_t, "lam_t": lam_t,
            "w1t": w1tt, "w3t": w3tt, "w2t": w2tt,
            "wqt": wqt, "wkt": wkt, "wvt": wvt, "wot": wot,
            "bq_t": bq_t, "bk_t": bk_t, "bv_t": bv_t, "bo_t": bo_t,
            "wg_in": wg_t, "wu_in": wu_t, "wd_in": wd_t,
            "pmask": pm, "consts": cst, "onesr_in": onesr,
        })
    res = run_bass_kernel_spmd(nc, in_maps, core_ids=list(range(8)))
    out = np.empty((4, S, D), np.float32)
    for core in range(8):
        b, p = core // 2, core % 2
        out[b, p * 2048:(p + 1) * 2048, :] = res.results[core]["out"]
    return out




# version 02
# speedup vs baseline: 1.1751x; 1.1751x over previous
import numpy as np
import ml_dtypes
import concourse.bass as bass
import concourse.mybir as mybir
from concourse.bass_utils import run_bass_kernel_spmd

F32 = mybir.dt.float32
BF16 = mybir.dt.bfloat16
AF = mybir.ActivationFunctionType
ALU = mybir.AluOpType
AX = mybir.AxisListType.X

D = 2048
S = 4096
KT = 16
H0 = 5504
H0T = 43
H0_TRUE = 5461
H1 = 8192
H1T = 64
NW = 8
WIN = 1152
EPS = 1e-5
ATT_SCALE = float(1.0 / np.sqrt(128.0))


def build_nc(debug=False):
    nc = bass.Bass("TRN2", target_bir_lowering=False)

    x0 = nc.dram_tensor("x0", [S, D], F32, kind="ExternalInput").ap()
    ident = nc.dram_tensor("ident", [128, 128], BF16, kind="ExternalInput").ap()
    wat = nc.dram_tensor("wat", [KT, 128, D], F32, kind="ExternalInput").ap()
    wxt = nc.dram_tensor("wxt", [KT, 128, D], F32, kind="ExternalInput").ap()
    ba_t = nc.dram_tensor("ba_t", [128, KT], F32, kind="ExternalInput").ap()
    bx_t = nc.dram_tensor("bx_t", [128, KT], F32, kind="ExternalInput").ap()
    lam_t = nc.dram_tensor("lam_t", [128, KT], F32, kind="ExternalInput").ap()
    w1t = nc.dram_tensor("w1t", [H0T, 128, D], F32, kind="ExternalInput").ap()
    w3t = nc.dram_tensor("w3t", [H0T, 128, D], F32, kind="ExternalInput").ap()
    w2t = nc.dram_tensor("w2t", [KT, 128, H0], F32, kind="ExternalInput").ap()
    wqt = nc.dram_tensor("wqt", [KT, 128, D], F32, kind="ExternalInput").ap()
    wkt = nc.dram_tensor("wkt", [2, 128, D], F32, kind="ExternalInput").ap()
    wvt = nc.dram_tensor("wvt", [2, 128, D], F32, kind="ExternalInput").ap()
    wot = nc.dram_tensor("wot", [KT, 128, D], F32, kind="ExternalInput").ap()
    bq_t = nc.dram_tensor("bq_t", [128, KT], F32, kind="ExternalInput").ap()
    bk_t = nc.dram_tensor("bk_t", [128, 2], F32, kind="ExternalInput").ap()
    bv_t = nc.dram_tensor("bv_t", [128, 2], F32, kind="ExternalInput").ap()
    bo_t = nc.dram_tensor("bo_t", [128, KT], F32, kind="ExternalInput").ap()
    wg_in = nc.dram_tensor("wg_in", [KT, 128, H1], F32, kind="ExternalInput").ap()
    wu_in = nc.dram_tensor("wu_in", [KT, 128, H1], F32, kind="ExternalInput").ap()
    wd_in = nc.dram_tensor("wd_in", [KT, 128, H1], F32, kind="ExternalInput").ap()
    pmask = nc.dram_tensor("pmask", [128, 2], F32, kind="ExternalInput").ap()
    consts = nc.dram_tensor("consts", [128, 2], F32, kind="ExternalInput").ap()
    onesr_in = nc.dram_tensor("onesr_in", [1, 128], F32, kind="ExternalInput").ap()

    out_e = nc.dram_tensor("out", [2048, D], F32, kind="ExternalOutput").ap()

    ikind = "ExternalOutput" if debug else "Internal"
    LOGA = nc.dram_tensor("LOGA", [KT, 128, S], BF16, kind="Internal").ap()
    UDAT = nc.dram_tensor("UDAT", [KT, 128, S], BF16, kind="Internal").ap()
    YSC = nc.dram_tensor("YSC", [D, S], BF16, kind="Internal").ap()
    xr1 = nc.dram_tensor("xr1", [S, D], F32, kind=ikind).ap()
    xr2 = nc.dram_tensor("xr2", [S, D], F32, kind=ikind).ap()
    xr3 = nc.dram_tensor("xr3", [2048, D], F32, kind=ikind).ap()
    q1s = nc.dram_tensor("q1s", [H0T, 128, D], BF16, kind="Internal").ap()
    q3s = nc.dram_tensor("q3s", [H0T, 128, D], BF16, kind="Internal").ap()
    q2s = nc.dram_tensor("q2s", [KT, 128, H0], BF16, kind="Internal").ap()
    wg_s = nc.dram_tensor("wg_s", [KT, 128, H1], BF16, kind="Internal").ap()
    wu_s = nc.dram_tensor("wu_s", [KT, 128, H1], BF16, kind="Internal").ap()
    wd_s = nc.dram_tensor("wd_s", [KT, 128, H1], BF16, kind="Internal").ap()
    och = nc.dram_tensor("och", [D, 2048], BF16, kind="Internal").ap()

    import contextlib
    es = contextlib.ExitStack()
    sb = lambda name, shape, dt: es.enter_context(nc.sbuf_tensor(name, shape, dt))

    NBF = 64 * 1024
    NF32 = 18 * 1024 + 512
    ABF = sb("ABF", [128, NBF], BF16)
    AF32 = sb("AF32", [128, NF32], F32)
    small = sb("small", [128, 128], F32)
    stat = sb("stat", [128, 64], F32)
    idn = sb("idn", [128, 128], BF16)
    ones_r = sb("ones_r", [1, 128], F32)
    ones_c = sb("ones_c", [128, 1], F32)
    s11 = sb("s11", [1, 8], F32)

    def f32s(c0, n):
        assert c0 + n <= NF32, (c0, n)
        return AF32[:, c0:c0 + n]

    def bfs(c0, n):
        assert c0 + n <= NBF, (c0, n)
        return ABF[:, c0:c0 + n]

    ba_c = small[:, 0:16]
    bx_c = small[:, 16:32]
    mco_c = small[:, 32:48]
    bq_c = small[:, 48:64]
    bk_c = small[:, 64:66]
    bv_c = small[:, 66:68]
    bo_c = small[:, 68:84]
    thr1 = small[:, 84:85]
    thr3 = small[:, 85:86]
    thr2 = small[:, 86:87]
    s1c = small[:, 87:88]
    me_c = small[:, 88:89]
    mo_c = small[:, 89:90]
    lam_c = small[:, 90:106]
    eps_c = small[:, 111:112]
    nthr1 = small[:, 107:108]
    ones_cc = small[:, 110:111]
    nthr3 = small[:, 108:109]
    nthr2 = small[:, 109:110]

    PS = [es.enter_context(nc.psum_tensor(f"PS{i}", [128, 512], F32)) for i in range(6)]
    PB = [es.enter_context(nc.psum_tensor(f"PB{i}", [128, 1024], BF16)) for i in range(2)]

    SEM = {}
    for s_ in ("a", "o", "w1", "w2", "w3", "s", "q", "wc", "ga", "gb",
               "mm", "act", "dve"):
        SEM[s_] = es.enter_context(nc.semaphore(s_))

    DMAF = ("a", "o", "w1", "w2", "w3", "s", "q", "wc", "ga", "gb")

    def stream(en, e):
        cnt = {f: 0 for f in DMAF}
        cnt.update({"mm": 0, "act": 0, "dve": 0})

        def ld(out_ap, in_ap, fam="a", transpose=False):
            prev = cnt[fam]
            cnt[fam] += 16
            if en == "sync":
                if prev:
                    e.wait_ge(SEM[fam], prev)
                e.dma_start(out=out_ap, in_=in_ap, transpose=transpose).then_inc(SEM[fam], 16)
            return (fam, cnt[fam])

        def gd(out_ap, in_ap, fam="ga"):
            prev = cnt[fam]
            cnt[fam] += 16
            if en == "gpsimd":
                if prev:
                    e.wait_ge(SEM[fam], prev)
                e.dma_start(out=out_ap, in_=in_ap).then_inc(SEM[fam], 16)
            return (fam, cnt[fam])

        def w(sem, val):
            if val:
                e.wait_ge(SEM[sem], val)

        def wv(tok):
            if tok:
                e.wait_ge(SEM[tok[0]], tok[1])

        def lwv(tok):
            return tok

        def mm(out, lhsT, rhs, start, stop, inc=True):
            if inc:
                cnt["mm"] += 1
            if en == "tensor":
                i = e.matmul(out, lhsT, rhs, start=start, stop=stop)
                if inc:
                    i.then_inc(SEM["mm"], 1)
            return cnt["mm"]

        def tp(out, in_, inc=True):
            if inc:
                cnt["mm"] += 1
            if en == "tensor":
                i = e.transpose(out, in_, idn[:])
                if inc:
                    i.then_inc(SEM["mm"], 1)
            return cnt["mm"]

        def act(fn, inc=True):
            if inc:
                cnt["act"] += 1
            if en == "scalar":
                if cnt["act"] > 1:
                    e.wait_ge(SEM["act"], cnt["act"] - 1)
                i = fn(e)
                if inc:
                    i.then_inc(SEM["act"], 1)
            return cnt["act"]

        def dve(fn, inc=True):
            if inc:
                cnt["dve"] += 1
            if en == "vector":
                if cnt["dve"] > 1:
                    e.wait_ge(SEM["dve"], cnt["dve"] - 1)
                i = fn(e)
                if inc:
                    i.then_inc(SEM["dve"], 1)
            return cnt["dve"]

        def barrier():
            tgt = dict(cnt)
            for s_ in list(DMAF) + ["mm", "act", "dve"]:
                w(s_, tgt[s_])

        def ln_stats(src, sc, sq_junk, pre_act=0, pre_dve=0):
            ssq = stat[:, sc:sc + 1]
            sm = stat[:, sc + 1:sc + 2]
            mn = stat[:, sc + 2:sc + 3]
            var = stat[:, sc + 3:sc + 4]
            rstd = stat[:, sc + 4:sc + 5]
            mb = stat[:, sc + 5:sc + 6]
            a1 = act(lambda s, pa=pre_act: (
                s.wait_ge(SEM["act"], pa) if pa else None,
                s.activation(out=sq_junk, in_=src, func=AF.Square, accum_out=ssq))[-1])
            d1 = dve(lambda v, pd=pre_dve: (
                v.wait_ge(SEM["dve"], pd) if pd else None,
                v.reduce_sum(sm, src, AX))[-1])
            if en == "vector":
                w("act", a1)
            dve(lambda v: v.tensor_scalar_mul(mn, sm, 1.0 / D))
            dve(lambda v: v.tensor_scalar_mul(var, ssq, 1.0 / D))
            dve(lambda v: v.tensor_tensor(mb, mn, mn, ALU.mult))
            d3 = dve(lambda v: v.tensor_tensor(var, var, mb, ALU.subtract))
            a2 = act(lambda s, dd=d3: (
                s.wait_ge(SEM["dve"], dd),
                s.activation(out=rstd, in_=var, func=AF.Sqrt, bias=eps_c))[-1])
            if en == "vector":
                w("act", a2)
            dve(lambda v: v.reciprocal(rstd, rstd))
            d5 = dve(lambda v: v.tensor_tensor(mb, mn, rstd, ALU.mult))
            return rstd, mb, d5

        def ln_apply(dst, src, rstd, mb):
            return dve(lambda v: v.tensor_scalar(dst, src, rstd, mb,
                                                 ALU.mult, ALU.subtract))

        def emit_t16(xln_ap, dst_tensor_ap, stride, coff, dapp):
            last = 0
            for half in range(2):
                pb = PB[half]
                if en == "tensor":
                    w("dve", dapp)
                tmm = 0
                for kk in range(8):
                    k = half * 8 + kk
                    tmm = tp(pb[:, kk * 128:(kk + 1) * 128],
                             xln_ap[:, k * 128:(k + 1) * 128])
                dst = dst_tensor_ap.rearrange("p (k c) -> p k c", c=stride)[
                    :, half * 8:(half + 1) * 8, coff:coff + 128]
                srcp = pb[:].rearrange("p (k c) -> p k c", k=8)
                last = act(lambda s, d=dst, sr=srcp, tm=tmm: (
                    s.wait_ge(SEM["mm"], tm),
                    s.copy(out=d, in_=sr))[-1])
                if en == "tensor":
                    w("act", last)
            return cnt["mm"], last

        ld(ba_c, ba_t)
        ld(bx_c, bx_t)
        ld(lam_c, lam_t)
        ld(bq_c, bq_t)
        ld(bk_c, bk_t)
        ld(bv_c, bv_t)
        ld(bo_c, bo_t)
        ld(small[:, 88:90], pmask)
        ld(idn[:], ident)
        ld(small[:, 110:112], consts)
        ld(ones_r[:], onesr_in)
        barrier()
        aS1 = act(lambda s: s.activation(out=mco_c, in_=lam_c, func=AF.Exp, scale=-1.0))
        aS2 = act(lambda s, p=aS1: (s.wait_ge(SEM["act"], p),
                  s.activation(out=mco_c, in_=mco_c, func=AF.Ln, bias=1.0))[-1])
        act(lambda s, p=aS2: (s.wait_ge(SEM["act"], p),
            s.mul(mco_c, mco_c, -8.0))[-1])
        barrier()

        XL0T = bfs(0, 32768)
        wam0 = bfs(32768, 2048)
        wam1 = bfs(34816, 2048)
        wxm0 = bfs(36864, 2048)
        wxm1 = bfs(38912, 2048)
        logab0 = bfs(40960, 2048)
        ub0 = bfs(43008, 2048)
        logab1 = bfs(45056, 2048)
        ub1 = bfs(47104, 2048)
        xlnA = bfs(49152, 2048)
        sqjA = bfs(51200, 2048)
        p1junk = bfs(53248, 5504)
        xtA0 = f32s(0, 2048)
        xtA1 = f32s(2048, 2048)
        rA = f32s(4096, 2048)
        iA = f32s(6144, 2048)
        ixA = f32s(8192, 2048)
        sqA = f32s(10240, 2048)
        wp1 = f32s(12288, 5504)
        QACC = f32s(17920, 104)

        pass1_list = ([(w1t[i], D) for i in range(H0T)] +
                      [(w3t[i], D) for i in range(H0T)] +
                      [(w2t[i], H0) for i in range(KT)])
        p1 = {"idx": 0, "aa": 0}

        def emit_pass1_some(nmax):
            done = 0
            while p1["idx"] < len(pass1_list) and done < nmax:
                ap_, cols = pass1_list[p1["idx"]]
                if en == "sync":
                    w("act", p1["aa"])
                t_ld = ld(wp1[:, 0:cols], ap_, "w3")
                p1["aa"] = act(lambda s, c_=cols, tl=t_ld, ii=p1["idx"]: (
                    s.wait_ge(SEM[tl[0]], tl[1]),
                    s.activation(out=p1junk[:, 0:c_], in_=wp1[:, 0:c_], func=AF.Abs,
                                 accum_out=QACC[:, ii:ii + 1]))[-1])
                p1["idx"] += 1
                done += 1

        for p in range(2):
            t0 = 2048 * p
            prev_tp = 0
            prev_ev = 0
            prev_dappA = 0
            for tt in range(16):
                xt = xtA0 if tt % 2 == 0 else xtA1
                if en == "sync":
                    w("dve", prev_dappA)
                t_x = ld(xt, x0[t0 + 128 * tt: t0 + 128 * (tt + 1), :])
                if en in ("scalar", "vector"):
                    wv(t_x)
                rstd, mb, dstat = ln_stats(xt, 0 if tt % 2 == 0 else 8, sqjA,
                                           pre_act=prev_ev)
                if en == "vector":
                    w("mm", prev_tp)
                dapp = ln_apply(xlnA, xt, rstd, mb)
                prev_dappA = dapp
                prev_tp, prev_ev = emit_t16(xlnA, XL0T[:], 2048, tt * 128, dapp)
                emit_pass1_some(1)
            barrier()

            nxt_w = (gd(wam0, wat[0], "ga"), gd(wxm0, wxt[0], "gb"))
            st_l = None
            st_u = None
            prev_du = 0
            prev_mlast = 0
            for m in range(KT):
                wam = wam0 if m % 2 == 0 else wam1
                wxm = wxm0 if m % 2 == 0 else wxm1
                t_w = nxt_w
                if m + 1 < KT:
                    if en == "gpsimd":
                        w("mm", prev_mlast)
                    nxt_w = (gd(wam1 if m % 2 == 0 else wam0, wat[m + 1], "ga"),
                             gd(wxm1 if m % 2 == 0 else wxm0, wxt[m + 1], "gb"))
                lb = logab0 if m % 2 == 0 else logab1
                uo = ub0 if m % 2 == 0 else ub1
                a_last = 0
                for c in range(4):
                    psr = PS[c % 2]
                    psi = PS[2 + c % 2]
                    if en == "tensor" and c == 0:
                        wv(t_w[0])
                        wv(t_w[1])
                    tmr = 0
                    for k in range(KT):
                        tmr = mm(psr[:, 0:512], wam[:, k * 128:(k + 1) * 128],
                                 XL0T[:, k * 2048 + 512 * c: k * 2048 + 512 * (c + 1)],
                                 start=(k == 0), stop=(k == KT - 1), inc=(k == KT - 1))
                    tmi = 0
                    for k in range(KT):
                        tmi = mm(psi[:, 0:512], wxm[:, k * 128:(k + 1) * 128],
                                 XL0T[:, k * 2048 + 512 * c: k * 2048 + 512 * (c + 1)],
                                 start=(k == 0), stop=(k == KT - 1), inc=(k == KT - 1))
                    ar = act(lambda s, ps=psr, cc=c, mm_=m, tm=tmr, pd=prev_du: (
                        s.wait_ge(SEM["mm"], tm),
                        s.wait_ge(SEM["dve"], pd) if (pd and cc == 0) else None,
                        s.activation(out=rA[:, 512 * cc:512 * (cc + 1)], in_=ps[:, 0:512],
                                     func=AF.Sigmoid, bias=ba_c[:, mm_:mm_ + 1]))[-1])
                    ai = act(lambda s, ps=psi, cc=c, mm_=m, tm=tmi: (
                        s.wait_ge(SEM["mm"], tm),
                        s.activation(out=iA[:, 512 * cc:512 * (cc + 1)], in_=ps[:, 0:512],
                                     func=AF.Sigmoid, bias=bx_c[:, mm_:mm_ + 1]))[-1])
                    a_last = ai
                    if en == "tensor":
                        w("act", ai)
                    if c == 3:
                        prev_mlast = tmi
                if en == "vector":
                    w("act", a_last)
                    wv(st_l)
                d_lg = dve(lambda v, lb_=lb, mm_=m: v.tensor_scalar_mul(
                    lb_, rA, mco_c[:, mm_:mm_ + 1]))
                d_ix = dve(lambda v, mm_=m: v.tensor_tensor(
                    ixA, iA, XL0T[:, mm_ * 2048:(mm_ + 1) * 2048], ALU.mult))
                a_a2 = act(lambda s, lb_=lb, dl=d_lg: (
                    s.wait_ge(SEM["dve"], dl),
                    s.activation(out=sqA, in_=lb_, func=AF.Exp, scale=2.0))[-1])
                a_sq = act(lambda s, p=a_a2: (s.wait_ge(SEM["act"], p),
                    s.activation(out=sqA, in_=sqA, func=AF.Sqrt,
                                 scale=-1.0, bias=1.0))[-1])
                if en == "vector":
                    w("act", a_sq)
                    wv(st_u)
                d_u = dve(lambda v, uo_=uo: v.tensor_tensor(uo_, ixA, sqA, ALU.mult))
                prev_du = d_u
                if en == "sync":
                    w("dve", d_u)
                st_l = ld(LOGA[m][:, t0:t0 + 2048], lb, "s")
                st_u = ld(UDAT[m][:, t0:t0 + 2048], uo, "s")
                emit_pass1_some(2)
            barrier()
        emit_pass1_some(10000)
        barrier()

        ranges = [(0, H0T, H0_TRUE * D, thr1, True),
                  (H0T, 2 * H0T, H0_TRUE * D, thr3, False),
                  (2 * H0T, 2 * H0T + KT, D * H0_TRUE, thr2, False)]
        for idx, (c0, c1, count, thr_col, want_s1) in enumerate(ranges):
            dr = dve(lambda v, a=c0, b=c1, ii=idx: v.reduce_sum(
                stat[:, 32 + ii:33 + ii], QACC[:, a:b], AX))
            if en == "tensor":
                w("dve", dr)
            tm = mm(PS[4][0:1, idx:idx + 1], stat[:, 32 + idx:33 + idx], ones_cc,
                    start=True, stop=True)
            aa = act(lambda s, ii=idx, cc=count, tm_=tm: (
                s.wait_ge(SEM["mm"], tm_),
                s.activation(out=s11[0:1, ii:ii + 1], in_=PS[4][0:1, ii:ii + 1],
                             func=AF.Copy, scale=0.5 / cc))[-1])
            if en == "tensor":
                w("act", aa)
            tb = mm(PS[5][:, idx:idx + 1], ones_r[:], s11[0:1, idx:idx + 1],
                    start=True, stop=True)
            a_thr = act(lambda s, th=thr_col, tb_=tb: (
                s.wait_ge(SEM["mm"], tb_),
                s.copy(out=th, in_=PS[5][:, idx:idx + 1]))[-1])
            if want_s1:
                act(lambda s: s.mul(s1c, thr1, 2.0))
            nth = (nthr1, nthr3, nthr2)[idx]
            dve(lambda v, th=thr_col, nt=nth, p=a_thr: (
                v.wait_ge(SEM["act"], p),
                v.tensor_scalar_mul(nt, th, -1.0))[-1])
            barrier()

        lgB0 = bfs(0, 4096)
        lgB1 = bfs(4096, 4096)
        uB0 = bfs(8192, 4096)
        uB1 = bfs(12288, 4096)
        yB0 = bfs(16384, 4096)
        yB1 = bfs(20480, 4096)
        sgB = bfs(24576, 5504)
        mkB = bfs(30080, 5504)
        qB = bfs(35584, 5504)
        yttB = bfs(41088, 2048)
        sqjB = bfs(43136, 2048)
        aB = f32s(0, 4096)
        uB32 = f32s(4096, 4096)
        xtB = f32s(8192, 2048)
        tB = f32s(10240, 2048)

        pass2_list = ([(w1t[i], q1s[i], D, thr1, nthr1) for i in range(H0T)] +
                      [(w3t[i], q3s[i], D, thr3, nthr3) for i in range(H0T)] +
                      [(w2t[i], q2s[i], H0, thr2, nthr2) for i in range(KT)])
        p2 = {"idx": 0, "st": None, "dmk": 0, "dq": 0}

        def emit_pass2_some(nmax):
            done = 0
            while p2["idx"] < len(pass2_list) and done < nmax:
                src, dst, cols, thr_col, nth_col = pass2_list[p2["idx"]]
                if en == "sync":
                    w("dve", p2["dmk"])
                t_ld = ld(wp1[:, 0:cols], src, "w3")
                dpos = dve(lambda v, c_=cols, th=thr_col, tl=t_ld: (
                    v.wait_ge(SEM[tl[0]], tl[1]),
                    v.tensor_scalar(mkB[:, 0:c_], wp1[:, 0:c_], th, None,
                                    ALU.is_gt))[-1])
                dmk = dve(lambda v, c_=cols, nt=nth_col, pl=p2["st"]: (
                    v.wait_ge(SEM[pl[0]], pl[1]) if pl else None,
                    v.tensor_scalar(sgB[:, 0:c_], wp1[:, 0:c_], nt, None,
                                    ALU.is_lt))[-1])
                dq = dve(lambda v, c_=cols: v.tensor_tensor(
                    qB[:, 0:c_], mkB[:, 0:c_], sgB[:, 0:c_], ALU.subtract))
                if en == "sync":
                    w("dve", dq)
                p2["st"] = ld(dst, qB[:, 0:cols], "q")
                p2["dmk"] = dmk
                p2["dq"] = dq
                p2["idx"] += 1
                done += 1

        st_y = None
        prev_dy = 0
        prev_au = 0
        nxt_in = (ld(lgB0, LOGA[0][:, :], "w1"), ld(uB0, UDAT[0][:, :], "w2"))
        for m in range(KT):
            lg = lgB0 if m % 2 == 0 else lgB1
            uu = uB0 if m % 2 == 0 else uB1
            yy = yB0 if m % 2 == 0 else yB1
            t_in = nxt_in
            if m + 1 < KT:
                if en == "sync":
                    w("act", prev_au)
                nxt_in = (ld(lgB1 if m % 2 == 0 else lgB0, LOGA[m + 1][:, :], "w1"),
                          ld(uB1 if m % 2 == 0 else uB0, UDAT[m + 1][:, :], "w2"))
            a_a = act(lambda s, lg_=lg, ti=t_in, pd=prev_dy: (
                s.wait_ge(SEM[ti[0][0]], ti[0][1]),
                s.wait_ge(SEM[ti[1][0]], ti[1][1]),
                s.wait_ge(SEM["dve"], pd) if pd else None,
                s.activation(out=aB, in_=lg_, func=AF.Exp))[-1])
            a_u = act(lambda s, uu_=uu: s.copy(out=uB32, in_=uu_))
            prev_au = a_u
            if en == "vector":
                w("act", a_u)
                wv(st_y)
            dve(lambda v, yy_=yy: v.tensor_tensor_scan(
                out=yy_[:, 0:2048], data0=aB[:, 0:2048], data1=uB32[:, 0:2048],
                initial=0.0, op0=ALU.mult, op1=ALU.add))
            dve(lambda v, yy_=yy: v.tensor_copy(stat[:, 30:31], yy_[:, 2047:2048]))
            d_y = dve(lambda v, yy_=yy: v.tensor_tensor_scan(
                out=yy_[:, 2048:4096], data0=aB[:, 2048:4096], data1=uB32[:, 2048:4096],
                initial=stat[:, 30:31], op0=ALU.mult, op1=ALU.add))
            prev_dy = d_y
            if en == "sync":
                w("dve", d_y)
            st_y = ld(YSC[m * 128:(m + 1) * 128, :], yy, "s")
            emit_pass2_some(4)
        emit_pass2_some(10000)
        barrier()

        prev_dr = 0
        prev_st = None
        for tt in range(32):
            if en == "sync":
                w("dve", prev_dr)
            t_y = ld(yttB, YSC[:, tt * 128:(tt + 1) * 128], "o", transpose=True)
            t_x = ld(xtB, x0[tt * 128:(tt + 1) * 128, :])
            if en in ("scalar", "vector"):
                wv(t_y)
            if en == "vector":
                wv(prev_st)
            rstd, mb, _ = ln_stats(yttB, 0, sqjB)
            ln_apply(tB, yttB, rstd, mb)
            if en == "vector":
                wv(t_x)
            d_r = dve(lambda v: v.tensor_tensor(tB, tB, xtB, ALU.add))
            prev_dr = d_r
            if en == "sync":
                w("dve", d_r)
            prev_st = ld(xr1[tt * 128:(tt + 1) * 128, :], tB, "s")
        barrier()

        XL1T = bfs(0, 8192)
        GBLK = bfs(8192, 43 * 512)
        q1b0 = bfs(30208, 2048)
        q1b1 = bfs(32256, 2048)
        q3b0 = bfs(34304, 2048)
        q3b1 = bfs(36352, 2048)
        q2b0 = bfs(38400, 5504)
        q2b1 = bfs(43904, 5504)
        xlnC = bfs(49408, 2048)
        sqjC = bfs(51456, 2048)
        ocbC = bfs(53504, 512)
        ytC = bfs(54016, 2048)
        xaC = f32s(0, 2048)
        xbC = f32s(2048, 2048)
        xqC = f32s(4096, 2048)
        silC = f32s(6144, 512)
        tC = f32s(6656, 2048)

        cb0 = bfs(56064, 4096)
        cb1 = bfs(60160, 4096)
        wcast_units = []
        for pair in ([(wg_in[k], wg_s[k]) for k in range(KT)] +
                     [(wu_in[k], wu_s[k]) for k in range(KT)] +
                     [(wd_in[k], wd_s[k]) for k in range(KT)]):
            srcw, dstw = pair
            wcast_units.append((srcw[:, 0:4096], dstw[:, 0:4096]))
            wcast_units.append((srcw[:, 4096:8192], dstw[:, 4096:8192]))
        wc = {"idx": 0, "st": [None, None]}

        def emit_wcast_some(nmax):
            done = 0
            while wc["idx"] < len(wcast_units) and done < nmax:
                srcp, dstp = wcast_units[wc["idx"]]
                buf = cb0 if wc["idx"] % 2 == 0 else cb1
                if en == "gpsimd":
                    wv(wc["st"][wc["idx"] % 2])
                t_c = gd(buf, srcp, "gb")
                if en == "sync":
                    wv(t_c)
                wc["st"][wc["idx"] % 2] = ld(dstp, buf, "wc")
                wc["idx"] += 1
                done += 1

        blocks = [("P", 1024, None), ("P", 1536, None), ("P", 2048, None),
                  ("P", 2560, None), ("Q", 0, 3072), ("Q", 512, 3584)]

        for btype, rA_, rB_ in blocks:
            prev_tp = 0
            prev_ev = 0
            prev_dapp = 0
            for tt in range(4):
                if en == "sync":
                    w("dve", prev_dapp)
                t_a = ld(xaC, xr1[rA_ + 128 * tt: rA_ + 128 * (tt + 1), :])
                src = xaC
                dpre = 0
                if btype == "Q":
                    t_b = ld(xbC, xr1[rB_ + 128 * tt: rB_ + 128 * (tt + 1), :])
                    if en == "vector":
                        wv(t_b)
                    dve(lambda v: v.tensor_scalar_mul(xqC, xaC, me_c))
                    dpre = dve(lambda v: v.scalar_tensor_tensor(
                        xqC, xbC, mo_c, xqC, ALU.mult, ALU.add))
                    src = xqC
                else:
                    if en == "vector":
                        wv(t_a)
                if en == "scalar":
                    if btype == "Q":
                        w("dve", dpre)
                    else:
                        wv(t_a)
                rstd, mb, _ = ln_stats(src, 0, sqjC, pre_act=prev_ev)
                if en == "vector":
                    w("mm", prev_tp)
                dapp = ln_apply(xlnC, src, rstd, mb)
                prev_dapp = dapp
                prev_tp, prev_ev = emit_t16(xlnC, XL1T[:], 512, tt * 128, dapp)
            barrier()

            nxt_q = (ld(q1b0, q1s[0], "w1"), ld(q3b0, q3s[0], "w2"))
            prev_dg = 0
            prev_h_mm = 0
            for mh in range(H0T):
                qb1 = q1b0 if mh % 2 == 0 else q1b1
                qb3 = q3b0 if mh % 2 == 0 else q3b1
                t_ql = nxt_q
                if mh + 1 < H0T:
                    if en == "sync":
                        w("mm", prev_h_mm)
                    nxt_q = (ld(q1b1 if mh % 2 == 0 else q1b0, q1s[mh + 1], "w1"),
                             ld(q3b1 if mh % 2 == 0 else q3b0, q3s[mh + 1], "w2"))
                emit_wcast_some(1)
                ph1 = PS[mh % 2]
                ph3 = PS[2 + mh % 2]
                if en == "tensor":
                    wv(t_ql[0])
                    wv(t_ql[1])
                tm1 = 0
                for k in range(KT):
                    tm1 = mm(ph1[:, 0:512], qb1[:, k * 128:(k + 1) * 128],
                             XL1T[:, k * 512:(k + 1) * 512],
                             start=(k == 0), stop=(k == KT - 1), inc=(k == KT - 1))
                tm3 = 0
                for k in range(KT):
                    tm3 = mm(ph3[:, 0:512], qb3[:, k * 128:(k + 1) * 128],
                             XL1T[:, k * 512:(k + 1) * 512],
                             start=(k == 0), stop=(k == KT - 1), inc=(k == KT - 1))
                a_s = act(lambda s, p_=ph1, tm=tm1, pd=prev_dg: (
                    s.wait_ge(SEM["mm"], tm),
                    s.wait_ge(SEM["dve"], pd) if pd else None,
                    s.activation(out=silC, in_=p_[:, 0:512], func=AF.Silu,
                                 scale=s1c))[-1])
                if en == "vector":
                    w("act", a_s)
                    w("mm", tm3)
                d_g = dve(lambda v, p_=ph3, mh_=mh: v.tensor_tensor(
                    GBLK[:, mh_ * 512:(mh_ + 1) * 512], silC, p_[:, 0:512], ALU.mult))
                prev_dg = d_g
                prev_h_mm = tm3
                if en == "tensor":
                    w("dve", d_g)

            nxt_q2 = ld(q2b0, q2s[0], "w3")
            st_o = None
            prev_d_mm = 0
            for m2 in range(KT):
                qb2 = q2b0 if m2 % 2 == 0 else q2b1
                t_ql = nxt_q2
                if m2 + 1 < KT:
                    if en == "sync":
                        w("mm", prev_d_mm)
                    nxt_q2 = ld(q2b1 if m2 % 2 == 0 else q2b0, q2s[m2 + 1], "w3")
                emit_wcast_some(1)
                po = PS[4 + m2 % 2]
                if en == "tensor":
                    wv(t_ql)
                tmo = 0
                for k in range(H0T):
                    tmo = mm(po[:, 0:512], qb2[:, k * 128:(k + 1) * 128],
                             GBLK[:, k * 512:(k + 1) * 512],
                             start=(k == 0), stop=(k == H0T - 1), inc=(k == H0T - 1))
                prev_d_mm = tmo
                aev = act(lambda s, p_=po, tm=tmo, so=st_o: (
                    s.wait_ge(SEM["mm"], tm),
                    s.wait_ge(SEM[so[0]], so[1]) if so else None,
                    s.copy(out=ocbC, in_=p_[:, 0:512]))[-1])
                if en == "sync":
                    w("act", aev)
                st_o = ld(och[m2 * 128:(m2 + 1) * 128, 0:512], ocbC, "q")
                if en == "tensor":
                    w("act", aev)
            barrier()

            prev_last = 0
            for tt in range(4):
                if en == "sync":
                    w("dve", prev_last)
                t_y = ld(ytC, och[:, tt * 128:(tt + 1) * 128], "o", transpose=True)
                t_a = ld(xaC, xr1[rA_ + 128 * tt: rA_ + 128 * (tt + 1), :])
                src = xaC
                if btype == "Q":
                    t_b = ld(xbC, xr1[rB_ + 128 * tt: rB_ + 128 * (tt + 1), :])
                    if en == "vector":
                        wv(t_b)
                    dve(lambda v: v.tensor_scalar_mul(xqC, xaC, me_c))
                    dve(lambda v: v.scalar_tensor_tensor(
                        xqC, xbC, mo_c, xqC, ALU.mult, ALU.add))
                    src = xqC
                else:
                    if en == "vector":
                        wv(t_a)
                if en in ("scalar", "vector"):
                    wv(t_y)
                rstd, mb, _ = ln_stats(ytC, 0, sqjC)
                ln_apply(tC, ytC, rstd, mb)
                d_r = dve(lambda v, s_=src: v.tensor_tensor(tC, tC, s_, ALU.add))
                prev_last = d_r
                if en == "sync":
                    w("dve", d_r)
                t_st = ld(xr2[rA_ + 128 * tt: rA_ + 128 * (tt + 1), :], tC, "s")
                if btype == "Q":
                    t_st = ld(xr2[rB_ + 128 * tt: rB_ + 128 * (tt + 1), :], tC, "s")
                if en == "vector":
                    wv(t_st)
            barrier()

        emit_wcast_some(100000)
        barrier()

        qS = bfs(0, KT * WIN)
        kS = bfs(18432, 2 * WIN)
        vS = bfs(20736, 2 * WIN)
        vT = bfs(23040, 2 * WIN)
        XWT = bfs(25344, KT * WIN)
        OSB = bfs(25344, KT * 2048)
        xlnD = bfs(43776, 2048)
        sqjD = bfs(45824, 2048)
        pnS = bfs(58112, 512)
        ptS = bfs(58624, 512)
        wqb0 = bfs(59136, 2048)
        wqb1 = bfs(61184, 2048)
        ytD = bfs(0, 2048)
        ocbD = bfs(2048, 512)
        sqjD2 = bfs(2560, 2048)
        xaD = f32s(0, 2048)
        xbD = f32s(2048, 2048)
        xqD = f32s(4096, 2048)
        tD = f32s(6144, 2048)

        prev_tp = 0
        prev_ev = 0
        prev_dapp = 0
        for tt in range(9):
            if en == "sync":
                w("dve", prev_dapp)
            t_a = ld(xaD, xr2[128 * tt:128 * (tt + 1), :])
            t_b = ld(xbD, xr2[1024 + 128 * tt: 1024 + 128 * (tt + 1), :])
            if en == "vector":
                wv(t_b)
            dve(lambda v: v.tensor_scalar_mul(xqD, xaD, me_c))
            dpre = dve(lambda v: v.scalar_tensor_tensor(
                xqD, xbD, mo_c, xqD, ALU.mult, ALU.add))
            if en == "scalar":
                w("dve", dpre)
            rstd, mb, _ = ln_stats(xqD, 0, sqjD, pre_act=prev_ev)
            if en == "vector":
                w("mm", prev_tp)
            dapp = ln_apply(xlnD, xqD, rstd, mb)
            prev_dapp = dapp
            prev_tp, prev_ev = emit_t16(xlnD, XWT[:], WIN, tt * 128, dapp)
        barrier()

        qkv_list = ([(wqt[m_], bq_c[:, m_:m_ + 1], qS, m_) for m_ in range(KT)] +
                    [(wkt[m_], bk_c[:, m_:m_ + 1], kS, m_) for m_ in range(2)] +
                    [(wvt[m_], bv_c[:, m_:m_ + 1], vS, m_) for m_ in range(2)])
        nxt_wq = gd(wqb0, qkv_list[0][0], "ga")
        prev_q_mm = 0
        for i_, (wsrc, bias, dstbuf, m_) in enumerate(qkv_list):
            wb = wqb0 if i_ % 2 == 0 else wqb1
            t_wl = nxt_wq
            if i_ + 1 < len(qkv_list):
                if en == "gpsimd":
                    w("mm", prev_q_mm)
                nxt_wq = gd(wqb1 if i_ % 2 == 0 else wqb0, qkv_list[i_ + 1][0], "ga")
            if en == "tensor":
                wv(t_wl)
            for ci, (c0, cw) in enumerate(((0, 512), (512, 512), (1024, 128))):
                pq = PS[ci % 2]
                tmq = 0
                for k in range(KT):
                    tmq = mm(pq[:, 0:cw], wb[:, k * 128:(k + 1) * 128],
                             XWT[:, k * WIN + c0: k * WIN + c0 + cw],
                             start=(k == 0), stop=(k == KT - 1), inc=(k == KT - 1))
                prev_q_mm = tmq
                aev = act(lambda s, p_=pq, d=dstbuf, mm_=m_, a=c0, ww=cw, b_=bias, tm=tmq: (
                    s.wait_ge(SEM["mm"], tm),
                    s.activation(out=d[:, mm_ * WIN + a: mm_ * WIN + a + ww],
                                 in_=p_[:, 0:ww], func=AF.Identity, bias=b_))[-1])
                if en == "tensor":
                    w("act", aev)
        for g_ in range(2):
            for tt in range(9):
                pb = PB[tt % 2]
                tmm = tp(pb[:, 0:128],
                         vS[:, g_ * WIN + tt * 128: g_ * WIN + (tt + 1) * 128])
                aev = act(lambda s, g0=g_, t0_=tt, tm=tmm: (
                    s.wait_ge(SEM["mm"], tm),
                    s.copy(out=vT[:, (g0 * 9 + t0_) * 128:(g0 * 9 + t0_ + 1) * 128],
                           in_=PB[t0_ % 2][:, 0:128]))[-1])
                if en == "tensor":
                    w("act", aev)
        barrier()

        last_o = {}
        last_p = {}
        prev_tmt = 0
        prev_tmo = 0
        for j in range(NW):
            for h in range(KT):
                g_ = h % 2
                ctx = (j * KT + h) % 2
                psS = PS[ctx]
                psO = PS[2 + ctx]
                pbP = PB[ctx]
                if en == "tensor" and ("s", ctx) in last_o:
                    w("act", last_o[("s", ctx)])
                tms = 0
                for qt in range(2):
                    tms = mm(psS[:, qt * 256:(qt + 1) * 256],
                             qS[:, h * WIN + j * 128 + qt * 128:
                                h * WIN + j * 128 + (qt + 1) * 128],
                             kS[:, g_ * WIN + j * 128: g_ * WIN + j * 128 + 256],
                             start=True, stop=True)
                dn_last = 0
                for qt in range(2):
                    s0 = 16 + 8 * qt
                    mx = stat[:, s0:s0 + 1]
                    mxs = stat[:, s0 + 1:s0 + 2]
                    rs = stat[:, s0 + 2:s0 + 3]
                    rr = stat[:, s0 + 3:s0 + 4]
                    dm = dve(lambda v, p_=psS, q_=qt, tm=tms, m_=mx: (
                        v.wait_ge(SEM["mm"], tm),
                        v.reduce_max(m_, p_[:, q_ * 256:(q_ + 1) * 256], AX))[-1])
                    dmx = dve(lambda v, m_=mx, ms=mxs: v.tensor_scalar_mul(
                        ms, m_, -ATT_SCALE))
                    ae = act(lambda s, p_=psS, q_=qt, dm_=dmx, ms=mxs, rs_=rs,
                             pt_=prev_tmt: (
                        s.wait_ge(SEM["dve"], dm_),
                        s.wait_ge(SEM["mm"], pt_) if (pt_ and q_ == 0) else None,
                        s.activation(out=pnS[:, q_ * 256:(q_ + 1) * 256],
                                     in_=p_[:, q_ * 256:(q_ + 1) * 256],
                                     func=AF.Exp, scale=ATT_SCALE, bias=ms,
                                     accum_out=rs_))[-1])
                    dr = dve(lambda v, a_=ae, rs_=rs, rr_=rr: (
                        v.wait_ge(SEM["act"], a_),
                        v.reciprocal(rr_, rs_))[-1])
                    dn_last = dve(lambda v, q_=qt, rr_=rr: v.tensor_scalar_mul(
                        pnS[:, q_ * 256:(q_ + 1) * 256],
                        pnS[:, q_ * 256:(q_ + 1) * 256], rr_))
                if en == "tensor":
                    w("dve", dn_last)
                    if ("p", ctx) in last_p:
                        w("act", last_p[("p", ctx)])
                tmt = 0
                for qt in range(2):
                    for kt in range(2):
                        tmt = tp(pbP[:, kt * 256 + qt * 128: kt * 256 + (qt + 1) * 128],
                                 pnS[:, qt * 256 + kt * 128: qt * 256 + (kt + 1) * 128])
                prev_tmt = tmt
                apt = act(lambda s, pb_=pbP, tm=tmt, po_=prev_tmo: (
                    s.wait_ge(SEM["mm"], tm),
                    s.wait_ge(SEM["mm"], po_) if po_ else None,
                    s.copy(out=ptS, in_=pb_[:, 0:512]))[-1])
                last_p[("p", ctx)] = apt
                if en == "tensor":
                    w("act", apt)
                tmo = 0
                for kt in range(2):
                    tmo = mm(psO[:, 0:256],
                             vT[:, (g_ * 9 + j + kt) * 128:(g_ * 9 + j + kt + 1) * 128],
                             ptS[:, kt * 256:(kt + 1) * 256],
                             start=(kt == 0), stop=(kt == 1), inc=(kt == 1))
                prev_tmo = tmo
                aoe = act(lambda s, p_=psO, h_=h, j_=j, tm=tmo: (
                    s.wait_ge(SEM["mm"], tm),
                    s.copy(out=OSB[:, h_ * 2048 + j_ * 256: h_ * 2048 + (j_ + 1) * 256],
                           in_=p_[:, 0:256]))[-1])
                last_o[("s", ctx)] = aoe
        barrier()

        nxt_wo = gd(wqb0, wot[0], "ga")
        st_o = None
        prev_o_mm = 0
        for m in range(KT):
            wb = wqb0 if m % 2 == 0 else wqb1
            t_wl = nxt_wo
            if m + 1 < KT:
                if en == "gpsimd":
                    w("mm", prev_o_mm)
                nxt_wo = gd(wqb1 if m % 2 == 0 else wqb0, wot[m + 1], "ga")
            for c4 in range(4):
                po = PS[c4 % 2]
                if en == "tensor":
                    wv(t_wl)
                tmo = 0
                for k in range(KT):
                    tmo = mm(po[:, 0:512], wb[:, k * 128:(k + 1) * 128],
                             OSB[:, k * 2048 + c4 * 512: k * 2048 + (c4 + 1) * 512],
                             start=(k == 0), stop=(k == KT - 1), inc=(k == KT - 1))
                prev_o_mm = tmo
                aev = act(lambda s, p_=po, m_=m, tm=tmo, so=st_o: (
                    s.wait_ge(SEM["mm"], tm),
                    s.wait_ge(SEM[so[0]], so[1]) if so else None,
                    s.activation(out=ocbD, in_=p_[:, 0:512], func=AF.Identity,
                                 bias=bo_c[:, m_:m_ + 1]))[-1])
                if en == "sync":
                    w("act", aev)
                st_o = ld(och[m * 128:(m + 1) * 128, c4 * 512:(c4 + 1) * 512], ocbD, "q")
                if en == "tensor":
                    w("act", aev)
        barrier()

        prev_last = 0
        for tt in range(16):
            if en == "sync":
                w("dve", prev_last)
            t_y = ld(ytD, och[:, tt * 128:(tt + 1) * 128], "o", transpose=True)
            t_a = ld(xaD, xr2[128 * tt:128 * (tt + 1), :])
            t_b = ld(xbD, xr2[2048 + 128 * tt: 2048 + 128 * (tt + 1), :])
            if en == "vector":
                wv(t_b)
            dve(lambda v: v.tensor_scalar_mul(xqD, xaD, me_c))
            dve(lambda v: v.scalar_tensor_tensor(xqD, xbD, mo_c, xqD, ALU.mult, ALU.add))
            if en in ("scalar", "vector"):
                wv(t_y)
            rstd, mb, _ = ln_stats(ytD, 0, sqjD2)
            ln_apply(tD, ytD, rstd, mb)
            d_r = dve(lambda v: v.tensor_tensor(tD, tD, xqD, ALU.add))
            prev_last = d_r
            if en == "sync":
                w("dve", d_r)
            t_st = ld(xr3[128 * tt:128 * (tt + 1), :], tD, "s")
            if en == "vector":
                wv(t_st)
        barrier()

        XL3T = bfs(0, KT * 512)
        GT = bfs(8192, H1T * 512)
        wgb = bfs(40960, H1)
        wub = bfs(49152, H1)
        wdb0 = wgb
        wdb1 = wub
        gsp = bfs(57344, 512)
        xlnE = bfs(57856, 2048)
        sqjE = bfs(59904, 2048)
        ocbE = bfs(61952, 512)
        ytE = bfs(62464, 1536)
        ytE = xlnE
        xaE = f32s(0, 2048)
        tE = f32s(2048, 2048)
        gfE = f32s(4096, 512)
        silE = f32s(4608, 512)
        abE = f32s(5120, 512)
        thrE = f32s(5632, 128)
        k0E = f32s(5760, 128)
        k1E = f32s(5888, 128)
        m1E = f32s(6016, 128)
        m2E = f32s(6144, 128)
        n1E = f32s(6272, 128)
        n2E = f32s(6400, 128)

        for q4 in range(4):
            r0 = q4 * 512
            prev_tp = 0
            prev_ev = 0
            prev_dapp = 0
            for tt in range(4):
                if en == "sync":
                    w("dve", prev_dapp)
                t_a = ld(xaE, xr3[r0 + 128 * tt: r0 + 128 * (tt + 1), :])
                if en in ("vector", "scalar"):
                    wv(t_a)
                rstd, mb, _ = ln_stats(xaE, 0, sqjE, pre_act=prev_ev)
                if en == "vector":
                    w("mm", prev_tp)
                dapp = ln_apply(xlnE, xaE, rstd, mb)
                prev_dapp = dapp
                prev_tp, prev_ev = emit_t16(xlnE, XL3T[:], 512, tt * 128, dapp)
            barrier()

            prev_tmu = 0
            prev_dsp = 0
            prev_tmt = 0
            for n in range(KT):
                if en == "sync":
                    w("mm", prev_tmu)
                t_wl = []
                for k in range(KT):
                    t_wl.append(ld(wgb[:, k * 512:(k + 1) * 512],
                                   wg_s[k][:, n * 512:(n + 1) * 512], "w1"))
                    t_wl.append(ld(wub[:, k * 512:(k + 1) * 512],
                                   wu_s[k][:, n * 512:(n + 1) * 512], "w2"))
                for mtk in range(4):
                    pg = PS[mtk % 2]
                    pu = PS[2 + mtk % 2]
                    if en == "tensor" and mtk == 0:
                        wv(t_wl[-2])
                        wv(t_wl[-1])
                    tmg = 0
                    for k in range(KT):
                        tmg = mm(pg[:, 0:512],
                                 XL3T[:, k * 512 + mtk * 128: k * 512 + (mtk + 1) * 128],
                                 wgb[:, k * 512:(k + 1) * 512],
                                 start=(k == 0), stop=(k == KT - 1), inc=(k == KT - 1))
                    tmu = 0
                    for k in range(KT):
                        tmu = mm(pu[:, 0:512],
                                 XL3T[:, k * 512 + mtk * 128: k * 512 + (mtk + 1) * 128],
                                 wub[:, k * 512:(k + 1) * 512],
                                 start=(k == 0), stop=(k == KT - 1), inc=(k == KT - 1))
                    prev_tmu = tmu
                    a_s = act(lambda s, p_=pg, tm=tmg, pd=prev_dsp: (
                        s.wait_ge(SEM["mm"], tm),
                        s.wait_ge(SEM["dve"], pd) if pd else None,
                        s.activation(out=silE, in_=p_[:, 0:512], func=AF.Silu))[-1])
                    if en == "vector":
                        w("act", a_s)
                        w("mm", tmu)
                        w("mm", prev_tmt)
                    d_g = dve(lambda v, p_=pu: v.tensor_tensor(
                        gfE, silE, p_[:, 0:512], ALU.mult))
                    a_ab = act(lambda s, dg=d_g: (
                        s.wait_ge(SEM["dve"], dg),
                        s.activation(out=abE, in_=gfE, func=AF.Abs))[-1])
                    if en == "vector":
                        w("act", a_ab)
                    ab3 = abE.rearrange("p (q f) -> p q f", f=4)
                    gf3 = gfE.rearrange("p (q f) -> p q f", f=4)
                    gs3 = gsp.rearrange("p (q f) -> p q f", f=4)
                    dve(lambda v: v.tensor_tensor(m1E, ab3[:, :, 0], ab3[:, :, 1], ALU.max))
                    dve(lambda v: v.tensor_tensor(m2E, ab3[:, :, 2], ab3[:, :, 3], ALU.max))
                    dve(lambda v: v.tensor_tensor(n1E, ab3[:, :, 0], ab3[:, :, 1], ALU.min))
                    dve(lambda v: v.tensor_tensor(n2E, ab3[:, :, 2], ab3[:, :, 3], ALU.min))
                    dve(lambda v: v.tensor_tensor(m1E, m1E, m2E, ALU.min))
                    dve(lambda v: v.tensor_tensor(n1E, n1E, n2E, ALU.max))
                    dve(lambda v: v.tensor_tensor(thrE, m1E, n1E, ALU.max))
                    d_sp = 0
                    for f4 in range(4):
                        kk_ = k0E if f4 % 2 == 0 else k1E
                        dve(lambda v, f_=f4, kb=kk_: v.tensor_tensor(
                            kb, ab3[:, :, f_], thrE, ALU.is_ge))
                        d_sp = dve(lambda v, f_=f4, kb=kk_: v.tensor_tensor(
                            gs3[:, :, f_], gf3[:, :, f_], kb, ALU.mult))
                    prev_dsp = d_sp
                    if en == "tensor":
                        w("dve", d_sp)
                    pbG = PB[mtk % 2]
                    tmt = 0
                    for c_ in range(4):
                        tmt = tp(pbG[:, c_ * 128:(c_ + 1) * 128],
                                 gsp[:, c_ * 128:(c_ + 1) * 128])
                    prev_tmt = tmt
                    aev = act(lambda s, pb_=pbG, n_=n, m_=mtk, tm=tmt: (
                        s.wait_ge(SEM["mm"], tm),
                        s.copy(out=GT[:].rearrange("p (k c) -> p k c", c=512)[
                            :, n_ * 4:(n_ + 1) * 4, m_ * 128:(m_ + 1) * 128],
                            in_=pb_[:, 0:512].rearrange("p (k c) -> p k c", k=4)))[-1])
                    if en == "tensor":
                        w("act", aev)
            barrier()

            nxt_wd = gd(wdb0, wd_s[0], "ga")
            st_o = None
            prev_e_mm = 0
            for m2 in range(KT):
                wb = wdb0 if m2 % 2 == 0 else wdb1
                t_wl = nxt_wd
                if m2 + 1 < KT:
                    if en == "gpsimd":
                        w("mm", prev_e_mm)
                    nxt_wd = gd(wdb1 if m2 % 2 == 0 else wdb0, wd_s[m2 + 1], "ga")
                po = PS[4 + m2 % 2]
                if en == "tensor":
                    wv(t_wl)
                tmo = 0
                for k in range(H1T):
                    tmo = mm(po[:, 0:512], wb[:, k * 128:(k + 1) * 128],
                             GT[:, k * 512:(k + 1) * 512],
                             start=(k == 0), stop=(k == H1T - 1), inc=(k == H1T - 1))
                prev_e_mm = tmo
                aev = act(lambda s, p_=po, tm=tmo, so=st_o: (
                    s.wait_ge(SEM["mm"], tm),
                    s.wait_ge(SEM[so[0]], so[1]) if so else None,
                    s.copy(out=ocbE, in_=p_[:, 0:512]))[-1])
                if en == "sync":
                    w("act", aev)
                st_o = ld(och[m2 * 128:(m2 + 1) * 128, r0:r0 + 512], ocbE, "q")
                if en == "tensor":
                    w("act", aev)
            barrier()

            prev_last = 0
            for tt in range(4):
                if en == "sync":
                    w("dve", prev_last)
                t_y = ld(ytE, och[:, r0 + tt * 128: r0 + (tt + 1) * 128], "o", transpose=True)
                t_a = ld(xaE, xr3[r0 + 128 * tt: r0 + 128 * (tt + 1), :])
                if en in ("scalar", "vector"):
                    wv(t_y)
                rstd, mb, _ = ln_stats(ytE, 0, sqjE)
                ln_apply(tE, ytE, rstd, mb)
                if en == "vector":
                    wv(t_a)
                d_r = dve(lambda v: v.tensor_tensor(tE, tE, xaE, ALU.add))
                prev_last = d_r
                if en == "sync":
                    w("dve", d_r)
                t_st = ld(out_e[r0 + 128 * tt: r0 + 128 * (tt + 1), :], tE, "s")
                if en == "vector":
                    wv(t_st)
            barrier()
        barrier()

    with nc.Block() as block:
        @block.sync
        def _(e):
            stream("sync", e)

        @block.gpsimd
        def _(e):
            stream("gpsimd", e)

        @block.tensor
        def _(e):
            stream("tensor", e)

        @block.scalar
        def _(e):
            stream("scalar", e)

        @block.vector
        def _(e):
            stream("vector", e)

    es.close()
    return nc



def _lhsT_tiles(A):
    K, M = A.shape
    kt, mt = K // 128, M // 128
    return np.ascontiguousarray(
        A.reshape(kt, 128, mt, 128).transpose(2, 1, 0, 3).reshape(mt, 128, kt * 128))


def kernel(**inp):
    x = np.asarray(inp["L0_attn_x"])
    Wa = np.asarray(inp["L0_attn_Wa"]); Wx = np.asarray(inp["L0_attn_Wx"])
    ba = np.asarray(inp["L0_attn_ba"]); bx = np.asarray(inp["L0_attn_bx"])
    Lam = np.asarray(inp["L0_attn_Lam"])
    w1 = np.asarray(inp["L0_ffn_w1"]); w3 = np.asarray(inp["L0_ffn_w3"])
    w2 = np.asarray(inp["L0_ffn_w2"])
    Wq = np.asarray(inp["L1_attn_Wq"]); bq = np.asarray(inp["L1_attn_bq"])
    Wk = np.asarray(inp["L1_attn_Wk"]); bk = np.asarray(inp["L1_attn_bk"])
    Wv = np.asarray(inp["L1_attn_Wv"]); bv = np.asarray(inp["L1_attn_bv"])
    Wo = np.asarray(inp["L1_attn_Wo"]); bo = np.asarray(inp["L1_attn_bo"])
    wg = np.asarray(inp["L1_ffn_w_gate"]); wu = np.asarray(inp["L1_ffn_w_up"])
    wd = np.asarray(inp["L1_ffn_w_down"])

    f32 = np.float32
    wat = _lhsT_tiles(np.ascontiguousarray(Wa.T).astype(f32))
    wxt = _lhsT_tiles(np.ascontiguousarray(Wx.T).astype(f32))
    w1p = np.zeros((D, H0), f32); w1p[:, :H0_TRUE] = w1.T
    w3p = np.zeros((D, H0), f32); w3p[:, :H0_TRUE] = w3.T
    w2p = np.zeros((H0, D), f32); w2p[:H0_TRUE, :] = w2.T
    w1tt = _lhsT_tiles(w1p)
    w3tt = _lhsT_tiles(w3p)
    w2tt = _lhsT_tiles(w2p)
    wqt = _lhsT_tiles(Wq.astype(f32))
    wkt = _lhsT_tiles(Wk.astype(f32))
    wvt = _lhsT_tiles(Wv.astype(f32))
    wot = _lhsT_tiles(Wo.astype(f32))
    wg_t = np.ascontiguousarray(wg.reshape(KT, 128, H1)).astype(f32)
    wu_t = np.ascontiguousarray(wu.reshape(KT, 128, H1)).astype(f32)
    wd_t = np.ascontiguousarray(_lhsT_tiles(wd.astype(f32)).reshape(KT, 128, H1))

    def colvec(v, n):
        return np.ascontiguousarray(np.asarray(v).reshape(n, 128).T).astype(f32)

    ba_t = colvec(ba, KT); bx_t = colvec(bx, KT); lam_t = colvec(Lam, KT)
    bq_t = colvec(bq, KT); bk_t = colvec(bk, 2); bv_t = colvec(bv, 2)
    bo_t = colvec(bo, KT)
    eye = np.eye(128).astype(ml_dtypes.bfloat16)
    cst = np.zeros((128, 2), f32); cst[:, 0] = 1.0; cst[:, 1] = EPS
    onesr = np.ones((1, 128), f32)

    nc = build_nc()
    in_maps = []
    for core in range(8):
        b, p = core // 2, core % 2
        pm = np.zeros((128, 2), f32)
        pm[:, 0] = 1.0 - p
        pm[:, 1] = p
        in_maps.append({
            "x0": np.ascontiguousarray(x[b]).astype(f32),
            "ident": eye,
            "wat": wat, "wxt": wxt, "ba_t": ba_t, "bx_t": bx_t, "lam_t": lam_t,
            "w1t": w1tt, "w3t": w3tt, "w2t": w2tt,
            "wqt": wqt, "wkt": wkt, "wvt": wvt, "wot": wot,
            "bq_t": bq_t, "bk_t": bk_t, "bv_t": bv_t, "bo_t": bo_t,
            "wg_in": wg_t, "wu_in": wu_t, "wd_in": wd_t,
            "pmask": pm, "consts": cst, "onesr_in": onesr,
        })
    res = run_bass_kernel_spmd(nc, in_maps, core_ids=list(range(8)))
    out = np.empty((4, S, D), np.float32)
    for core in range(8):
        b, p = core // 2, core % 2
        out[b, p * 2048:(p + 1) * 2048, :] = res.results[core]["out"]
    return out




# version 03
# speedup vs baseline: 1.3754x; 1.1704x over previous
import numpy as np
import ml_dtypes
import concourse.bass as bass
import concourse.mybir as mybir
from concourse.bass_utils import run_bass_kernel_spmd

F32 = mybir.dt.float32
BF16 = mybir.dt.bfloat16
AF = mybir.ActivationFunctionType
ALU = mybir.AluOpType
AX = mybir.AxisListType.X

D = 2048
S = 4096
KT = 16
H0 = 5504
H0T = 43
H0_TRUE = 5461
H1 = 8192
H1T = 64
NW = 8
WIN = 1152
EPS = 1e-5
ATT_SCALE = float(1.0 / np.sqrt(128.0))


def build_nc(debug=False):
    nc = bass.Bass("TRN2", target_bir_lowering=False)

    x0 = nc.dram_tensor("x0", [S, D], F32, kind="ExternalInput").ap()
    ident = nc.dram_tensor("ident", [128, 128], BF16, kind="ExternalInput").ap()
    wat = nc.dram_tensor("wat", [KT, 128, D], F32, kind="ExternalInput").ap()
    wxt = nc.dram_tensor("wxt", [KT, 128, D], F32, kind="ExternalInput").ap()
    ba_t = nc.dram_tensor("ba_t", [128, KT], F32, kind="ExternalInput").ap()
    bx_t = nc.dram_tensor("bx_t", [128, KT], F32, kind="ExternalInput").ap()
    lam_t = nc.dram_tensor("lam_t", [128, KT], F32, kind="ExternalInput").ap()
    w1t = nc.dram_tensor("w1t", [H0T, 128, D], F32, kind="ExternalInput").ap()
    w3t = nc.dram_tensor("w3t", [H0T, 128, D], F32, kind="ExternalInput").ap()
    w2t = nc.dram_tensor("w2t", [KT, 128, H0], F32, kind="ExternalInput").ap()
    wqt = nc.dram_tensor("wqt", [KT, 128, D], F32, kind="ExternalInput").ap()
    wkt = nc.dram_tensor("wkt", [2, 128, D], F32, kind="ExternalInput").ap()
    wvt = nc.dram_tensor("wvt", [2, 128, D], F32, kind="ExternalInput").ap()
    wot = nc.dram_tensor("wot", [KT, 128, D], F32, kind="ExternalInput").ap()
    bq_t = nc.dram_tensor("bq_t", [128, KT], F32, kind="ExternalInput").ap()
    bk_t = nc.dram_tensor("bk_t", [128, 2], F32, kind="ExternalInput").ap()
    bv_t = nc.dram_tensor("bv_t", [128, 2], F32, kind="ExternalInput").ap()
    bo_t = nc.dram_tensor("bo_t", [128, KT], F32, kind="ExternalInput").ap()
    wg_in = nc.dram_tensor("wg_in", [KT, 128, H1], F32, kind="ExternalInput").ap()
    wu_in = nc.dram_tensor("wu_in", [KT, 128, H1], F32, kind="ExternalInput").ap()
    wd_in = nc.dram_tensor("wd_in", [KT, 128, H1], F32, kind="ExternalInput").ap()
    pmask = nc.dram_tensor("pmask", [128, 2], F32, kind="ExternalInput").ap()
    consts = nc.dram_tensor("consts", [128, 2], F32, kind="ExternalInput").ap()
    onesr_in = nc.dram_tensor("onesr_in", [1, 128], F32, kind="ExternalInput").ap()

    out_e = nc.dram_tensor("out", [2048, D], F32, kind="ExternalOutput").ap()

    ikind = "ExternalOutput" if debug else "Internal"
    LOGA = nc.dram_tensor("LOGA", [KT, 128, S], BF16, kind="Internal").ap()
    UDAT = nc.dram_tensor("UDAT", [KT, 128, S], BF16, kind="Internal").ap()
    YSC = nc.dram_tensor("YSC", [D, S], BF16, kind="Internal").ap()
    xr1 = nc.dram_tensor("xr1", [S, D], F32, kind=ikind).ap()
    xr2 = nc.dram_tensor("xr2", [S, D], F32, kind=ikind).ap()
    xr3 = nc.dram_tensor("xr3", [2048, D], F32, kind=ikind).ap()
    q1s = nc.dram_tensor("q1s", [H0T, 128, D], BF16, kind="Internal").ap()
    q3s = nc.dram_tensor("q3s", [H0T, 128, D], BF16, kind="Internal").ap()
    q2s = nc.dram_tensor("q2s", [KT, 128, H0], BF16, kind="Internal").ap()
    wg_s = nc.dram_tensor("wg_s", [KT, 128, H1], BF16, kind="Internal").ap()
    wu_s = nc.dram_tensor("wu_s", [KT, 128, H1], BF16, kind="Internal").ap()
    wd_s = nc.dram_tensor("wd_s", [KT, 128, H1], BF16, kind="Internal").ap()
    och = nc.dram_tensor("och", [D, 2048], BF16, kind="Internal").ap()

    import contextlib
    es = contextlib.ExitStack()
    sb = lambda name, shape, dt: es.enter_context(nc.sbuf_tensor(name, shape, dt))

    NBF = 64 * 1024
    NF32 = 18 * 1024 + 512
    ABF = sb("ABF", [128, NBF], BF16)
    AF32 = sb("AF32", [128, NF32], F32)
    small = sb("small", [128, 128], F32)
    stat = sb("stat", [128, 64], F32)
    idn = sb("idn", [128, 128], BF16)
    ones_r = sb("ones_r", [1, 128], F32)
    ones_c = sb("ones_c", [128, 1], F32)
    s11 = sb("s11", [1, 8], F32)

    def f32s(c0, n):
        assert c0 + n <= NF32, (c0, n)
        return AF32[:, c0:c0 + n]

    def bfs(c0, n):
        assert c0 + n <= NBF, (c0, n)
        return ABF[:, c0:c0 + n]

    ba_c = small[:, 0:16]
    bx_c = small[:, 16:32]
    mco_c = small[:, 32:48]
    bq_c = small[:, 48:64]
    bk_c = small[:, 64:66]
    bv_c = small[:, 66:68]
    bo_c = small[:, 68:84]
    thr1 = small[:, 84:85]
    thr3 = small[:, 85:86]
    thr2 = small[:, 86:87]
    s1c = small[:, 87:88]
    me_c = small[:, 88:89]
    mo_c = small[:, 89:90]
    lam_c = small[:, 90:106]
    eps_c = small[:, 111:112]
    nthr1 = small[:, 107:108]
    ones_cc = small[:, 110:111]
    nthr3 = small[:, 108:109]
    nthr2 = small[:, 109:110]

    PS = [es.enter_context(nc.psum_tensor(f"PS{i}", [128, 512], F32)) for i in range(6)]
    PB = [es.enter_context(nc.psum_tensor(f"PB{i}", [128, 1024], BF16)) for i in range(2)]

    SEM = {}
    for s_ in ("a", "o", "w1", "w2", "w3", "s", "q", "wc", "ga", "gb",
               "mm", "act", "dve"):
        SEM[s_] = es.enter_context(nc.semaphore(s_))

    DMAF = ("a", "o", "w1", "w2", "w3", "s", "q", "wc", "ga", "gb")

    def stream(en, e):
        cnt = {f: 0 for f in DMAF}
        cnt.update({"mm": 0, "act": 0, "dve": 0})

        def ld(out_ap, in_ap, fam="a", transpose=False):
            prev = cnt[fam]
            cnt[fam] += 16
            if en == "sync":
                if prev:
                    e.wait_ge(SEM[fam], prev)
                e.dma_start(out=out_ap, in_=in_ap, transpose=transpose).then_inc(SEM[fam], 16)
            return (fam, cnt[fam])

        def gd(out_ap, in_ap, fam="ga"):
            prev = cnt[fam]
            cnt[fam] += 16
            if en == "gpsimd":
                if prev:
                    e.wait_ge(SEM[fam], prev)
                e.dma_start(out=out_ap, in_=in_ap).then_inc(SEM[fam], 16)
            return (fam, cnt[fam])

        def w(sem, val):
            if val:
                e.wait_ge(SEM[sem], val)

        def wv(tok):
            if tok:
                e.wait_ge(SEM[tok[0]], tok[1])

        def lwv(tok):
            return tok

        def mm(out, lhsT, rhs, start, stop, inc=True):
            if inc:
                cnt["mm"] += 1
            if en == "tensor":
                i = e.matmul(out, lhsT, rhs, start=start, stop=stop)
                if inc:
                    i.then_inc(SEM["mm"], 1)
            return cnt["mm"]

        def tp(out, in_, inc=True):
            if inc:
                cnt["mm"] += 1
            if en == "tensor":
                i = e.transpose(out, in_, idn[:])
                if inc:
                    i.then_inc(SEM["mm"], 1)
            return cnt["mm"]

        def act(fn, inc=True):
            if inc:
                cnt["act"] += 1
            if en == "scalar":
                if cnt["act"] > 1:
                    e.wait_ge(SEM["act"], cnt["act"] - 1)
                i = fn(e)
                if inc:
                    i.then_inc(SEM["act"], 1)
            return cnt["act"]

        def dve(fn, inc=True):
            if inc:
                cnt["dve"] += 1
            if en == "vector":
                if cnt["dve"] > 1:
                    e.wait_ge(SEM["dve"], cnt["dve"] - 1)
                i = fn(e)
                if inc:
                    i.then_inc(SEM["dve"], 1)
            return cnt["dve"]

        def barrier():
            tgt = dict(cnt)
            for s_ in list(DMAF) + ["mm", "act", "dve"]:
                w(s_, tgt[s_])

        def ln_stats(src, sc, sq_junk, pre_act=0, pre_dve=0):
            ssq = stat[:, sc:sc + 1]
            sm = stat[:, sc + 1:sc + 2]
            mn = stat[:, sc + 2:sc + 3]
            var = stat[:, sc + 3:sc + 4]
            rstd = stat[:, sc + 4:sc + 5]
            mb = stat[:, sc + 5:sc + 6]
            a1 = act(lambda s, pa=pre_act: (
                s.wait_ge(SEM["act"], pa) if pa else None,
                s.activation(out=sq_junk, in_=src, func=AF.Square, accum_out=ssq))[-1])
            d1 = dve(lambda v, pd=pre_dve: (
                v.wait_ge(SEM["dve"], pd) if pd else None,
                v.reduce_sum(sm, src, AX))[-1])
            if en == "vector":
                w("act", a1)
            dve(lambda v: v.tensor_scalar_mul(mn, sm, 1.0 / D))
            dve(lambda v: v.tensor_scalar_mul(var, ssq, 1.0 / D))
            dve(lambda v: v.tensor_tensor(mb, mn, mn, ALU.mult))
            d3 = dve(lambda v: v.tensor_tensor(var, var, mb, ALU.subtract))
            a2 = act(lambda s, dd=d3: (
                s.wait_ge(SEM["dve"], dd),
                s.activation(out=rstd, in_=var, func=AF.Sqrt, bias=eps_c))[-1])
            if en == "vector":
                w("act", a2)
            dve(lambda v: v.reciprocal(rstd, rstd))
            d5 = dve(lambda v: v.tensor_tensor(mb, mn, rstd, ALU.mult))
            return rstd, mb, d5

        def ln_apply(dst, src, rstd, mb):
            return dve(lambda v: v.tensor_scalar(dst, src, rstd, mb,
                                                 ALU.mult, ALU.subtract))

        def emit_t16(xln_ap, dst_tensor_ap, stride, coff, dapp, pbev=None):
            last = 0
            for half in range(2):
                pb = PB[half]
                if en == "tensor":
                    w("dve", dapp)
                    if pbev is not None:
                        w("act", pbev.get(half, 0))
                tmm = 0
                for kk in range(8):
                    k = half * 8 + kk
                    tmm = tp(pb[:, kk * 128:(kk + 1) * 128],
                             xln_ap[:, k * 128:(k + 1) * 128])
                dst = dst_tensor_ap.rearrange("p (k c) -> p k c", c=stride)[
                    :, half * 8:(half + 1) * 8, coff:coff + 128]
                srcp = pb[:].rearrange("p (k c) -> p k c", k=8)
                last = act(lambda s, d=dst, sr=srcp, tm=tmm: (
                    s.wait_ge(SEM["mm"], tm),
                    s.copy(out=d, in_=sr))[-1])
                if pbev is None:
                    if en == "tensor":
                        w("act", last)
                else:
                    pbev[half] = last
            return cnt["mm"], last

        ld(ba_c, ba_t)
        ld(bx_c, bx_t)
        ld(lam_c, lam_t)
        ld(bq_c, bq_t)
        ld(bk_c, bk_t)
        ld(bv_c, bv_t)
        ld(bo_c, bo_t)
        ld(small[:, 88:90], pmask)
        ld(idn[:], ident)
        ld(small[:, 110:112], consts)
        ld(ones_r[:], onesr_in)
        barrier()
        aS1 = act(lambda s: s.activation(out=mco_c, in_=lam_c, func=AF.Exp, scale=-1.0))
        aS2 = act(lambda s, p=aS1: (s.wait_ge(SEM["act"], p),
                  s.activation(out=mco_c, in_=mco_c, func=AF.Ln, bias=1.0))[-1])
        act(lambda s, p=aS2: (s.wait_ge(SEM["act"], p),
            s.mul(mco_c, mco_c, -8.0))[-1])
        barrier()

        XL0T = bfs(0, 32768)
        wam0 = bfs(32768, 2048)
        wam1 = bfs(34816, 2048)
        wxm0 = bfs(36864, 2048)
        wxm1 = bfs(38912, 2048)
        logab0 = bfs(40960, 2048)
        ub0 = bfs(43008, 2048)
        logab1 = bfs(45056, 2048)
        ub1 = bfs(47104, 2048)
        xlnA = bfs(49152, 2048)
        sqjA = bfs(51200, 2048)
        p1junk = bfs(53248, 5504)
        xtA0 = f32s(0, 2048)
        xtA1 = f32s(2048, 2048)
        rA = f32s(4096, 2048)
        iA = f32s(6144, 2048)
        ixA = f32s(8192, 2048)
        sqA = f32s(10240, 2048)
        wp1 = f32s(12288, 5504)
        QACC = f32s(17920, 104)

        pass1_list = ([(w1t[i], D) for i in range(H0T)] +
                      [(w3t[i], D) for i in range(H0T)] +
                      [(w2t[i], H0) for i in range(KT)])
        p1 = {"idx": 0, "aa": 0}

        def emit_pass1_some(nmax):
            done = 0
            while p1["idx"] < len(pass1_list) and done < nmax:
                ap_, cols = pass1_list[p1["idx"]]
                if en == "sync":
                    w("act", p1["aa"])
                t_ld = ld(wp1[:, 0:cols], ap_, "w3")
                p1["aa"] = act(lambda s, c_=cols, tl=t_ld, ii=p1["idx"]: (
                    s.wait_ge(SEM[tl[0]], tl[1]),
                    s.activation(out=p1junk[:, 0:c_], in_=wp1[:, 0:c_], func=AF.Abs,
                                 accum_out=QACC[:, ii:ii + 1]))[-1])
                p1["idx"] += 1
                done += 1

        for p in range(2):
            t0 = 2048 * p
            prev_tp = 0
            prev_ev = 0
            prev_dappA = 0
            pbevA = {}
            for tt in range(16):
                xt = xtA0 if tt % 2 == 0 else xtA1
                if en == "sync":
                    w("dve", prev_dappA)
                t_x = ld(xt, x0[t0 + 128 * tt: t0 + 128 * (tt + 1), :])
                if en in ("scalar", "vector"):
                    wv(t_x)
                rstd, mb, dstat = ln_stats(xt, 0 if tt % 2 == 0 else 8, sqjA,
                                           pre_act=prev_ev)
                if en == "vector":
                    w("mm", prev_tp)
                dapp = ln_apply(xlnA, xt, rstd, mb)
                prev_dappA = dapp
                prev_tp, prev_ev = emit_t16(xlnA, XL0T[:], 2048, tt * 128, dapp, pbevA)
                emit_pass1_some(1)
            barrier()

            nxt_w = (gd(wam0, wat[0], "ga"), gd(wxm0, wxt[0], "gb"))
            st_l = None
            st_u = None
            prev_du = 0
            prev_mlast = 0
            bank_ev = {}
            for m in range(KT):
                wam = wam0 if m % 2 == 0 else wam1
                wxm = wxm0 if m % 2 == 0 else wxm1
                t_w = nxt_w
                if m + 1 < KT:
                    if en == "gpsimd":
                        w("mm", prev_mlast)
                    nxt_w = (gd(wam1 if m % 2 == 0 else wam0, wat[m + 1], "ga"),
                             gd(wxm1 if m % 2 == 0 else wxm0, wxt[m + 1], "gb"))
                lb = logab0 if m % 2 == 0 else logab1
                uo = ub0 if m % 2 == 0 else ub1
                a_last = 0
                for c in range(4):
                    psr = PS[c % 2]
                    psi = PS[2 + c % 2]
                    if en == "tensor" and c == 0:
                        wv(t_w[0])
                        wv(t_w[1])
                    if en == "tensor":
                        w("act", bank_ev.get(c % 2, 0))
                        w("act", bank_ev.get(2 + c % 2, 0))
                    tmr = 0
                    for k in range(KT):
                        tmr = mm(psr[:, 0:512], wam[:, k * 128:(k + 1) * 128],
                                 XL0T[:, k * 2048 + 512 * c: k * 2048 + 512 * (c + 1)],
                                 start=(k == 0), stop=(k == KT - 1), inc=(k == KT - 1))
                    tmi = 0
                    for k in range(KT):
                        tmi = mm(psi[:, 0:512], wxm[:, k * 128:(k + 1) * 128],
                                 XL0T[:, k * 2048 + 512 * c: k * 2048 + 512 * (c + 1)],
                                 start=(k == 0), stop=(k == KT - 1), inc=(k == KT - 1))
                    ar = act(lambda s, ps=psr, cc=c, mm_=m, tm=tmr, pd=prev_du: (
                        s.wait_ge(SEM["mm"], tm),
                        s.wait_ge(SEM["dve"], pd) if (pd and cc == 0) else None,
                        s.activation(out=rA[:, 512 * cc:512 * (cc + 1)], in_=ps[:, 0:512],
                                     func=AF.Sigmoid, bias=ba_c[:, mm_:mm_ + 1]))[-1])
                    ai = act(lambda s, ps=psi, cc=c, mm_=m, tm=tmi: (
                        s.wait_ge(SEM["mm"], tm),
                        s.activation(out=iA[:, 512 * cc:512 * (cc + 1)], in_=ps[:, 0:512],
                                     func=AF.Sigmoid, bias=bx_c[:, mm_:mm_ + 1]))[-1])
                    a_last = ai
                    bank_ev[c % 2] = ar
                    bank_ev[2 + c % 2] = ai
                    if c == 3:
                        prev_mlast = tmi
                if en == "vector":
                    w("act", a_last)
                    wv(st_l)
                d_lg = dve(lambda v, lb_=lb, mm_=m: v.tensor_scalar_mul(
                    lb_, rA, mco_c[:, mm_:mm_ + 1]))
                d_ix = dve(lambda v, mm_=m: v.tensor_tensor(
                    ixA, iA, XL0T[:, mm_ * 2048:(mm_ + 1) * 2048], ALU.mult))
                a_a2 = act(lambda s, lb_=lb, dl=d_lg: (
                    s.wait_ge(SEM["dve"], dl),
                    s.activation(out=sqA, in_=lb_, func=AF.Exp, scale=2.0))[-1])
                a_sq = act(lambda s, p=a_a2: (s.wait_ge(SEM["act"], p),
                    s.activation(out=sqA, in_=sqA, func=AF.Sqrt,
                                 scale=-1.0, bias=1.0))[-1])
                if en == "vector":
                    w("act", a_sq)
                    wv(st_u)
                d_u = dve(lambda v, uo_=uo: v.tensor_tensor(uo_, ixA, sqA, ALU.mult))
                prev_du = d_u
                if en == "sync":
                    w("dve", d_u)
                st_l = ld(LOGA[m][:, t0:t0 + 2048], lb, "s")
                st_u = ld(UDAT[m][:, t0:t0 + 2048], uo, "s")
                emit_pass1_some(2)
            barrier()
        emit_pass1_some(10000)
        barrier()

        ranges = [(0, H0T, H0_TRUE * D, thr1, True),
                  (H0T, 2 * H0T, H0_TRUE * D, thr3, False),
                  (2 * H0T, 2 * H0T + KT, D * H0_TRUE, thr2, False)]
        for idx, (c0, c1, count, thr_col, want_s1) in enumerate(ranges):
            dr = dve(lambda v, a=c0, b=c1, ii=idx: v.reduce_sum(
                stat[:, 32 + ii:33 + ii], QACC[:, a:b], AX))
            if en == "tensor":
                w("dve", dr)
            tm = mm(PS[4][0:1, idx:idx + 1], stat[:, 32 + idx:33 + idx], ones_cc,
                    start=True, stop=True)
            aa = act(lambda s, ii=idx, cc=count, tm_=tm: (
                s.wait_ge(SEM["mm"], tm_),
                s.activation(out=s11[0:1, ii:ii + 1], in_=PS[4][0:1, ii:ii + 1],
                             func=AF.Copy, scale=0.5 / cc))[-1])
            if en == "tensor":
                w("act", aa)
            tb = mm(PS[5][:, idx:idx + 1], ones_r[:], s11[0:1, idx:idx + 1],
                    start=True, stop=True)
            a_thr = act(lambda s, th=thr_col, tb_=tb: (
                s.wait_ge(SEM["mm"], tb_),
                s.copy(out=th, in_=PS[5][:, idx:idx + 1]))[-1])
            if want_s1:
                act(lambda s: s.mul(s1c, thr1, 2.0))
            nth = (nthr1, nthr3, nthr2)[idx]
            dve(lambda v, th=thr_col, nt=nth, p=a_thr: (
                v.wait_ge(SEM["act"], p),
                v.tensor_scalar_mul(nt, th, -1.0))[-1])
            barrier()

        lgB0 = bfs(0, 4096)
        lgB1 = bfs(4096, 4096)
        uB0 = bfs(8192, 4096)
        uB1 = bfs(12288, 4096)
        yB0 = bfs(16384, 4096)
        yB1 = bfs(20480, 4096)
        sgB = bfs(24576, 5504)
        mkB = bfs(30080, 5504)
        qB = bfs(35584, 5504)
        yttB = bfs(41088, 2048)
        sqjB = bfs(43136, 2048)
        aB = f32s(0, 4096)
        uB32 = f32s(4096, 4096)
        xtB = f32s(8192, 2048)
        tB = f32s(10240, 2048)

        pass2_list = ([(w1t[i], q1s[i], D, thr1, nthr1) for i in range(H0T)] +
                      [(w3t[i], q3s[i], D, thr3, nthr3) for i in range(H0T)] +
                      [(w2t[i], q2s[i], H0, thr2, nthr2) for i in range(KT)])
        p2 = {"idx": 0, "st": None, "dmk": 0, "dq": 0}

        def emit_pass2_some(nmax):
            done = 0
            while p2["idx"] < len(pass2_list) and done < nmax:
                src, dst, cols, thr_col, nth_col = pass2_list[p2["idx"]]
                if en == "sync":
                    w("dve", p2["dmk"])
                t_ld = ld(wp1[:, 0:cols], src, "w3")
                dpos = dve(lambda v, c_=cols, th=thr_col, tl=t_ld: (
                    v.wait_ge(SEM[tl[0]], tl[1]),
                    v.tensor_scalar(mkB[:, 0:c_], wp1[:, 0:c_], th, None,
                                    ALU.is_gt))[-1])
                dmk = dve(lambda v, c_=cols, nt=nth_col, pl=p2["st"]: (
                    v.wait_ge(SEM[pl[0]], pl[1]) if pl else None,
                    v.tensor_scalar(sgB[:, 0:c_], wp1[:, 0:c_], nt, None,
                                    ALU.is_lt))[-1])
                dq = dve(lambda v, c_=cols: v.tensor_tensor(
                    qB[:, 0:c_], mkB[:, 0:c_], sgB[:, 0:c_], ALU.subtract))
                if en == "sync":
                    w("dve", dq)
                p2["st"] = ld(dst, qB[:, 0:cols], "q")
                p2["dmk"] = dmk
                p2["dq"] = dq
                p2["idx"] += 1
                done += 1

        st_y = None
        prev_dy = 0
        prev_au = 0
        nxt_in = (ld(lgB0, LOGA[0][:, :], "w1"), ld(uB0, UDAT[0][:, :], "w2"))
        for m in range(KT):
            lg = lgB0 if m % 2 == 0 else lgB1
            uu = uB0 if m % 2 == 0 else uB1
            yy = yB0 if m % 2 == 0 else yB1
            t_in = nxt_in
            if m + 1 < KT:
                if en == "sync":
                    w("act", prev_au)
                nxt_in = (ld(lgB1 if m % 2 == 0 else lgB0, LOGA[m + 1][:, :], "w1"),
                          ld(uB1 if m % 2 == 0 else uB0, UDAT[m + 1][:, :], "w2"))
            a_a = act(lambda s, lg_=lg, ti=t_in, pd=prev_dy: (
                s.wait_ge(SEM[ti[0][0]], ti[0][1]),
                s.wait_ge(SEM[ti[1][0]], ti[1][1]),
                s.wait_ge(SEM["dve"], pd) if pd else None,
                s.activation(out=aB, in_=lg_, func=AF.Exp))[-1])
            a_u = act(lambda s, uu_=uu: s.copy(out=uB32, in_=uu_))
            prev_au = a_u
            if en == "vector":
                w("act", a_u)
                wv(st_y)
            dve(lambda v, yy_=yy: v.tensor_tensor_scan(
                out=yy_[:, 0:2048], data0=aB[:, 0:2048], data1=uB32[:, 0:2048],
                initial=0.0, op0=ALU.mult, op1=ALU.add))
            dve(lambda v, yy_=yy: v.tensor_copy(stat[:, 30:31], yy_[:, 2047:2048]))
            d_y = dve(lambda v, yy_=yy: v.tensor_tensor_scan(
                out=yy_[:, 2048:4096], data0=aB[:, 2048:4096], data1=uB32[:, 2048:4096],
                initial=stat[:, 30:31], op0=ALU.mult, op1=ALU.add))
            prev_dy = d_y
            if en == "sync":
                w("dve", d_y)
            st_y = ld(YSC[m * 128:(m + 1) * 128, :], yy, "s")
            emit_pass2_some(4)
        emit_pass2_some(10000)
        barrier()

        prev_dr = 0
        prev_st = None
        for tt in range(32):
            if en == "sync":
                w("dve", prev_dr)
            t_y = ld(yttB, YSC[:, tt * 128:(tt + 1) * 128], "o", transpose=True)
            t_x = ld(xtB, x0[tt * 128:(tt + 1) * 128, :])
            if en in ("scalar", "vector"):
                wv(t_y)
            if en == "vector":
                wv(prev_st)
            rstd, mb, _ = ln_stats(yttB, 0, sqjB)
            ln_apply(tB, yttB, rstd, mb)
            if en == "vector":
                wv(t_x)
            d_r = dve(lambda v: v.tensor_tensor(tB, tB, xtB, ALU.add))
            prev_dr = d_r
            if en == "sync":
                w("dve", d_r)
            prev_st = ld(xr1[tt * 128:(tt + 1) * 128, :], tB, "s")
        barrier()

        XL1T = bfs(0, 8192)
        GBLK = bfs(8192, 43 * 512)
        q1b0 = bfs(30208, 2048)
        q1b1 = bfs(32256, 2048)
        q3b0 = bfs(34304, 2048)
        q3b1 = bfs(36352, 2048)
        q2b0 = bfs(38400, 5504)
        q2b1 = bfs(43904, 5504)
        xlnC = bfs(49408, 2048)
        sqjC = bfs(51456, 2048)
        ocbC = bfs(53504, 512)
        ytC = bfs(54016, 2048)
        xaC = f32s(0, 2048)
        xbC = f32s(2048, 2048)
        xqC = f32s(4096, 2048)
        silC = f32s(6144, 512)
        tC = f32s(6656, 2048)

        cb0 = bfs(56064, 4096)
        cb1 = bfs(60160, 4096)
        wcast_units = []
        for pair in ([(wg_in[k], wg_s[k]) for k in range(KT)] +
                     [(wu_in[k], wu_s[k]) for k in range(KT)] +
                     [(wd_in[k], wd_s[k]) for k in range(KT)]):
            srcw, dstw = pair
            wcast_units.append((srcw[:, 0:4096], dstw[:, 0:4096]))
            wcast_units.append((srcw[:, 4096:8192], dstw[:, 4096:8192]))
        wc = {"idx": 0, "st": [None, None]}

        def emit_wcast_some(nmax):
            done = 0
            while wc["idx"] < len(wcast_units) and done < nmax:
                srcp, dstp = wcast_units[wc["idx"]]
                buf = cb0 if wc["idx"] % 2 == 0 else cb1
                if en == "gpsimd":
                    wv(wc["st"][wc["idx"] % 2])
                t_c = gd(buf, srcp, "gb")
                if en == "sync":
                    wv(t_c)
                wc["st"][wc["idx"] % 2] = ld(dstp, buf, "wc")
                wc["idx"] += 1
                done += 1

        blocks = [("P", 1024, None), ("P", 1536, None), ("P", 2048, None),
                  ("P", 2560, None), ("Q", 0, 3072), ("Q", 512, 3584)]

        for btype, rA_, rB_ in blocks:
            prev_tp = 0
            prev_ev = 0
            prev_dapp = 0
            pbevC = {}
            for tt in range(4):
                if en == "sync":
                    w("dve", prev_dapp)
                t_a = ld(xaC, xr1[rA_ + 128 * tt: rA_ + 128 * (tt + 1), :])
                src = xaC
                dpre = 0
                if btype == "Q":
                    t_b = ld(xbC, xr1[rB_ + 128 * tt: rB_ + 128 * (tt + 1), :])
                    if en == "vector":
                        wv(t_b)
                    dve(lambda v: v.tensor_scalar_mul(xqC, xaC, me_c))
                    dpre = dve(lambda v: v.scalar_tensor_tensor(
                        xqC, xbC, mo_c, xqC, ALU.mult, ALU.add))
                    src = xqC
                else:
                    if en == "vector":
                        wv(t_a)
                if en == "scalar":
                    if btype == "Q":
                        w("dve", dpre)
                    else:
                        wv(t_a)
                rstd, mb, _ = ln_stats(src, 0, sqjC, pre_act=prev_ev)
                if en == "vector":
                    w("mm", prev_tp)
                dapp = ln_apply(xlnC, src, rstd, mb)
                prev_dapp = dapp
                prev_tp, prev_ev = emit_t16(xlnC, XL1T[:], 512, tt * 128, dapp, pbevC)
            barrier()

            nxt_q = (ld(q1b0, q1s[0], "w1"), ld(q3b0, q3s[0], "w2"))
            prev_dg = 0
            prev_h_mm = 0
            h1_rd = {}
            h3_rd = {}
            for mh in range(H0T):
                qb1 = q1b0 if mh % 2 == 0 else q1b1
                qb3 = q3b0 if mh % 2 == 0 else q3b1
                t_ql = nxt_q
                if mh + 1 < H0T:
                    if en == "sync":
                        w("mm", prev_h_mm)
                    nxt_q = (ld(q1b1 if mh % 2 == 0 else q1b0, q1s[mh + 1], "w1"),
                             ld(q3b1 if mh % 2 == 0 else q3b0, q3s[mh + 1], "w2"))
                emit_wcast_some(1)
                ph1 = PS[mh % 2]
                ph3 = PS[2 + mh % 2]
                if en == "tensor":
                    wv(t_ql[0])
                    wv(t_ql[1])
                    w("act", h1_rd.get(mh % 2, 0))
                    w("dve", h3_rd.get(mh % 2, 0))
                tm1 = 0
                for k in range(KT):
                    tm1 = mm(ph1[:, 0:512], qb1[:, k * 128:(k + 1) * 128],
                             XL1T[:, k * 512:(k + 1) * 512],
                             start=(k == 0), stop=(k == KT - 1), inc=(k == KT - 1))
                tm3 = 0
                for k in range(KT):
                    tm3 = mm(ph3[:, 0:512], qb3[:, k * 128:(k + 1) * 128],
                             XL1T[:, k * 512:(k + 1) * 512],
                             start=(k == 0), stop=(k == KT - 1), inc=(k == KT - 1))
                a_s = act(lambda s, p_=ph1, tm=tm1, pd=prev_dg: (
                    s.wait_ge(SEM["mm"], tm),
                    s.wait_ge(SEM["dve"], pd) if pd else None,
                    s.activation(out=silC, in_=p_[:, 0:512], func=AF.Silu,
                                 scale=s1c))[-1])
                if en == "vector":
                    w("act", a_s)
                    w("mm", tm3)
                d_g = dve(lambda v, p_=ph3, mh_=mh: v.tensor_tensor(
                    GBLK[:, mh_ * 512:(mh_ + 1) * 512], silC, p_[:, 0:512], ALU.mult))
                prev_dg = d_g
                prev_h_mm = tm3
                h1_rd[mh % 2] = a_s
                h3_rd[mh % 2] = d_g

            nxt_q2 = ld(q2b0, q2s[0], "w3")
            st_o = None
            prev_d_mm = 0
            dn_ev = {}
            if en == "tensor":
                w("dve", prev_dg)
            for m2 in range(KT):
                qb2 = q2b0 if m2 % 2 == 0 else q2b1
                t_ql = nxt_q2
                if m2 + 1 < KT:
                    if en == "sync":
                        w("mm", prev_d_mm)
                    nxt_q2 = ld(q2b1 if m2 % 2 == 0 else q2b0, q2s[m2 + 1], "w3")
                emit_wcast_some(1)
                po = PS[4 + m2 % 2]
                if en == "tensor":
                    wv(t_ql)
                    w("act", dn_ev.get(m2 % 2, 0))
                tmo = 0
                for k in range(H0T):
                    tmo = mm(po[:, 0:512], qb2[:, k * 128:(k + 1) * 128],
                             GBLK[:, k * 512:(k + 1) * 512],
                             start=(k == 0), stop=(k == H0T - 1), inc=(k == H0T - 1))
                prev_d_mm = tmo
                aev = act(lambda s, p_=po, tm=tmo, so=st_o: (
                    s.wait_ge(SEM["mm"], tm),
                    s.wait_ge(SEM[so[0]], so[1]) if so else None,
                    s.copy(out=ocbC, in_=p_[:, 0:512]))[-1])
                dn_ev[m2 % 2] = aev
                if en == "sync":
                    w("act", aev)
                st_o = ld(och[m2 * 128:(m2 + 1) * 128, 0:512], ocbC, "q")
            barrier()

            prev_last = 0
            for tt in range(4):
                if en == "sync":
                    w("dve", prev_last)
                t_y = ld(ytC, och[:, tt * 128:(tt + 1) * 128], "o", transpose=True)
                t_a = ld(xaC, xr1[rA_ + 128 * tt: rA_ + 128 * (tt + 1), :])
                src = xaC
                if btype == "Q":
                    t_b = ld(xbC, xr1[rB_ + 128 * tt: rB_ + 128 * (tt + 1), :])
                    if en == "vector":
                        wv(t_b)
                    dve(lambda v: v.tensor_scalar_mul(xqC, xaC, me_c))
                    dve(lambda v: v.scalar_tensor_tensor(
                        xqC, xbC, mo_c, xqC, ALU.mult, ALU.add))
                    src = xqC
                else:
                    if en == "vector":
                        wv(t_a)
                if en in ("scalar", "vector"):
                    wv(t_y)
                rstd, mb, _ = ln_stats(ytC, 0, sqjC)
                ln_apply(tC, ytC, rstd, mb)
                d_r = dve(lambda v, s_=src: v.tensor_tensor(tC, tC, s_, ALU.add))
                prev_last = d_r
                if en == "sync":
                    w("dve", d_r)
                t_st = ld(xr2[rA_ + 128 * tt: rA_ + 128 * (tt + 1), :], tC, "s")
                if btype == "Q":
                    t_st = ld(xr2[rB_ + 128 * tt: rB_ + 128 * (tt + 1), :], tC, "s")
                if en == "vector":
                    wv(t_st)
            barrier()

        emit_wcast_some(100000)
        barrier()

        qS = bfs(0, KT * WIN)
        kS = bfs(18432, 2 * WIN)
        vS = bfs(20736, 2 * WIN)
        vT = bfs(23040, 2 * WIN)
        XWT = bfs(25344, KT * WIN)
        OSB = bfs(25344, KT * 2048)
        xlnD = bfs(43776, 2048)
        sqjD = bfs(45824, 2048)
        pnS = bfs(58112, 512)
        ptS = bfs(58624, 512)
        wqb0 = bfs(59136, 2048)
        wqb1 = bfs(61184, 2048)
        ytD = bfs(0, 2048)
        ocbD = bfs(2048, 512)
        sqjD2 = bfs(2560, 2048)
        xaD = f32s(0, 2048)
        xbD = f32s(2048, 2048)
        xqD = f32s(4096, 2048)
        tD = f32s(6144, 2048)

        prev_tp = 0
        prev_ev = 0
        prev_dapp = 0
        pbevD = {}
        for tt in range(9):
            if en == "sync":
                w("dve", prev_dapp)
            t_a = ld(xaD, xr2[128 * tt:128 * (tt + 1), :])
            t_b = ld(xbD, xr2[1024 + 128 * tt: 1024 + 128 * (tt + 1), :])
            if en == "vector":
                wv(t_b)
            dve(lambda v: v.tensor_scalar_mul(xqD, xaD, me_c))
            dpre = dve(lambda v: v.scalar_tensor_tensor(
                xqD, xbD, mo_c, xqD, ALU.mult, ALU.add))
            if en == "scalar":
                w("dve", dpre)
            rstd, mb, _ = ln_stats(xqD, 0, sqjD, pre_act=prev_ev)
            if en == "vector":
                w("mm", prev_tp)
            dapp = ln_apply(xlnD, xqD, rstd, mb)
            prev_dapp = dapp
            prev_tp, prev_ev = emit_t16(xlnD, XWT[:], WIN, tt * 128, dapp, pbevD)
        barrier()

        qkv_list = ([(wqt[m_], bq_c[:, m_:m_ + 1], qS, m_) for m_ in range(KT)] +
                    [(wkt[m_], bk_c[:, m_:m_ + 1], kS, m_) for m_ in range(2)] +
                    [(wvt[m_], bv_c[:, m_:m_ + 1], vS, m_) for m_ in range(2)])
        nxt_wq = gd(wqb0, qkv_list[0][0], "ga")
        qkv_ev = {}
        prev_q_mm = 0
        for i_, (wsrc, bias, dstbuf, m_) in enumerate(qkv_list):
            wb = wqb0 if i_ % 2 == 0 else wqb1
            t_wl = nxt_wq
            if i_ + 1 < len(qkv_list):
                if en == "gpsimd":
                    w("mm", prev_q_mm)
                nxt_wq = gd(wqb1 if i_ % 2 == 0 else wqb0, qkv_list[i_ + 1][0], "ga")
            if en == "tensor":
                wv(t_wl)
            for ci, (c0, cw) in enumerate(((0, 512), (512, 512), (1024, 128))):
                pq = PS[ci % 2]
                if en == "tensor":
                    w("act", qkv_ev.get(ci % 2, 0))
                tmq = 0
                for k in range(KT):
                    tmq = mm(pq[:, 0:cw], wb[:, k * 128:(k + 1) * 128],
                             XWT[:, k * WIN + c0: k * WIN + c0 + cw],
                             start=(k == 0), stop=(k == KT - 1), inc=(k == KT - 1))
                prev_q_mm = tmq
                aev = act(lambda s, p_=pq, d=dstbuf, mm_=m_, a=c0, ww=cw, b_=bias, tm=tmq: (
                    s.wait_ge(SEM["mm"], tm),
                    s.activation(out=d[:, mm_ * WIN + a: mm_ * WIN + a + ww],
                                 in_=p_[:, 0:ww], func=AF.Identity, bias=b_))[-1])
                qkv_ev[ci % 2] = aev
        for g_ in range(2):
            for tt in range(9):
                pb = PB[tt % 2]
                tmm = tp(pb[:, 0:128],
                         vS[:, g_ * WIN + tt * 128: g_ * WIN + (tt + 1) * 128])
                aev = act(lambda s, g0=g_, t0_=tt, tm=tmm: (
                    s.wait_ge(SEM["mm"], tm),
                    s.copy(out=vT[:, (g0 * 9 + t0_) * 128:(g0 * 9 + t0_ + 1) * 128],
                           in_=PB[t0_ % 2][:, 0:128]))[-1])
                if en == "tensor":
                    w("act", aev)
        barrier()

        last_o = {}
        last_p = {}
        prev_tmt = 0
        prev_tmo = 0
        for j in range(NW):
            for h in range(KT):
                g_ = h % 2
                ctx = (j * KT + h) % 2
                psS = PS[ctx]
                psO = PS[2 + ctx]
                pbP = PB[ctx]
                if en == "tensor" and ("s", ctx) in last_o:
                    w("act", last_o[("s", ctx)])
                tms = 0
                for qt in range(2):
                    tms = mm(psS[:, qt * 256:(qt + 1) * 256],
                             qS[:, h * WIN + j * 128 + qt * 128:
                                h * WIN + j * 128 + (qt + 1) * 128],
                             kS[:, g_ * WIN + j * 128: g_ * WIN + j * 128 + 256],
                             start=True, stop=True)
                dn_last = 0
                for qt in range(2):
                    s0 = 16 + 8 * qt
                    mx = stat[:, s0:s0 + 1]
                    mxs = stat[:, s0 + 1:s0 + 2]
                    rs = stat[:, s0 + 2:s0 + 3]
                    rr = stat[:, s0 + 3:s0 + 4]
                    dm = dve(lambda v, p_=psS, q_=qt, tm=tms, m_=mx: (
                        v.wait_ge(SEM["mm"], tm),
                        v.reduce_max(m_, p_[:, q_ * 256:(q_ + 1) * 256], AX))[-1])
                    dmx = dve(lambda v, m_=mx, ms=mxs: v.tensor_scalar_mul(
                        ms, m_, -ATT_SCALE))
                    ae = act(lambda s, p_=psS, q_=qt, dm_=dmx, ms=mxs, rs_=rs,
                             pt_=prev_tmt: (
                        s.wait_ge(SEM["dve"], dm_),
                        s.wait_ge(SEM["mm"], pt_) if (pt_ and q_ == 0) else None,
                        s.activation(out=pnS[:, q_ * 256:(q_ + 1) * 256],
                                     in_=p_[:, q_ * 256:(q_ + 1) * 256],
                                     func=AF.Exp, scale=ATT_SCALE, bias=ms,
                                     accum_out=rs_))[-1])
                    dr = dve(lambda v, a_=ae, rs_=rs, rr_=rr: (
                        v.wait_ge(SEM["act"], a_),
                        v.reciprocal(rr_, rs_))[-1])
                    dn_last = dve(lambda v, q_=qt, rr_=rr: v.tensor_scalar_mul(
                        pnS[:, q_ * 256:(q_ + 1) * 256],
                        pnS[:, q_ * 256:(q_ + 1) * 256], rr_))
                if en == "tensor":
                    w("dve", dn_last)
                    if ("p", ctx) in last_p:
                        w("act", last_p[("p", ctx)])
                tmt = 0
                for qt in range(2):
                    for kt in range(2):
                        tmt = tp(pbP[:, kt * 256 + qt * 128: kt * 256 + (qt + 1) * 128],
                                 pnS[:, qt * 256 + kt * 128: qt * 256 + (kt + 1) * 128])
                prev_tmt = tmt
                apt = act(lambda s, pb_=pbP, tm=tmt, po_=prev_tmo: (
                    s.wait_ge(SEM["mm"], tm),
                    s.wait_ge(SEM["mm"], po_) if po_ else None,
                    s.copy(out=ptS, in_=pb_[:, 0:512]))[-1])
                last_p[("p", ctx)] = apt
                if en == "tensor":
                    w("act", apt)
                tmo = 0
                for kt in range(2):
                    tmo = mm(psO[:, 0:256],
                             vT[:, (g_ * 9 + j + kt) * 128:(g_ * 9 + j + kt + 1) * 128],
                             ptS[:, kt * 256:(kt + 1) * 256],
                             start=(kt == 0), stop=(kt == 1), inc=(kt == 1))
                prev_tmo = tmo
                aoe = act(lambda s, p_=psO, h_=h, j_=j, tm=tmo: (
                    s.wait_ge(SEM["mm"], tm),
                    s.copy(out=OSB[:, h_ * 2048 + j_ * 256: h_ * 2048 + (j_ + 1) * 256],
                           in_=p_[:, 0:256]))[-1])
                last_o[("s", ctx)] = aoe
        barrier()

        nxt_wo = gd(wqb0, wot[0], "ga")
        st_o = None
        op_ev = {}
        prev_o_mm = 0
        for m in range(KT):
            wb = wqb0 if m % 2 == 0 else wqb1
            t_wl = nxt_wo
            if m + 1 < KT:
                if en == "gpsimd":
                    w("mm", prev_o_mm)
                nxt_wo = gd(wqb1 if m % 2 == 0 else wqb0, wot[m + 1], "ga")
            for c4 in range(4):
                po = PS[c4 % 2]
                if en == "tensor":
                    wv(t_wl)
                    w("act", op_ev.get(c4 % 2, 0))
                tmo = 0
                for k in range(KT):
                    tmo = mm(po[:, 0:512], wb[:, k * 128:(k + 1) * 128],
                             OSB[:, k * 2048 + c4 * 512: k * 2048 + (c4 + 1) * 512],
                             start=(k == 0), stop=(k == KT - 1), inc=(k == KT - 1))
                prev_o_mm = tmo
                aev = act(lambda s, p_=po, m_=m, tm=tmo, so=st_o: (
                    s.wait_ge(SEM["mm"], tm),
                    s.wait_ge(SEM[so[0]], so[1]) if so else None,
                    s.activation(out=ocbD, in_=p_[:, 0:512], func=AF.Identity,
                                 bias=bo_c[:, m_:m_ + 1]))[-1])
                op_ev[c4 % 2] = aev
                if en == "sync":
                    w("act", aev)
                st_o = ld(och[m * 128:(m + 1) * 128, c4 * 512:(c4 + 1) * 512], ocbD, "q")
        barrier()

        prev_last = 0
        for tt in range(16):
            if en == "sync":
                w("dve", prev_last)
            t_y = ld(ytD, och[:, tt * 128:(tt + 1) * 128], "o", transpose=True)
            t_a = ld(xaD, xr2[128 * tt:128 * (tt + 1), :])
            t_b = ld(xbD, xr2[2048 + 128 * tt: 2048 + 128 * (tt + 1), :])
            if en == "vector":
                wv(t_b)
            dve(lambda v: v.tensor_scalar_mul(xqD, xaD, me_c))
            dve(lambda v: v.scalar_tensor_tensor(xqD, xbD, mo_c, xqD, ALU.mult, ALU.add))
            if en in ("scalar", "vector"):
                wv(t_y)
            rstd, mb, _ = ln_stats(ytD, 0, sqjD2)
            ln_apply(tD, ytD, rstd, mb)
            d_r = dve(lambda v: v.tensor_tensor(tD, tD, xqD, ALU.add))
            prev_last = d_r
            if en == "sync":
                w("dve", d_r)
            t_st = ld(xr3[128 * tt:128 * (tt + 1), :], tD, "s")
            if en == "vector":
                wv(t_st)
        barrier()

        XL3T = bfs(0, KT * 512)
        GT = bfs(8192, H1T * 512)
        wgb = bfs(40960, H1)
        wub = bfs(49152, H1)
        wdb0 = wgb
        wdb1 = wub
        gsp0 = bfs(57344, 512)
        gsp1 = bfs(62464, 512)
        xlnE = bfs(57856, 2048)
        sqjE = bfs(59904, 2048)
        ocbE = bfs(61952, 512)
        ytE = bfs(62464, 1536)
        ytE = xlnE
        xaE = f32s(0, 2048)
        tE = f32s(2048, 2048)
        gfE = f32s(4096, 512)
        silE = f32s(4608, 512)
        abE = f32s(5120, 512)
        thrE = f32s(5632, 128)
        k0E = f32s(5760, 128)
        k1E = f32s(5888, 128)
        m1E = f32s(6016, 128)
        m2E = f32s(6144, 128)
        n1E = f32s(6272, 128)
        n2E = f32s(6400, 128)

        for q4 in range(4):
            r0 = q4 * 512
            prev_tp = 0
            prev_ev = 0
            prev_dapp = 0
            pbevE = {}
            for tt in range(4):
                if en == "sync":
                    w("dve", prev_dapp)
                t_a = ld(xaE, xr3[r0 + 128 * tt: r0 + 128 * (tt + 1), :])
                if en in ("vector", "scalar"):
                    wv(t_a)
                rstd, mb, _ = ln_stats(xaE, 0, sqjE, pre_act=prev_ev)
                if en == "vector":
                    w("mm", prev_tp)
                dapp = ln_apply(xlnE, xaE, rstd, mb)
                prev_dapp = dapp
                prev_tp, prev_ev = emit_t16(xlnE, XL3T[:], 512, tt * 128, dapp, pbevE)
            barrier()

            prev_tmu = 0
            prev_dsp = 0
            ga_ev = {}
            gu_rd = {}
            gsp_rd = {}
            gt_ev = {}
            pending = None

            def flush_pending():
                nonlocal pending
                if pending is None:
                    return
                (pn, pm, pds) = pending
                par = (pn * 4 + pm) % 2
                gspb = gsp0 if par == 0 else gsp1
                pbG = PB[par]
                if en == "tensor":
                    w("dve", pds)
                    w("act", gt_ev.get(par, 0))
                tmt = 0
                for c_ in range(4):
                    tmt = tp(pbG[:, c_ * 128:(c_ + 1) * 128],
                             gspb[:, c_ * 128:(c_ + 1) * 128])
                gsp_rd[par] = tmt
                aev = act(lambda s, pb_=pbG, n_=pn, m_=pm, tm=tmt: (
                    s.wait_ge(SEM["mm"], tm),
                    s.copy(out=GT[:].rearrange("p (k c) -> p k c", c=512)[
                        :, n_ * 4:(n_ + 1) * 4, m_ * 128:(m_ + 1) * 128],
                        in_=pb_[:, 0:512].rearrange("p (k c) -> p k c", k=4)))[-1])
                gt_ev[par] = aev
                pending = None

            for n in range(KT):
                if en == "sync":
                    w("mm", prev_tmu)
                t_wl = []
                for k in range(KT):
                    t_wl.append(ld(wgb[:, k * 512:(k + 1) * 512],
                                   wg_s[k][:, n * 512:(n + 1) * 512], "w1"))
                    t_wl.append(ld(wub[:, k * 512:(k + 1) * 512],
                                   wu_s[k][:, n * 512:(n + 1) * 512], "w2"))
                for mtk in range(4):
                    it = n * 4 + mtk
                    par = it % 2
                    gspb = gsp0 if par == 0 else gsp1
                    pg = PS[par]
                    pu = PS[2 + par]
                    if en == "tensor" and mtk == 0:
                        wv(t_wl[-2])
                        wv(t_wl[-1])
                    if en == "tensor":
                        w("act", ga_ev.get(par, 0))
                        w("dve", gu_rd.get(par, 0))
                    tmg = 0
                    for k in range(KT):
                        tmg = mm(pg[:, 0:512],
                                 XL3T[:, k * 512 + mtk * 128: k * 512 + (mtk + 1) * 128],
                                 wgb[:, k * 512:(k + 1) * 512],
                                 start=(k == 0), stop=(k == KT - 1), inc=(k == KT - 1))
                    tmu = 0
                    for k in range(KT):
                        tmu = mm(pu[:, 0:512],
                                 XL3T[:, k * 512 + mtk * 128: k * 512 + (mtk + 1) * 128],
                                 wub[:, k * 512:(k + 1) * 512],
                                 start=(k == 0), stop=(k == KT - 1), inc=(k == KT - 1))
                    prev_tmu = tmu
                    flush_pending()
                    a_s = act(lambda s, p_=pg, tm=tmg, pd=prev_dsp: (
                        s.wait_ge(SEM["mm"], tm),
                        s.wait_ge(SEM["dve"], pd) if pd else None,
                        s.activation(out=silE, in_=p_[:, 0:512], func=AF.Silu))[-1])
                    ga_ev[par] = a_s
                    if en == "vector":
                        w("act", a_s)
                        w("mm", tmu)
                        w("mm", gsp_rd.get(par, 0))
                    d_g = dve(lambda v, p_=pu: v.tensor_tensor(
                        gfE, silE, p_[:, 0:512], ALU.mult))
                    a_ab = act(lambda s, dg=d_g: (
                        s.wait_ge(SEM["dve"], dg),
                        s.activation(out=abE, in_=gfE, func=AF.Abs))[-1])
                    if en == "vector":
                        w("act", a_ab)
                    ab3 = abE.rearrange("p (q f) -> p q f", f=4)
                    gf3 = gfE.rearrange("p (q f) -> p q f", f=4)
                    gs3 = gspb.rearrange("p (q f) -> p q f", f=4)
                    dve(lambda v: v.tensor_tensor(m1E, ab3[:, :, 0], ab3[:, :, 1], ALU.max))
                    dve(lambda v: v.tensor_tensor(m2E, ab3[:, :, 2], ab3[:, :, 3], ALU.max))
                    dve(lambda v: v.tensor_tensor(n1E, ab3[:, :, 0], ab3[:, :, 1], ALU.min))
                    dve(lambda v: v.tensor_tensor(n2E, ab3[:, :, 2], ab3[:, :, 3], ALU.min))
                    dve(lambda v: v.tensor_tensor(m1E, m1E, m2E, ALU.min))
                    dve(lambda v: v.tensor_tensor(n1E, n1E, n2E, ALU.max))
                    dve(lambda v: v.tensor_tensor(thrE, m1E, n1E, ALU.max))
                    d_sp = 0
                    for f4 in range(4):
                        kk_ = k0E if f4 % 2 == 0 else k1E
                        dve(lambda v, f_=f4, kb=kk_: v.tensor_tensor(
                            kb, ab3[:, :, f_], thrE, ALU.is_ge))
                        d_sp = dve(lambda v, f_=f4, kb=kk_, g3=gs3: v.tensor_tensor(
                            g3[:, :, f_], gf3[:, :, f_], kb, ALU.mult))
                    prev_dsp = d_sp
                    gu_rd[par] = d_g
                    pending = (n, mtk, d_sp)
            flush_pending()
            barrier()

            nxt_wd = gd(wdb0, wd_s[0], "ga")
            st_o = None
            prev_e_mm = 0
            de_ev = {}
            for m2 in range(KT):
                wb = wdb0 if m2 % 2 == 0 else wdb1
                t_wl = nxt_wd
                if m2 + 1 < KT:
                    if en == "gpsimd":
                        w("mm", prev_e_mm)
                    nxt_wd = gd(wdb1 if m2 % 2 == 0 else wdb0, wd_s[m2 + 1], "ga")
                po = PS[4 + m2 % 2]
                if en == "tensor":
                    wv(t_wl)
                    w("act", de_ev.get(m2 % 2, 0))
                tmo = 0
                for k in range(H1T):
                    tmo = mm(po[:, 0:512], wb[:, k * 128:(k + 1) * 128],
                             GT[:, k * 512:(k + 1) * 512],
                             start=(k == 0), stop=(k == H1T - 1), inc=(k == H1T - 1))
                prev_e_mm = tmo
                aev = act(lambda s, p_=po, tm=tmo, so=st_o: (
                    s.wait_ge(SEM["mm"], tm),
                    s.wait_ge(SEM[so[0]], so[1]) if so else None,
                    s.copy(out=ocbE, in_=p_[:, 0:512]))[-1])
                de_ev[m2 % 2] = aev
                if en == "sync":
                    w("act", aev)
                st_o = ld(och[m2 * 128:(m2 + 1) * 128, r0:r0 + 512], ocbE, "q")
            barrier()

            prev_last = 0
            for tt in range(4):
                if en == "sync":
                    w("dve", prev_last)
                t_y = ld(ytE, och[:, r0 + tt * 128: r0 + (tt + 1) * 128], "o", transpose=True)
                t_a = ld(xaE, xr3[r0 + 128 * tt: r0 + 128 * (tt + 1), :])
                if en in ("scalar", "vector"):
                    wv(t_y)
                rstd, mb, _ = ln_stats(ytE, 0, sqjE)
                ln_apply(tE, ytE, rstd, mb)
                if en == "vector":
                    wv(t_a)
                d_r = dve(lambda v: v.tensor_tensor(tE, tE, xaE, ALU.add))
                prev_last = d_r
                if en == "sync":
                    w("dve", d_r)
                t_st = ld(out_e[r0 + 128 * tt: r0 + 128 * (tt + 1), :], tE, "s")
                if en == "vector":
                    wv(t_st)
            barrier()
        barrier()

    with nc.Block() as block:
        @block.sync
        def _(e):
            stream("sync", e)

        @block.gpsimd
        def _(e):
            stream("gpsimd", e)

        @block.tensor
        def _(e):
            stream("tensor", e)

        @block.scalar
        def _(e):
            stream("scalar", e)

        @block.vector
        def _(e):
            stream("vector", e)

    es.close()
    return nc



def _lhsT_tiles(A):
    K, M = A.shape
    kt, mt = K // 128, M // 128
    return np.ascontiguousarray(
        A.reshape(kt, 128, mt, 128).transpose(2, 1, 0, 3).reshape(mt, 128, kt * 128))


def kernel(**inp):
    x = np.asarray(inp["L0_attn_x"])
    Wa = np.asarray(inp["L0_attn_Wa"]); Wx = np.asarray(inp["L0_attn_Wx"])
    ba = np.asarray(inp["L0_attn_ba"]); bx = np.asarray(inp["L0_attn_bx"])
    Lam = np.asarray(inp["L0_attn_Lam"])
    w1 = np.asarray(inp["L0_ffn_w1"]); w3 = np.asarray(inp["L0_ffn_w3"])
    w2 = np.asarray(inp["L0_ffn_w2"])
    Wq = np.asarray(inp["L1_attn_Wq"]); bq = np.asarray(inp["L1_attn_bq"])
    Wk = np.asarray(inp["L1_attn_Wk"]); bk = np.asarray(inp["L1_attn_bk"])
    Wv = np.asarray(inp["L1_attn_Wv"]); bv = np.asarray(inp["L1_attn_bv"])
    Wo = np.asarray(inp["L1_attn_Wo"]); bo = np.asarray(inp["L1_attn_bo"])
    wg = np.asarray(inp["L1_ffn_w_gate"]); wu = np.asarray(inp["L1_ffn_w_up"])
    wd = np.asarray(inp["L1_ffn_w_down"])

    f32 = np.float32
    wat = _lhsT_tiles(np.ascontiguousarray(Wa.T).astype(f32))
    wxt = _lhsT_tiles(np.ascontiguousarray(Wx.T).astype(f32))
    w1p = np.zeros((D, H0), f32); w1p[:, :H0_TRUE] = w1.T
    w3p = np.zeros((D, H0), f32); w3p[:, :H0_TRUE] = w3.T
    w2p = np.zeros((H0, D), f32); w2p[:H0_TRUE, :] = w2.T
    w1tt = _lhsT_tiles(w1p)
    w3tt = _lhsT_tiles(w3p)
    w2tt = _lhsT_tiles(w2p)
    wqt = _lhsT_tiles(Wq.astype(f32))
    wkt = _lhsT_tiles(Wk.astype(f32))
    wvt = _lhsT_tiles(Wv.astype(f32))
    wot = _lhsT_tiles(Wo.astype(f32))
    wg_t = np.ascontiguousarray(wg.reshape(KT, 128, H1)).astype(f32)
    wu_t = np.ascontiguousarray(wu.reshape(KT, 128, H1)).astype(f32)
    wd_t = np.ascontiguousarray(_lhsT_tiles(wd.astype(f32)).reshape(KT, 128, H1))

    def colvec(v, n):
        return np.ascontiguousarray(np.asarray(v).reshape(n, 128).T).astype(f32)

    ba_t = colvec(ba, KT); bx_t = colvec(bx, KT); lam_t = colvec(Lam, KT)
    bq_t = colvec(bq, KT); bk_t = colvec(bk, 2); bv_t = colvec(bv, 2)
    bo_t = colvec(bo, KT)
    eye = np.eye(128).astype(ml_dtypes.bfloat16)
    cst = np.zeros((128, 2), f32); cst[:, 0] = 1.0; cst[:, 1] = EPS
    onesr = np.ones((1, 128), f32)

    nc = build_nc()
    in_maps = []
    for core in range(8):
        b, p = core // 2, core % 2
        pm = np.zeros((128, 2), f32)
        pm[:, 0] = 1.0 - p
        pm[:, 1] = p
        in_maps.append({
            "x0": np.ascontiguousarray(x[b]).astype(f32),
            "ident": eye,
            "wat": wat, "wxt": wxt, "ba_t": ba_t, "bx_t": bx_t, "lam_t": lam_t,
            "w1t": w1tt, "w3t": w3tt, "w2t": w2tt,
            "wqt": wqt, "wkt": wkt, "wvt": wvt, "wot": wot,
            "bq_t": bq_t, "bk_t": bk_t, "bv_t": bv_t, "bo_t": bo_t,
            "wg_in": wg_t, "wu_in": wu_t, "wd_in": wd_t,
            "pmask": pm, "consts": cst, "onesr_in": onesr,
        })
    res = run_bass_kernel_spmd(nc, in_maps, core_ids=list(range(8)))
    out = np.empty((4, S, D), np.float32)
    for core in range(8):
        b, p = core // 2, core % 2
        out[b, p * 2048:(p + 1) * 2048, :] = res.results[core]["out"]
    return out




# version 04
# speedup vs baseline: 1.5440x; 1.1226x over previous
import numpy as np
import ml_dtypes
import concourse.bass as bass
import concourse.mybir as mybir
from concourse.bass_utils import run_bass_kernel_spmd

F32 = mybir.dt.float32
BF16 = mybir.dt.bfloat16
AF = mybir.ActivationFunctionType
ALU = mybir.AluOpType
AX = mybir.AxisListType.X

D = 2048
S = 4096
KT = 16
H0 = 5504
H0T = 43
H0_TRUE = 5461
H1 = 8192
H1T = 64
NW = 8
WIN = 1152
EPS = 1e-5
ATT_SCALE = float(1.0 / np.sqrt(128.0))


def build_nc(debug=False):
    nc = bass.Bass("TRN2", target_bir_lowering=False)

    x0 = nc.dram_tensor("x0", [S, D], F32, kind="ExternalInput").ap()
    ident = nc.dram_tensor("ident", [128, 128], BF16, kind="ExternalInput").ap()
    wat = nc.dram_tensor("wat", [KT, 128, D], F32, kind="ExternalInput").ap()
    wxt = nc.dram_tensor("wxt", [KT, 128, D], F32, kind="ExternalInput").ap()
    ba_t = nc.dram_tensor("ba_t", [128, KT], F32, kind="ExternalInput").ap()
    bx_t = nc.dram_tensor("bx_t", [128, KT], F32, kind="ExternalInput").ap()
    lam_t = nc.dram_tensor("lam_t", [128, KT], F32, kind="ExternalInput").ap()
    w1t = nc.dram_tensor("w1t", [H0T, 128, D], F32, kind="ExternalInput").ap()
    w3t = nc.dram_tensor("w3t", [H0T, 128, D], F32, kind="ExternalInput").ap()
    w2t = nc.dram_tensor("w2t", [KT, 128, H0], F32, kind="ExternalInput").ap()
    wqt = nc.dram_tensor("wqt", [KT, 128, D], F32, kind="ExternalInput").ap()
    wkt = nc.dram_tensor("wkt", [2, 128, D], F32, kind="ExternalInput").ap()
    wvt = nc.dram_tensor("wvt", [2, 128, D], F32, kind="ExternalInput").ap()
    wot = nc.dram_tensor("wot", [KT, 128, D], F32, kind="ExternalInput").ap()
    bq_t = nc.dram_tensor("bq_t", [128, KT], F32, kind="ExternalInput").ap()
    bk_t = nc.dram_tensor("bk_t", [128, 2], F32, kind="ExternalInput").ap()
    bv_t = nc.dram_tensor("bv_t", [128, 2], F32, kind="ExternalInput").ap()
    bo_t = nc.dram_tensor("bo_t", [128, KT], F32, kind="ExternalInput").ap()
    wg_in = nc.dram_tensor("wg_in", [KT, 128, H1], F32, kind="ExternalInput").ap()
    wu_in = nc.dram_tensor("wu_in", [KT, 128, H1], F32, kind="ExternalInput").ap()
    wd_in = nc.dram_tensor("wd_in", [KT, 128, H1], F32, kind="ExternalInput").ap()
    pmask = nc.dram_tensor("pmask", [128, 2], F32, kind="ExternalInput").ap()
    consts = nc.dram_tensor("consts", [128, 2], F32, kind="ExternalInput").ap()
    onesr_in = nc.dram_tensor("onesr_in", [1, 128], F32, kind="ExternalInput").ap()

    out_e = nc.dram_tensor("out", [2048, D], F32, kind="ExternalOutput").ap()

    ikind = "ExternalOutput" if debug else "Internal"
    LOGA = nc.dram_tensor("LOGA", [KT, 128, S], BF16, kind="Internal").ap()
    UDAT = nc.dram_tensor("UDAT", [KT, 128, S], BF16, kind="Internal").ap()
    YSC = nc.dram_tensor("YSC", [D, S], BF16, kind="Internal").ap()
    xr1 = nc.dram_tensor("xr1", [S, D], F32, kind=ikind).ap()
    xr2 = nc.dram_tensor("xr2", [S, D], F32, kind=ikind).ap()
    xr3 = nc.dram_tensor("xr3", [2048, D], F32, kind=ikind).ap()
    q1s = nc.dram_tensor("q1s", [H0T, 128, D], BF16, kind="Internal").ap()
    q3s = nc.dram_tensor("q3s", [H0T, 128, D], BF16, kind="Internal").ap()
    q2s = nc.dram_tensor("q2s", [KT, 128, H0], BF16, kind="Internal").ap()
    wg_s = nc.dram_tensor("wg_s", [KT, 128, H1], BF16, kind="Internal").ap()
    wu_s = nc.dram_tensor("wu_s", [KT, 128, H1], BF16, kind="Internal").ap()
    wd_s = nc.dram_tensor("wd_s", [KT, 128, H1], BF16, kind="Internal").ap()
    och = nc.dram_tensor("och", [D, 2048], BF16, kind="Internal").ap()

    import contextlib
    es = contextlib.ExitStack()
    sb = lambda name, shape, dt: es.enter_context(nc.sbuf_tensor(name, shape, dt))

    NBF = 64 * 1024
    NF32 = 18 * 1024 + 512
    ABF = sb("ABF", [128, NBF], BF16)
    AF32 = sb("AF32", [128, NF32], F32)
    small = sb("small", [128, 128], F32)
    stat = sb("stat", [128, 64], F32)
    idn = sb("idn", [128, 128], BF16)
    ones_r = sb("ones_r", [1, 128], F32)
    ones_c = sb("ones_c", [128, 1], F32)
    s11 = sb("s11", [1, 8], F32)

    def f32s(c0, n):
        assert c0 + n <= NF32, (c0, n)
        return AF32[:, c0:c0 + n]

    def bfs(c0, n):
        assert c0 + n <= NBF, (c0, n)
        return ABF[:, c0:c0 + n]

    ba_c = small[:, 0:16]
    bx_c = small[:, 16:32]
    mco_c = small[:, 32:48]
    bq_c = small[:, 48:64]
    bk_c = small[:, 64:66]
    bv_c = small[:, 66:68]
    bo_c = small[:, 68:84]
    thr1 = small[:, 84:85]
    thr3 = small[:, 85:86]
    thr2 = small[:, 86:87]
    s1c = small[:, 87:88]
    me_c = small[:, 88:89]
    mo_c = small[:, 89:90]
    lam_c = small[:, 90:106]
    eps_c = small[:, 111:112]
    nthr1 = small[:, 107:108]
    ones_cc = small[:, 110:111]
    nthr3 = small[:, 108:109]
    nthr2 = small[:, 109:110]

    PS = [es.enter_context(nc.psum_tensor(f"PS{i}", [128, 512], F32)) for i in range(6)]
    PB = [es.enter_context(nc.psum_tensor(f"PB{i}", [128, 1024], BF16)) for i in range(2)]

    SEM = {}
    for s_ in ("a", "o", "w1", "w2", "w3", "s", "q", "wc", "ga", "gb",
               "mm", "act", "dve"):
        SEM[s_] = es.enter_context(nc.semaphore(s_))

    DMAF = ("a", "o", "w1", "w2", "w3", "s", "q", "wc", "ga", "gb")

    def stream(en, e):
        cnt = {f: 0 for f in DMAF}
        cnt.update({"mm": 0, "act": 0, "dve": 0})

        def ld(out_ap, in_ap, fam="a", transpose=False):
            prev = cnt[fam]
            cnt[fam] += 16
            if en == "sync":
                if prev:
                    e.wait_ge(SEM[fam], prev)
                e.dma_start(out=out_ap, in_=in_ap, transpose=transpose).then_inc(SEM[fam], 16)
            return (fam, cnt[fam])

        def gd(out_ap, in_ap, fam="ga"):
            prev = cnt[fam]
            cnt[fam] += 16
            if en == "gpsimd":
                if prev:
                    e.wait_ge(SEM[fam], prev)
                e.dma_start(out=out_ap, in_=in_ap).then_inc(SEM[fam], 16)
            return (fam, cnt[fam])

        def w(sem, val):
            if val:
                e.wait_ge(SEM[sem], val)

        def wv(tok):
            if tok:
                e.wait_ge(SEM[tok[0]], tok[1])

        def lwv(tok):
            return tok

        def mm(out, lhsT, rhs, start, stop, inc=True):
            if inc:
                cnt["mm"] += 1
            if en == "tensor":
                i = e.matmul(out, lhsT, rhs, start=start, stop=stop)
                if inc:
                    i.then_inc(SEM["mm"], 1)
            return cnt["mm"]

        def tp(out, in_, inc=True):
            if inc:
                cnt["mm"] += 1
            if en == "tensor":
                i = e.transpose(out, in_, idn[:])
                if inc:
                    i.then_inc(SEM["mm"], 1)
            return cnt["mm"]

        def act(fn, inc=True):
            if inc:
                cnt["act"] += 1
            if en == "scalar":
                if cnt["act"] > 1:
                    e.wait_ge(SEM["act"], cnt["act"] - 1)
                i = fn(e)
                if inc:
                    i.then_inc(SEM["act"], 1)
            return cnt["act"]

        def dve(fn, inc=True):
            if inc:
                cnt["dve"] += 1
            if en == "vector":
                if cnt["dve"] > 1:
                    e.wait_ge(SEM["dve"], cnt["dve"] - 1)
                i = fn(e)
                if inc:
                    i.then_inc(SEM["dve"], 1)
            return cnt["dve"]

        def barrier():
            tgt = dict(cnt)
            for s_ in list(DMAF) + ["mm", "act", "dve"]:
                w(s_, tgt[s_])

        def ln_stats(src, sc, sq_junk, pre_act=0, pre_dve=0):
            ssq = stat[:, sc:sc + 1]
            sm = stat[:, sc + 1:sc + 2]
            mn = stat[:, sc + 2:sc + 3]
            var = stat[:, sc + 3:sc + 4]
            rstd = stat[:, sc + 4:sc + 5]
            mb = stat[:, sc + 5:sc + 6]
            a1 = act(lambda s, pa=pre_act: (
                s.wait_ge(SEM["act"], pa) if pa else None,
                s.activation(out=sq_junk, in_=src, func=AF.Square, accum_out=ssq))[-1])
            d1 = dve(lambda v, pd=pre_dve: (
                v.wait_ge(SEM["dve"], pd) if pd else None,
                v.reduce_sum(sm, src, AX))[-1])
            if en == "vector":
                w("act", a1)
            dve(lambda v: v.tensor_scalar_mul(mn, sm, 1.0 / D))
            dve(lambda v: v.tensor_scalar_mul(var, ssq, 1.0 / D))
            dve(lambda v: v.tensor_tensor(mb, mn, mn, ALU.mult))
            d3 = dve(lambda v: v.tensor_tensor(var, var, mb, ALU.subtract))
            a2 = act(lambda s, dd=d3: (
                s.wait_ge(SEM["dve"], dd),
                s.activation(out=rstd, in_=var, func=AF.Sqrt, bias=eps_c))[-1])
            if en == "vector":
                w("act", a2)
            dve(lambda v: v.reciprocal(rstd, rstd))
            d5 = dve(lambda v: v.tensor_tensor(mb, mn, rstd, ALU.mult))
            return rstd, mb, d5

        def ln_apply(dst, src, rstd, mb):
            return dve(lambda v: v.tensor_scalar(dst, src, rstd, mb,
                                                 ALU.mult, ALU.subtract))

        def emit_t16(xln_ap, dst_tensor_ap, stride, coff, dapp, pbev=None):
            last = 0
            for half in range(2):
                pb = PB[half]
                if en == "tensor":
                    w("dve", dapp)
                    if pbev is not None:
                        w("act", pbev.get(half, 0))
                tmm = 0
                for kk in range(8):
                    k = half * 8 + kk
                    tmm = tp(pb[:, kk * 128:(kk + 1) * 128],
                             xln_ap[:, k * 128:(k + 1) * 128])
                dst = dst_tensor_ap.rearrange("p (k c) -> p k c", c=stride)[
                    :, half * 8:(half + 1) * 8, coff:coff + 128]
                srcp = pb[:].rearrange("p (k c) -> p k c", k=8)
                last = act(lambda s, d=dst, sr=srcp, tm=tmm: (
                    s.wait_ge(SEM["mm"], tm),
                    s.copy(out=d, in_=sr))[-1])
                if pbev is None:
                    if en == "tensor":
                        w("act", last)
                else:
                    pbev[half] = last
            return cnt["mm"], last

        ld(ba_c, ba_t)
        ld(bx_c, bx_t)
        ld(lam_c, lam_t)
        ld(bq_c, bq_t)
        ld(bk_c, bk_t)
        ld(bv_c, bv_t)
        ld(bo_c, bo_t)
        ld(small[:, 88:90], pmask)
        ld(idn[:], ident)
        ld(small[:, 110:112], consts)
        ld(ones_r[:], onesr_in)
        barrier()
        aS1 = act(lambda s: s.activation(out=mco_c, in_=lam_c, func=AF.Exp, scale=-1.0))
        aS2 = act(lambda s, p=aS1: (s.wait_ge(SEM["act"], p),
                  s.activation(out=mco_c, in_=mco_c, func=AF.Ln, bias=1.0))[-1])
        act(lambda s, p=aS2: (s.wait_ge(SEM["act"], p),
            s.mul(mco_c, mco_c, -8.0))[-1])
        barrier()

        XL0T = bfs(0, 32768)
        wam0 = bfs(32768, 2048)
        wam1 = bfs(34816, 2048)
        wxm0 = bfs(36864, 2048)
        wxm1 = bfs(38912, 2048)
        logab0 = bfs(40960, 2048)
        ub0 = bfs(43008, 2048)
        logab1 = bfs(45056, 2048)
        ub1 = bfs(47104, 2048)
        xlnA = bfs(49152, 2048)
        sqjA = bfs(51200, 2048)
        p1junk = bfs(53248, 5504)
        xtA0 = f32s(0, 2048)
        xtA1 = f32s(2048, 2048)
        rA = f32s(4096, 2048)
        iA = f32s(6144, 2048)
        ixA = f32s(8192, 2048)
        sqA = f32s(10240, 2048)
        wp1 = f32s(12288, 5504)
        QACC = f32s(17920, 104)

        pass1_list = ([(w1t[i], D) for i in range(H0T)] +
                      [(w3t[i], D) for i in range(H0T)] +
                      [(w2t[i], H0) for i in range(KT)])
        p1 = {"idx": 0, "aa": 0}

        def emit_pass1_some(nmax):
            done = 0
            while p1["idx"] < len(pass1_list) and done < nmax:
                ap_, cols = pass1_list[p1["idx"]]
                if en == "sync":
                    w("act", p1["aa"])
                t_ld = ld(wp1[:, 0:cols], ap_, "w3")
                p1["aa"] = act(lambda s, c_=cols, tl=t_ld, ii=p1["idx"]: (
                    s.wait_ge(SEM[tl[0]], tl[1]),
                    s.activation(out=p1junk[:, 0:c_], in_=wp1[:, 0:c_], func=AF.Abs,
                                 accum_out=QACC[:, ii:ii + 1]))[-1])
                p1["idx"] += 1
                done += 1

        for p in range(2):
            t0 = 2048 * p
            prev_tp = 0
            prev_ev = 0
            prev_dappA = 0
            pbevA = {}
            for tt in range(16):
                xt = xtA0 if tt % 2 == 0 else xtA1
                if en == "sync":
                    w("dve", prev_dappA)
                t_x = ld(xt, x0[t0 + 128 * tt: t0 + 128 * (tt + 1), :])
                if en in ("scalar", "vector"):
                    wv(t_x)
                rstd, mb, dstat = ln_stats(xt, 0 if tt % 2 == 0 else 8, sqjA,
                                           pre_act=prev_ev)
                if en == "vector":
                    w("mm", prev_tp)
                dapp = ln_apply(xlnA, xt, rstd, mb)
                prev_dappA = dapp
                prev_tp, prev_ev = emit_t16(xlnA, XL0T[:], 2048, tt * 128, dapp, pbevA)
                emit_pass1_some(1)
            barrier()

            nxt_w = (gd(wam0, wat[0], "ga"), gd(wxm0, wxt[0], "gb"))
            st_l = None
            st_u = None
            prev_du = 0
            prev_mlast = 0
            bank_ev = {}
            for m in range(KT):
                wam = wam0 if m % 2 == 0 else wam1
                wxm = wxm0 if m % 2 == 0 else wxm1
                t_w = nxt_w
                if m + 1 < KT:
                    if en == "gpsimd":
                        w("mm", prev_mlast)
                    nxt_w = (gd(wam1 if m % 2 == 0 else wam0, wat[m + 1], "ga"),
                             gd(wxm1 if m % 2 == 0 else wxm0, wxt[m + 1], "gb"))
                lb = logab0 if m % 2 == 0 else logab1
                uo = ub0 if m % 2 == 0 else ub1
                a_last = 0
                for c in range(4):
                    psr = PS[c % 2]
                    psi = PS[2 + c % 2]
                    if en == "tensor" and c == 0:
                        wv(t_w[0])
                        wv(t_w[1])
                    if en == "tensor":
                        w("act", bank_ev.get(c % 2, 0))
                        w("act", bank_ev.get(2 + c % 2, 0))
                    tmr = 0
                    for k in range(KT):
                        tmr = mm(psr[:, 0:512], wam[:, k * 128:(k + 1) * 128],
                                 XL0T[:, k * 2048 + 512 * c: k * 2048 + 512 * (c + 1)],
                                 start=(k == 0), stop=(k == KT - 1), inc=(k == KT - 1))
                    tmi = 0
                    for k in range(KT):
                        tmi = mm(psi[:, 0:512], wxm[:, k * 128:(k + 1) * 128],
                                 XL0T[:, k * 2048 + 512 * c: k * 2048 + 512 * (c + 1)],
                                 start=(k == 0), stop=(k == KT - 1), inc=(k == KT - 1))
                    ar = act(lambda s, ps=psr, cc=c, mm_=m, tm=tmr, pd=prev_du: (
                        s.wait_ge(SEM["mm"], tm),
                        s.wait_ge(SEM["dve"], pd) if (pd and cc == 0) else None,
                        s.activation(out=rA[:, 512 * cc:512 * (cc + 1)], in_=ps[:, 0:512],
                                     func=AF.Sigmoid, bias=ba_c[:, mm_:mm_ + 1]))[-1])
                    ai = act(lambda s, ps=psi, cc=c, mm_=m, tm=tmi: (
                        s.wait_ge(SEM["mm"], tm),
                        s.activation(out=iA[:, 512 * cc:512 * (cc + 1)], in_=ps[:, 0:512],
                                     func=AF.Sigmoid, bias=bx_c[:, mm_:mm_ + 1]))[-1])
                    a_last = ai
                    bank_ev[c % 2] = ar
                    bank_ev[2 + c % 2] = ai
                    if c == 3:
                        prev_mlast = tmi
                if en == "vector":
                    w("act", a_last)
                    wv(st_l)
                d_lg = dve(lambda v, lb_=lb, mm_=m: v.tensor_scalar_mul(
                    lb_, rA, mco_c[:, mm_:mm_ + 1]))
                d_ix = dve(lambda v, mm_=m: v.tensor_tensor(
                    ixA, iA, XL0T[:, mm_ * 2048:(mm_ + 1) * 2048], ALU.mult))
                a_a2 = act(lambda s, lb_=lb, dl=d_lg: (
                    s.wait_ge(SEM["dve"], dl),
                    s.activation(out=sqA, in_=lb_, func=AF.Exp, scale=2.0))[-1])
                a_sq = act(lambda s, p=a_a2: (s.wait_ge(SEM["act"], p),
                    s.activation(out=sqA, in_=sqA, func=AF.Sqrt,
                                 scale=-1.0, bias=1.0))[-1])
                if en == "vector":
                    w("act", a_sq)
                    wv(st_u)
                d_u = dve(lambda v, uo_=uo: v.tensor_tensor(uo_, ixA, sqA, ALU.mult))
                prev_du = d_u
                if en == "sync":
                    w("dve", d_u)
                st_l = ld(LOGA[m][:, t0:t0 + 2048], lb, "s")
                st_u = ld(UDAT[m][:, t0:t0 + 2048], uo, "s")
                emit_pass1_some(2)
            barrier()
        emit_pass1_some(10000)
        barrier()

        ranges = [(0, H0T, H0_TRUE * D, thr1, True),
                  (H0T, 2 * H0T, H0_TRUE * D, thr3, False),
                  (2 * H0T, 2 * H0T + KT, D * H0_TRUE, thr2, False)]
        for idx, (c0, c1, count, thr_col, want_s1) in enumerate(ranges):
            dr = dve(lambda v, a=c0, b=c1, ii=idx: v.reduce_sum(
                stat[:, 32 + ii:33 + ii], QACC[:, a:b], AX))
            if en == "tensor":
                w("dve", dr)
            tm = mm(PS[4][0:1, idx:idx + 1], stat[:, 32 + idx:33 + idx], ones_cc,
                    start=True, stop=True)
            aa = act(lambda s, ii=idx, cc=count, tm_=tm: (
                s.wait_ge(SEM["mm"], tm_),
                s.activation(out=s11[0:1, ii:ii + 1], in_=PS[4][0:1, ii:ii + 1],
                             func=AF.Copy, scale=0.5 / cc))[-1])
            if en == "tensor":
                w("act", aa)
            tb = mm(PS[5][:, idx:idx + 1], ones_r[:], s11[0:1, idx:idx + 1],
                    start=True, stop=True)
            a_thr = act(lambda s, th=thr_col, tb_=tb: (
                s.wait_ge(SEM["mm"], tb_),
                s.copy(out=th, in_=PS[5][:, idx:idx + 1]))[-1])
            if want_s1:
                act(lambda s: s.mul(s1c, thr1, 2.0))
            nth = (nthr1, nthr3, nthr2)[idx]
            dve(lambda v, th=thr_col, nt=nth, p=a_thr: (
                v.wait_ge(SEM["act"], p),
                v.tensor_scalar_mul(nt, th, -1.0))[-1])
            barrier()

        lgB0 = bfs(0, 4096)
        lgB1 = bfs(4096, 4096)
        uB0 = bfs(8192, 4096)
        uB1 = bfs(12288, 4096)
        yB0 = bfs(16384, 4096)
        yB1 = bfs(20480, 4096)
        sgB = bfs(24576, 5504)
        mkB = bfs(30080, 5504)
        qB = bfs(35584, 5504)
        yttB = bfs(41088, 2048)
        sqjB = bfs(43136, 2048)
        aB = f32s(0, 4096)
        uB32 = f32s(4096, 4096)
        xtB = f32s(8192, 2048)
        tB = f32s(10240, 2048)

        pass2_list = ([(w1t[i], q1s[i], D, thr1, nthr1) for i in range(H0T)] +
                      [(w3t[i], q3s[i], D, thr3, nthr3) for i in range(H0T)] +
                      [(w2t[i], q2s[i], H0, thr2, nthr2) for i in range(KT)])
        p2 = {"idx": 0, "st": None, "dmk": 0, "dq": 0}

        def emit_pass2_some(nmax):
            done = 0
            while p2["idx"] < len(pass2_list) and done < nmax:
                src, dst, cols, thr_col, nth_col = pass2_list[p2["idx"]]
                if en == "sync":
                    w("dve", p2["dmk"])
                t_ld = ld(wp1[:, 0:cols], src, "w3")
                dpos = dve(lambda v, c_=cols, th=thr_col, tl=t_ld: (
                    v.wait_ge(SEM[tl[0]], tl[1]),
                    v.tensor_scalar(mkB[:, 0:c_], wp1[:, 0:c_], th, None,
                                    ALU.is_gt))[-1])
                dmk = dve(lambda v, c_=cols, nt=nth_col, pl=p2["st"]: (
                    v.wait_ge(SEM[pl[0]], pl[1]) if pl else None,
                    v.tensor_scalar(sgB[:, 0:c_], wp1[:, 0:c_], nt, None,
                                    ALU.is_lt))[-1])
                dq = dve(lambda v, c_=cols: v.tensor_tensor(
                    qB[:, 0:c_], mkB[:, 0:c_], sgB[:, 0:c_], ALU.subtract))
                if en == "sync":
                    w("dve", dq)
                p2["st"] = ld(dst, qB[:, 0:cols], "q")
                p2["dmk"] = dmk
                p2["dq"] = dq
                p2["idx"] += 1
                done += 1

        st_y = None
        prev_dy = 0
        prev_au = 0
        nxt_in = (ld(lgB0, LOGA[0][:, :], "w1"), ld(uB0, UDAT[0][:, :], "w2"))
        for m in range(KT):
            lg = lgB0 if m % 2 == 0 else lgB1
            uu = uB0 if m % 2 == 0 else uB1
            yy = yB0 if m % 2 == 0 else yB1
            t_in = nxt_in
            if m + 1 < KT:
                if en == "sync":
                    w("act", prev_au)
                nxt_in = (ld(lgB1 if m % 2 == 0 else lgB0, LOGA[m + 1][:, :], "w1"),
                          ld(uB1 if m % 2 == 0 else uB0, UDAT[m + 1][:, :], "w2"))
            a_a = act(lambda s, lg_=lg, ti=t_in, pd=prev_dy: (
                s.wait_ge(SEM[ti[0][0]], ti[0][1]),
                s.wait_ge(SEM[ti[1][0]], ti[1][1]),
                s.wait_ge(SEM["dve"], pd) if pd else None,
                s.activation(out=aB, in_=lg_, func=AF.Exp))[-1])
            a_u = act(lambda s, uu_=uu: s.copy(out=uB32, in_=uu_))
            prev_au = a_u
            if en == "vector":
                w("act", a_u)
                wv(st_y)
            dve(lambda v, yy_=yy: v.tensor_tensor_scan(
                out=yy_[:, 0:2048], data0=aB[:, 0:2048], data1=uB32[:, 0:2048],
                initial=0.0, op0=ALU.mult, op1=ALU.add))
            dve(lambda v, yy_=yy: v.tensor_copy(stat[:, 30:31], yy_[:, 2047:2048]))
            d_y = dve(lambda v, yy_=yy: v.tensor_tensor_scan(
                out=yy_[:, 2048:4096], data0=aB[:, 2048:4096], data1=uB32[:, 2048:4096],
                initial=stat[:, 30:31], op0=ALU.mult, op1=ALU.add))
            prev_dy = d_y
            if en == "sync":
                w("dve", d_y)
            st_y = ld(YSC[m * 128:(m + 1) * 128, :], yy, "s")
            emit_pass2_some(4)
        emit_pass2_some(10000)
        barrier()

        prev_dr = 0
        prev_st = None
        for tt in range(32):
            if en == "sync":
                w("dve", prev_dr)
            t_y = ld(yttB, YSC[:, tt * 128:(tt + 1) * 128], "o", transpose=True)
            t_x = ld(xtB, x0[tt * 128:(tt + 1) * 128, :])
            if en in ("scalar", "vector"):
                wv(t_y)
            if en == "vector":
                wv(prev_st)
            rstd, mb, _ = ln_stats(yttB, 0, sqjB)
            ln_apply(tB, yttB, rstd, mb)
            if en == "vector":
                wv(t_x)
            d_r = dve(lambda v: v.tensor_tensor(tB, tB, xtB, ALU.add))
            prev_dr = d_r
            if en == "sync":
                w("dve", d_r)
            prev_st = ld(xr1[tt * 128:(tt + 1) * 128, :], tB, "s")
        barrier()

        XL1T = bfs(0, 8192)
        GBLK = bfs(8192, 43 * 512)
        q1b0 = bfs(30208, 2048)
        q1b1 = bfs(32256, 2048)
        q3b0 = bfs(34304, 2048)
        q3b1 = bfs(36352, 2048)
        q2b0 = bfs(38400, 5504)
        q2b1 = bfs(43904, 5504)
        xlnC = bfs(49408, 2048)
        sqjC = bfs(51456, 2048)
        ocbC = bfs(53504, 512)
        ytC = bfs(54016, 2048)
        xaC = f32s(0, 2048)
        xbC = f32s(2048, 2048)
        xqC = f32s(4096, 2048)
        silC = f32s(6144, 512)
        tC = f32s(6656, 2048)

        cb0 = bfs(56064, 4096)
        cb1 = bfs(60160, 4096)
        wcast_units = []
        for k in range(KT):
            for half in range(2):
                src = wg_in[k][:, half * 4096:(half + 1) * 4096]
                dst = wg_s[half * 8:(half + 1) * 8, :, k * 512:(k + 1) * 512]\
                    .rearrange("n p c -> p n c")
                wcast_units.append((src, dst, True))
        for k in range(KT):
            for half in range(2):
                src = wu_in[k][:, half * 4096:(half + 1) * 4096]
                dst = wu_s[half * 8:(half + 1) * 8, :, k * 512:(k + 1) * 512]\
                    .rearrange("n p c -> p n c")
                wcast_units.append((src, dst, True))
        for k in range(KT):
            for half in range(2):
                wcast_units.append((wd_in[k][:, half * 4096:(half + 1) * 4096],
                                    wd_s[k][:, half * 4096:(half + 1) * 4096], False))
        wc = {"idx": 0, "st": [None, None]}

        def emit_wcast_some(nmax):
            done = 0
            while wc["idx"] < len(wcast_units) and done < nmax:
                srcp, dstp, scatter = wcast_units[wc["idx"]]
                buf = cb0 if wc["idx"] % 2 == 0 else cb1
                if en == "gpsimd":
                    wv(wc["st"][wc["idx"] % 2])
                t_c = gd(buf, srcp, "gb")
                if en == "sync":
                    wv(t_c)
                if scatter:
                    srcv = buf.rearrange("p (n c) -> p n c", n=8)
                else:
                    srcv = buf
                wc["st"][wc["idx"] % 2] = ld(dstp, srcv, "wc")
                wc["idx"] += 1
                done += 1

        blocks = [("P", 1024, None), ("P", 1536, None), ("P", 2048, None),
                  ("P", 2560, None), ("Q", 0, 3072), ("Q", 512, 3584)]

        for btype, rA_, rB_ in blocks:
            prev_tp = 0
            prev_ev = 0
            prev_dapp = 0
            pbevC = {}
            for tt in range(4):
                if en == "sync":
                    w("dve", prev_dapp)
                t_a = ld(xaC, xr1[rA_ + 128 * tt: rA_ + 128 * (tt + 1), :])
                src = xaC
                dpre = 0
                if btype == "Q":
                    t_b = ld(xbC, xr1[rB_ + 128 * tt: rB_ + 128 * (tt + 1), :])
                    if en == "vector":
                        wv(t_b)
                    dve(lambda v: v.tensor_scalar_mul(xqC, xaC, me_c))
                    dpre = dve(lambda v: v.scalar_tensor_tensor(
                        xqC, xbC, mo_c, xqC, ALU.mult, ALU.add))
                    src = xqC
                else:
                    if en == "vector":
                        wv(t_a)
                if en == "scalar":
                    if btype == "Q":
                        w("dve", dpre)
                    else:
                        wv(t_a)
                rstd, mb, _ = ln_stats(src, 0, sqjC, pre_act=prev_ev)
                if en == "vector":
                    w("mm", prev_tp)
                dapp = ln_apply(xlnC, src, rstd, mb)
                prev_dapp = dapp
                prev_tp, prev_ev = emit_t16(xlnC, XL1T[:], 512, tt * 128, dapp, pbevC)
            barrier()

            nxt_q = (ld(q1b0, q1s[0], "w1"), ld(q3b0, q3s[0], "w2"))
            prev_dg = 0
            prev_h_mm = 0
            h1_rd = {}
            h3_rd = {}
            for mh in range(H0T):
                qb1 = q1b0 if mh % 2 == 0 else q1b1
                qb3 = q3b0 if mh % 2 == 0 else q3b1
                t_ql = nxt_q
                if mh + 1 < H0T:
                    if en == "sync":
                        w("mm", prev_h_mm)
                    nxt_q = (ld(q1b1 if mh % 2 == 0 else q1b0, q1s[mh + 1], "w1"),
                             ld(q3b1 if mh % 2 == 0 else q3b0, q3s[mh + 1], "w2"))
                emit_wcast_some(1)
                ph1 = PS[mh % 2]
                ph3 = PS[2 + mh % 2]
                if en == "tensor":
                    wv(t_ql[0])
                    wv(t_ql[1])
                    w("act", h1_rd.get(mh % 2, 0))
                    w("dve", h3_rd.get(mh % 2, 0))
                tm1 = 0
                for k in range(KT):
                    tm1 = mm(ph1[:, 0:512], qb1[:, k * 128:(k + 1) * 128],
                             XL1T[:, k * 512:(k + 1) * 512],
                             start=(k == 0), stop=(k == KT - 1), inc=(k == KT - 1))
                tm3 = 0
                for k in range(KT):
                    tm3 = mm(ph3[:, 0:512], qb3[:, k * 128:(k + 1) * 128],
                             XL1T[:, k * 512:(k + 1) * 512],
                             start=(k == 0), stop=(k == KT - 1), inc=(k == KT - 1))
                a_s = act(lambda s, p_=ph1, tm=tm1, pd=prev_dg: (
                    s.wait_ge(SEM["mm"], tm),
                    s.wait_ge(SEM["dve"], pd) if pd else None,
                    s.activation(out=silC, in_=p_[:, 0:512], func=AF.Silu,
                                 scale=s1c))[-1])
                if en == "vector":
                    w("act", a_s)
                    w("mm", tm3)
                d_g = dve(lambda v, p_=ph3, mh_=mh: v.tensor_tensor(
                    GBLK[:, mh_ * 512:(mh_ + 1) * 512], silC, p_[:, 0:512], ALU.mult))
                prev_dg = d_g
                prev_h_mm = tm3
                h1_rd[mh % 2] = a_s
                h3_rd[mh % 2] = d_g

            nxt_q2 = ld(q2b0, q2s[0], "w3")
            st_o = None
            prev_d_mm = 0
            dn_ev = {}
            if en == "tensor":
                w("dve", prev_dg)
            for m2 in range(KT):
                qb2 = q2b0 if m2 % 2 == 0 else q2b1
                t_ql = nxt_q2
                if m2 + 1 < KT:
                    if en == "sync":
                        w("mm", prev_d_mm)
                    nxt_q2 = ld(q2b1 if m2 % 2 == 0 else q2b0, q2s[m2 + 1], "w3")
                emit_wcast_some(1)
                po = PS[4 + m2 % 2]
                if en == "tensor":
                    wv(t_ql)
                    w("act", dn_ev.get(m2 % 2, 0))
                tmo = 0
                for k in range(H0T):
                    tmo = mm(po[:, 0:512], qb2[:, k * 128:(k + 1) * 128],
                             GBLK[:, k * 512:(k + 1) * 512],
                             start=(k == 0), stop=(k == H0T - 1), inc=(k == H0T - 1))
                prev_d_mm = tmo
                aev = act(lambda s, p_=po, tm=tmo, so=st_o: (
                    s.wait_ge(SEM["mm"], tm),
                    s.wait_ge(SEM[so[0]], so[1]) if so else None,
                    s.copy(out=ocbC, in_=p_[:, 0:512]))[-1])
                dn_ev[m2 % 2] = aev
                if en == "sync":
                    w("act", aev)
                st_o = ld(och[m2 * 128:(m2 + 1) * 128, 0:512], ocbC, "q")
            barrier()

            prev_last = 0
            for tt in range(4):
                if en == "sync":
                    w("dve", prev_last)
                t_y = ld(ytC, och[:, tt * 128:(tt + 1) * 128], "o", transpose=True)
                t_a = ld(xaC, xr1[rA_ + 128 * tt: rA_ + 128 * (tt + 1), :])
                src = xaC
                if btype == "Q":
                    t_b = ld(xbC, xr1[rB_ + 128 * tt: rB_ + 128 * (tt + 1), :])
                    if en == "vector":
                        wv(t_b)
                    dve(lambda v: v.tensor_scalar_mul(xqC, xaC, me_c))
                    dve(lambda v: v.scalar_tensor_tensor(
                        xqC, xbC, mo_c, xqC, ALU.mult, ALU.add))
                    src = xqC
                else:
                    if en == "vector":
                        wv(t_a)
                if en in ("scalar", "vector"):
                    wv(t_y)
                rstd, mb, _ = ln_stats(ytC, 0, sqjC)
                ln_apply(tC, ytC, rstd, mb)
                d_r = dve(lambda v, s_=src: v.tensor_tensor(tC, tC, s_, ALU.add))
                prev_last = d_r
                if en == "sync":
                    w("dve", d_r)
                t_st = ld(xr2[rA_ + 128 * tt: rA_ + 128 * (tt + 1), :], tC, "s")
                if btype == "Q":
                    t_st = ld(xr2[rB_ + 128 * tt: rB_ + 128 * (tt + 1), :], tC, "s")
                if en == "vector":
                    wv(t_st)
            barrier()

        emit_wcast_some(100000)
        barrier()

        qS = bfs(0, KT * WIN)
        kS = bfs(18432, 2 * WIN)
        vS = bfs(20736, 2 * WIN)
        vT = bfs(23040, 2 * WIN)
        XWT = bfs(25344, KT * WIN)
        OSB = bfs(25344, KT * 2048)
        xlnD = bfs(43776, 2048)
        sqjD = bfs(45824, 2048)
        pnS = bfs(58112, 512)
        ptS = bfs(58624, 512)
        wqb0 = bfs(59136, 2048)
        wqb1 = bfs(61184, 2048)
        ytD = bfs(0, 2048)
        ocbD = bfs(2048, 512)
        sqjD2 = bfs(2560, 2048)
        xaD = f32s(0, 2048)
        xbD = f32s(2048, 2048)
        xqD = f32s(4096, 2048)
        tD = f32s(6144, 2048)

        prev_tp = 0
        prev_ev = 0
        prev_dapp = 0
        pbevD = {}
        for tt in range(9):
            if en == "sync":
                w("dve", prev_dapp)
            t_a = ld(xaD, xr2[128 * tt:128 * (tt + 1), :])
            t_b = ld(xbD, xr2[1024 + 128 * tt: 1024 + 128 * (tt + 1), :])
            if en == "vector":
                wv(t_b)
            dve(lambda v: v.tensor_scalar_mul(xqD, xaD, me_c))
            dpre = dve(lambda v: v.scalar_tensor_tensor(
                xqD, xbD, mo_c, xqD, ALU.mult, ALU.add))
            if en == "scalar":
                w("dve", dpre)
            rstd, mb, _ = ln_stats(xqD, 0, sqjD, pre_act=prev_ev)
            if en == "vector":
                w("mm", prev_tp)
            dapp = ln_apply(xlnD, xqD, rstd, mb)
            prev_dapp = dapp
            prev_tp, prev_ev = emit_t16(xlnD, XWT[:], WIN, tt * 128, dapp, pbevD)
        barrier()

        qkv_list = ([(wqt[m_], bq_c[:, m_:m_ + 1], qS, m_) for m_ in range(KT)] +
                    [(wkt[m_], bk_c[:, m_:m_ + 1], kS, m_) for m_ in range(2)] +
                    [(wvt[m_], bv_c[:, m_:m_ + 1], vS, m_) for m_ in range(2)])
        nxt_wq = gd(wqb0, qkv_list[0][0], "ga")
        qkv_ev = {}
        prev_q_mm = 0
        for i_, (wsrc, bias, dstbuf, m_) in enumerate(qkv_list):
            wb = wqb0 if i_ % 2 == 0 else wqb1
            t_wl = nxt_wq
            if i_ + 1 < len(qkv_list):
                if en == "gpsimd":
                    w("mm", prev_q_mm)
                nxt_wq = gd(wqb1 if i_ % 2 == 0 else wqb0, qkv_list[i_ + 1][0], "ga")
            if en == "tensor":
                wv(t_wl)
            for ci, (c0, cw) in enumerate(((0, 512), (512, 512), (1024, 128))):
                pq = PS[ci % 2]
                if en == "tensor":
                    w("act", qkv_ev.get(ci % 2, 0))
                tmq = 0
                for k in range(KT):
                    tmq = mm(pq[:, 0:cw], wb[:, k * 128:(k + 1) * 128],
                             XWT[:, k * WIN + c0: k * WIN + c0 + cw],
                             start=(k == 0), stop=(k == KT - 1), inc=(k == KT - 1))
                prev_q_mm = tmq
                aev = act(lambda s, p_=pq, d=dstbuf, mm_=m_, a=c0, ww=cw, b_=bias, tm=tmq: (
                    s.wait_ge(SEM["mm"], tm),
                    s.activation(out=d[:, mm_ * WIN + a: mm_ * WIN + a + ww],
                                 in_=p_[:, 0:ww], func=AF.Identity, bias=b_))[-1])
                qkv_ev[ci % 2] = aev
        for g_ in range(2):
            for tt in range(9):
                pb = PB[tt % 2]
                tmm = tp(pb[:, 0:128],
                         vS[:, g_ * WIN + tt * 128: g_ * WIN + (tt + 1) * 128])
                aev = act(lambda s, g0=g_, t0_=tt, tm=tmm: (
                    s.wait_ge(SEM["mm"], tm),
                    s.copy(out=vT[:, (g0 * 9 + t0_) * 128:(g0 * 9 + t0_ + 1) * 128],
                           in_=PB[t0_ % 2][:, 0:128]))[-1])
                if en == "tensor":
                    w("act", aev)
        barrier()

        last_o = {}
        last_p = {}
        prev_tmt = 0
        prev_tmo = 0
        for j in range(NW):
            for h in range(KT):
                g_ = h % 2
                ctx = (j * KT + h) % 2
                psS = PS[ctx]
                psO = PS[2 + ctx]
                pbP = PB[ctx]
                if en == "tensor" and ("s", ctx) in last_o:
                    w("act", last_o[("s", ctx)])
                tms = 0
                for qt in range(2):
                    tms = mm(psS[:, qt * 256:(qt + 1) * 256],
                             qS[:, h * WIN + j * 128 + qt * 128:
                                h * WIN + j * 128 + (qt + 1) * 128],
                             kS[:, g_ * WIN + j * 128: g_ * WIN + j * 128 + 256],
                             start=True, stop=True)
                dn_last = 0
                for qt in range(2):
                    s0 = 16 + 8 * qt
                    mx = stat[:, s0:s0 + 1]
                    mxs = stat[:, s0 + 1:s0 + 2]
                    rs = stat[:, s0 + 2:s0 + 3]
                    rr = stat[:, s0 + 3:s0 + 4]
                    dm = dve(lambda v, p_=psS, q_=qt, tm=tms, m_=mx: (
                        v.wait_ge(SEM["mm"], tm),
                        v.reduce_max(m_, p_[:, q_ * 256:(q_ + 1) * 256], AX))[-1])
                    dmx = dve(lambda v, m_=mx, ms=mxs: v.tensor_scalar_mul(
                        ms, m_, -ATT_SCALE))
                    ae = act(lambda s, p_=psS, q_=qt, dm_=dmx, ms=mxs, rs_=rs,
                             pt_=prev_tmt: (
                        s.wait_ge(SEM["dve"], dm_),
                        s.wait_ge(SEM["mm"], pt_) if (pt_ and q_ == 0) else None,
                        s.activation(out=pnS[:, q_ * 256:(q_ + 1) * 256],
                                     in_=p_[:, q_ * 256:(q_ + 1) * 256],
                                     func=AF.Exp, scale=ATT_SCALE, bias=ms,
                                     accum_out=rs_))[-1])
                    dr = dve(lambda v, a_=ae, rs_=rs, rr_=rr: (
                        v.wait_ge(SEM["act"], a_),
                        v.reciprocal(rr_, rs_))[-1])
                    dn_last = dve(lambda v, q_=qt, rr_=rr: v.tensor_scalar_mul(
                        pnS[:, q_ * 256:(q_ + 1) * 256],
                        pnS[:, q_ * 256:(q_ + 1) * 256], rr_))
                if en == "tensor":
                    w("dve", dn_last)
                    if ("p", ctx) in last_p:
                        w("act", last_p[("p", ctx)])
                tmt = 0
                for qt in range(2):
                    for kt in range(2):
                        tmt = tp(pbP[:, kt * 256 + qt * 128: kt * 256 + (qt + 1) * 128],
                                 pnS[:, qt * 256 + kt * 128: qt * 256 + (kt + 1) * 128])
                prev_tmt = tmt
                apt = act(lambda s, pb_=pbP, tm=tmt, po_=prev_tmo: (
                    s.wait_ge(SEM["mm"], tm),
                    s.wait_ge(SEM["mm"], po_) if po_ else None,
                    s.copy(out=ptS, in_=pb_[:, 0:512]))[-1])
                last_p[("p", ctx)] = apt
                if en == "tensor":
                    w("act", apt)
                tmo = 0
                for kt in range(2):
                    tmo = mm(psO[:, 0:256],
                             vT[:, (g_ * 9 + j + kt) * 128:(g_ * 9 + j + kt + 1) * 128],
                             ptS[:, kt * 256:(kt + 1) * 256],
                             start=(kt == 0), stop=(kt == 1), inc=(kt == 1))
                prev_tmo = tmo
                aoe = act(lambda s, p_=psO, h_=h, j_=j, tm=tmo: (
                    s.wait_ge(SEM["mm"], tm),
                    s.copy(out=OSB[:, h_ * 2048 + j_ * 256: h_ * 2048 + (j_ + 1) * 256],
                           in_=p_[:, 0:256]))[-1])
                last_o[("s", ctx)] = aoe
        barrier()

        nxt_wo = gd(wqb0, wot[0], "ga")
        st_o = None
        op_ev = {}
        prev_o_mm = 0
        for m in range(KT):
            wb = wqb0 if m % 2 == 0 else wqb1
            t_wl = nxt_wo
            if m + 1 < KT:
                if en == "gpsimd":
                    w("mm", prev_o_mm)
                nxt_wo = gd(wqb1 if m % 2 == 0 else wqb0, wot[m + 1], "ga")
            for c4 in range(4):
                po = PS[c4 % 2]
                if en == "tensor":
                    wv(t_wl)
                    w("act", op_ev.get(c4 % 2, 0))
                tmo = 0
                for k in range(KT):
                    tmo = mm(po[:, 0:512], wb[:, k * 128:(k + 1) * 128],
                             OSB[:, k * 2048 + c4 * 512: k * 2048 + (c4 + 1) * 512],
                             start=(k == 0), stop=(k == KT - 1), inc=(k == KT - 1))
                prev_o_mm = tmo
                aev = act(lambda s, p_=po, m_=m, tm=tmo, so=st_o: (
                    s.wait_ge(SEM["mm"], tm),
                    s.wait_ge(SEM[so[0]], so[1]) if so else None,
                    s.activation(out=ocbD, in_=p_[:, 0:512], func=AF.Identity,
                                 bias=bo_c[:, m_:m_ + 1]))[-1])
                op_ev[c4 % 2] = aev
                if en == "sync":
                    w("act", aev)
                st_o = ld(och[m * 128:(m + 1) * 128, c4 * 512:(c4 + 1) * 512], ocbD, "q")
        barrier()

        prev_last = 0
        for tt in range(16):
            if en == "sync":
                w("dve", prev_last)
            t_y = ld(ytD, och[:, tt * 128:(tt + 1) * 128], "o", transpose=True)
            t_a = ld(xaD, xr2[128 * tt:128 * (tt + 1), :])
            t_b = ld(xbD, xr2[2048 + 128 * tt: 2048 + 128 * (tt + 1), :])
            if en == "vector":
                wv(t_b)
            dve(lambda v: v.tensor_scalar_mul(xqD, xaD, me_c))
            dve(lambda v: v.scalar_tensor_tensor(xqD, xbD, mo_c, xqD, ALU.mult, ALU.add))
            if en in ("scalar", "vector"):
                wv(t_y)
            rstd, mb, _ = ln_stats(ytD, 0, sqjD2)
            ln_apply(tD, ytD, rstd, mb)
            d_r = dve(lambda v: v.tensor_tensor(tD, tD, xqD, ALU.add))
            prev_last = d_r
            if en == "sync":
                w("dve", d_r)
            t_st = ld(xr3[128 * tt:128 * (tt + 1), :], tD, "s")
            if en == "vector":
                wv(t_st)
        barrier()

        XL3T = bfs(0, KT * 512)
        GT = bfs(8192, H1T * 512)
        wgb = bfs(40960, H1)
        wub = bfs(49152, H1)
        wdb0 = wgb
        wdb1 = wub
        gsp0 = bfs(57344, 512)
        gsp1 = bfs(62464, 512)
        xlnE = bfs(57856, 2048)
        sqjE = bfs(59904, 2048)
        ocbE = bfs(61952, 512)
        ytE = bfs(62464, 1536)
        ytE = xlnE
        xaE = f32s(0, 2048)
        tE = f32s(2048, 2048)
        gfE = f32s(4096, 512)
        silE = f32s(4608, 512)
        abE = f32s(5120, 512)
        thrE = f32s(5632, 128)
        k0E = f32s(5760, 128)
        k1E = f32s(5888, 128)
        m1E = f32s(6016, 128)
        m2E = f32s(6144, 128)
        n1E = f32s(6272, 128)
        n2E = f32s(6400, 128)

        for q4 in range(4):
            r0 = q4 * 512
            prev_tp = 0
            prev_ev = 0
            prev_dapp = 0
            pbevE = {}
            for tt in range(4):
                if en == "sync":
                    w("dve", prev_dapp)
                t_a = ld(xaE, xr3[r0 + 128 * tt: r0 + 128 * (tt + 1), :])
                if en in ("vector", "scalar"):
                    wv(t_a)
                rstd, mb, _ = ln_stats(xaE, 0, sqjE, pre_act=prev_ev)
                if en == "vector":
                    w("mm", prev_tp)
                dapp = ln_apply(xlnE, xaE, rstd, mb)
                prev_dapp = dapp
                prev_tp, prev_ev = emit_t16(xlnE, XL3T[:], 512, tt * 128, dapp, pbevE)
            barrier()

            prev_tmu = 0
            prev_dsp = 0
            ga_ev = {}
            gu_rd = {}
            gsp_rd = {}
            gt_ev = {}
            pending = None

            def flush_pending():
                nonlocal pending
                if pending is None:
                    return
                (pn, pm, pds) = pending
                par = (pn * 4 + pm) % 2
                gspb = gsp0 if par == 0 else gsp1
                pbG = PB[par]
                if en == "tensor":
                    w("dve", pds)
                    w("act", gt_ev.get(par, 0))
                tmt = 0
                for c_ in range(4):
                    tmt = tp(pbG[:, c_ * 128:(c_ + 1) * 128],
                             gspb[:, c_ * 128:(c_ + 1) * 128])
                gsp_rd[par] = tmt
                aev = act(lambda s, pb_=pbG, n_=pn, m_=pm, tm=tmt: (
                    s.wait_ge(SEM["mm"], tm),
                    s.copy(out=GT[:].rearrange("p (k c) -> p k c", c=512)[
                        :, n_ * 4:(n_ + 1) * 4, m_ * 128:(m_ + 1) * 128],
                        in_=pb_[:, 0:512].rearrange("p (k c) -> p k c", k=4)))[-1])
                gt_ev[par] = aev
                pending = None

            for n in range(KT):
                if en == "sync":
                    w("mm", prev_tmu)
                t_wl = [ld(wgb, wg_s[n], "w1"), ld(wub, wu_s[n], "w2")]
                for mtk in range(4):
                    it = n * 4 + mtk
                    par = it % 2
                    gspb = gsp0 if par == 0 else gsp1
                    pg = PS[par]
                    pu = PS[2 + par]
                    if en == "tensor" and mtk == 0:
                        wv(t_wl[-2])
                        wv(t_wl[-1])
                    if en == "tensor":
                        w("act", ga_ev.get(par, 0))
                        w("dve", gu_rd.get(par, 0))
                    tmg = 0
                    for k in range(KT):
                        tmg = mm(pg[:, 0:512],
                                 XL3T[:, k * 512 + mtk * 128: k * 512 + (mtk + 1) * 128],
                                 wgb[:, k * 512:(k + 1) * 512],
                                 start=(k == 0), stop=(k == KT - 1), inc=(k == KT - 1))
                    tmu = 0
                    for k in range(KT):
                        tmu = mm(pu[:, 0:512],
                                 XL3T[:, k * 512 + mtk * 128: k * 512 + (mtk + 1) * 128],
                                 wub[:, k * 512:(k + 1) * 512],
                                 start=(k == 0), stop=(k == KT - 1), inc=(k == KT - 1))
                    prev_tmu = tmu
                    flush_pending()
                    a_s = act(lambda s, p_=pg, tm=tmg, pd=prev_dsp: (
                        s.wait_ge(SEM["mm"], tm),
                        s.wait_ge(SEM["dve"], pd) if pd else None,
                        s.activation(out=silE, in_=p_[:, 0:512], func=AF.Silu))[-1])
                    ga_ev[par] = a_s
                    if en == "vector":
                        w("act", a_s)
                        w("mm", tmu)
                        w("mm", gsp_rd.get(par, 0))
                    d_g = dve(lambda v, p_=pu: v.tensor_tensor(
                        gfE, silE, p_[:, 0:512], ALU.mult))
                    a_ab = act(lambda s, dg=d_g: (
                        s.wait_ge(SEM["dve"], dg),
                        s.activation(out=abE, in_=gfE, func=AF.Abs))[-1])
                    if en == "vector":
                        w("act", a_ab)
                    ab3 = abE.rearrange("p (q f) -> p q f", f=4)
                    gf3 = gfE.rearrange("p (q f) -> p q f", f=4)
                    gs3 = gspb.rearrange("p (q f) -> p q f", f=4)
                    dve(lambda v: v.tensor_tensor(m1E, ab3[:, :, 0], ab3[:, :, 1], ALU.max))
                    dve(lambda v: v.tensor_tensor(m2E, ab3[:, :, 2], ab3[:, :, 3], ALU.max))
                    dve(lambda v: v.tensor_tensor(n1E, ab3[:, :, 0], ab3[:, :, 1], ALU.min))
                    dve(lambda v: v.tensor_tensor(n2E, ab3[:, :, 2], ab3[:, :, 3], ALU.min))
                    dve(lambda v: v.tensor_tensor(m1E, m1E, m2E, ALU.min))
                    dve(lambda v: v.tensor_tensor(n1E, n1E, n2E, ALU.max))
                    dve(lambda v: v.tensor_tensor(thrE, m1E, n1E, ALU.max))
                    d_sp = 0
                    for f4 in range(4):
                        kk_ = k0E if f4 % 2 == 0 else k1E
                        dve(lambda v, f_=f4, kb=kk_: v.tensor_tensor(
                            kb, ab3[:, :, f_], thrE, ALU.is_ge))
                        d_sp = dve(lambda v, f_=f4, kb=kk_, g3=gs3: v.tensor_tensor(
                            g3[:, :, f_], gf3[:, :, f_], kb, ALU.mult))
                    prev_dsp = d_sp
                    gu_rd[par] = d_g
                    pending = (n, mtk, d_sp)
            flush_pending()
            barrier()

            nxt_wd = gd(wdb0, wd_s[0], "ga")
            st_o = None
            prev_e_mm = 0
            de_ev = {}
            for m2 in range(KT):
                wb = wdb0 if m2 % 2 == 0 else wdb1
                t_wl = nxt_wd
                if m2 + 1 < KT:
                    if en == "gpsimd":
                        w("mm", prev_e_mm)
                    nxt_wd = gd(wdb1 if m2 % 2 == 0 else wdb0, wd_s[m2 + 1], "ga")
                po = PS[4 + m2 % 2]
                if en == "tensor":
                    wv(t_wl)
                    w("act", de_ev.get(m2 % 2, 0))
                tmo = 0
                for k in range(H1T):
                    tmo = mm(po[:, 0:512], wb[:, k * 128:(k + 1) * 128],
                             GT[:, k * 512:(k + 1) * 512],
                             start=(k == 0), stop=(k == H1T - 1), inc=(k == H1T - 1))
                prev_e_mm = tmo
                aev = act(lambda s, p_=po, tm=tmo, so=st_o: (
                    s.wait_ge(SEM["mm"], tm),
                    s.wait_ge(SEM[so[0]], so[1]) if so else None,
                    s.copy(out=ocbE, in_=p_[:, 0:512]))[-1])
                de_ev[m2 % 2] = aev
                if en == "sync":
                    w("act", aev)
                st_o = ld(och[m2 * 128:(m2 + 1) * 128, r0:r0 + 512], ocbE, "q")
            barrier()

            prev_last = 0
            for tt in range(4):
                if en == "sync":
                    w("dve", prev_last)
                t_y = ld(ytE, och[:, r0 + tt * 128: r0 + (tt + 1) * 128], "o", transpose=True)
                t_a = ld(xaE, xr3[r0 + 128 * tt: r0 + 128 * (tt + 1), :])
                if en in ("scalar", "vector"):
                    wv(t_y)
                rstd, mb, _ = ln_stats(ytE, 0, sqjE)
                ln_apply(tE, ytE, rstd, mb)
                if en == "vector":
                    wv(t_a)
                d_r = dve(lambda v: v.tensor_tensor(tE, tE, xaE, ALU.add))
                prev_last = d_r
                if en == "sync":
                    w("dve", d_r)
                t_st = ld(out_e[r0 + 128 * tt: r0 + 128 * (tt + 1), :], tE, "s")
                if en == "vector":
                    wv(t_st)
            barrier()
        barrier()

    with nc.Block() as block:
        @block.sync
        def _(e):
            stream("sync", e)

        @block.gpsimd
        def _(e):
            stream("gpsimd", e)

        @block.tensor
        def _(e):
            stream("tensor", e)

        @block.scalar
        def _(e):
            stream("scalar", e)

        @block.vector
        def _(e):
            stream("vector", e)

    es.close()
    return nc



def _lhsT_tiles(A):
    K, M = A.shape
    kt, mt = K // 128, M // 128
    return np.ascontiguousarray(
        A.reshape(kt, 128, mt, 128).transpose(2, 1, 0, 3).reshape(mt, 128, kt * 128))


def kernel(**inp):
    x = np.asarray(inp["L0_attn_x"])
    Wa = np.asarray(inp["L0_attn_Wa"]); Wx = np.asarray(inp["L0_attn_Wx"])
    ba = np.asarray(inp["L0_attn_ba"]); bx = np.asarray(inp["L0_attn_bx"])
    Lam = np.asarray(inp["L0_attn_Lam"])
    w1 = np.asarray(inp["L0_ffn_w1"]); w3 = np.asarray(inp["L0_ffn_w3"])
    w2 = np.asarray(inp["L0_ffn_w2"])
    Wq = np.asarray(inp["L1_attn_Wq"]); bq = np.asarray(inp["L1_attn_bq"])
    Wk = np.asarray(inp["L1_attn_Wk"]); bk = np.asarray(inp["L1_attn_bk"])
    Wv = np.asarray(inp["L1_attn_Wv"]); bv = np.asarray(inp["L1_attn_bv"])
    Wo = np.asarray(inp["L1_attn_Wo"]); bo = np.asarray(inp["L1_attn_bo"])
    wg = np.asarray(inp["L1_ffn_w_gate"]); wu = np.asarray(inp["L1_ffn_w_up"])
    wd = np.asarray(inp["L1_ffn_w_down"])

    f32 = np.float32
    wat = _lhsT_tiles(np.ascontiguousarray(Wa.T).astype(f32))
    wxt = _lhsT_tiles(np.ascontiguousarray(Wx.T).astype(f32))
    w1p = np.zeros((D, H0), f32); w1p[:, :H0_TRUE] = w1.T
    w3p = np.zeros((D, H0), f32); w3p[:, :H0_TRUE] = w3.T
    w2p = np.zeros((H0, D), f32); w2p[:H0_TRUE, :] = w2.T
    w1tt = _lhsT_tiles(w1p)
    w3tt = _lhsT_tiles(w3p)
    w2tt = _lhsT_tiles(w2p)
    wqt = _lhsT_tiles(Wq.astype(f32))
    wkt = _lhsT_tiles(Wk.astype(f32))
    wvt = _lhsT_tiles(Wv.astype(f32))
    wot = _lhsT_tiles(Wo.astype(f32))
    wg_t = np.ascontiguousarray(wg.reshape(KT, 128, H1)).astype(f32)
    wu_t = np.ascontiguousarray(wu.reshape(KT, 128, H1)).astype(f32)
    wd_t = np.ascontiguousarray(_lhsT_tiles(wd.astype(f32)).reshape(KT, 128, H1))

    def colvec(v, n):
        return np.ascontiguousarray(np.asarray(v).reshape(n, 128).T).astype(f32)

    ba_t = colvec(ba, KT); bx_t = colvec(bx, KT); lam_t = colvec(Lam, KT)
    bq_t = colvec(bq, KT); bk_t = colvec(bk, 2); bv_t = colvec(bv, 2)
    bo_t = colvec(bo, KT)
    eye = np.eye(128).astype(ml_dtypes.bfloat16)
    cst = np.zeros((128, 2), f32); cst[:, 0] = 1.0; cst[:, 1] = EPS
    onesr = np.ones((1, 128), f32)

    nc = build_nc()
    in_maps = []
    for core in range(8):
        b, p = core // 2, core % 2
        pm = np.zeros((128, 2), f32)
        pm[:, 0] = 1.0 - p
        pm[:, 1] = p
        in_maps.append({
            "x0": np.ascontiguousarray(x[b]).astype(f32),
            "ident": eye,
            "wat": wat, "wxt": wxt, "ba_t": ba_t, "bx_t": bx_t, "lam_t": lam_t,
            "w1t": w1tt, "w3t": w3tt, "w2t": w2tt,
            "wqt": wqt, "wkt": wkt, "wvt": wvt, "wot": wot,
            "bq_t": bq_t, "bk_t": bk_t, "bv_t": bv_t, "bo_t": bo_t,
            "wg_in": wg_t, "wu_in": wu_t, "wd_in": wd_t,
            "pmask": pm, "consts": cst, "onesr_in": onesr,
        })
    res = run_bass_kernel_spmd(nc, in_maps, core_ids=list(range(8)))
    out = np.empty((4, S, D), np.float32)
    for core in range(8):
        b, p = core // 2, core % 2
        out[b, p * 2048:(p + 1) * 2048, :] = res.results[core]["out"]
    return out


